# version 06
# baseline (speedup 1.0000x reference)
import numpy as np

import concourse.bass as bass
import concourse.mybir as mybir
from concourse.bass_utils import run_bass_kernel_spmd

F32 = mybir.dt.float32
I32 = mybir.dt.int32
AF = mybir.ActivationFunctionType
ALU = mybir.AluOpType
AX = mybir.AxisListType

L = 2048
D = 1024
DI = 2048
NH = 32
NS = 128
PD = 64
TC = 128
NCH = L // TC
IN_DIM = 2 * DI + NH + NH * 4 * NS
HID = 2730
FH = 8
FD = 64
FI = 512
FW = 512

PI = float(np.pi)
PI32 = float(np.float32(np.pi))
NEG_BIG = -1e9



_RMW_METHODS = {"select", "copy_predicated"}
_DMA_QUEUE_SEM = {"t": "dmt", "v": "dmv", "s": "dms", "g": "dmg", "y": "dmy"}


def _ap_of(x):
    return x if isinstance(x, bass.AP) else None


def _tname(ap):
    return ap.tensor.name


class Op:
    __slots__ = ("engine", "method", "args", "kwargs", "outs", "ins",
                 "inc_token", "waits", "drain", "pos", "is_dma", "dma_q",
                 "_inc_val", "_dma_val", "seq", "dma_slot", "emitted")

    def __init__(self, engine, method, args, kwargs, outs, ins, is_dma=False,
                 dma_q=None):
        self.engine = engine
        self.method = method
        self.args = args
        self.kwargs = kwargs
        self.outs = outs
        self.ins = ins
        self.inc_token = False
        self.waits = []
        self.drain = False
        self.pos = -1
        self.is_dma = is_dma
        self.dma_q = dma_q
        self.emitted = False


class Eng:
    def __init__(self, mk, engine):
        self._mk = mk
        self._e = engine

    def __getattr__(self, method):
        def call(*args, **kwargs):
            return self._mk._record(self._e, method, args, kwargs)
        return call


class MK:
    ENGS = ("t", "v", "s", "g", "y")
    ENG_ATTR = {"t": "tensor", "v": "vector", "s": "scalar", "g": "gpsimd",
                "y": "sync"}

    def __init__(self, nc):
        self.nc = nc
        self.eng_ops = {e: [] for e in self.ENGS}
        self.writer = {}
        self.readers = {}
        self._seq = 0
        self._dma_count = {}
        self.dma_slots = {}
        self._rec_hw = {}
        self._drain_cov = {}
        self.NSLOT = 8
        self.t = Eng(self, "t")
        self.v = Eng(self, "v")
        self.s = Eng(self, "s")
        self.g = Eng(self, "g")
        self.y = Eng(self, "y")

    def _token_of(self, op):
        return ("dma", op.dma_q, op) if op.is_dma else ("eng", op.engine, op)

    def _add_dep(self, op, tok):
        if tok[0] == "eng" and tok[1] == op.engine:
            e = op.engine
            cov = self._drain_cov.get(e, -1)
            if tok[2].seq > cov:
                op.drain = True
                prev = self.eng_ops[e][-1] if self.eng_ops[e] else None
                self._drain_cov[e] = prev.seq if prev is not None else -1
            return
        hw = self._rec_hw.setdefault(op.engine, {})
        key = self._tok_key(tok)
        prev = hw.get(key)
        if prev is not None and prev.seq >= tok[2].seq:
            return
        hw[key] = tok[2]
        if tok[0] == "eng":
            tok[2].inc_token = True
        for t2 in op.waits:
            if t2[2] is tok[2]:
                return
        op.waits.append(tok)

    def _classify(self, method, args, kwargs):
        pos_aps = [_ap_of(a) for a in args]
        pos_aps = [a for a in pos_aps if a is not None]
        kw_out = []
        kw_in = []
        for k, a in kwargs.items():
            ap = _ap_of(a)
            if ap is None:
                continue
            if k in ("out",):
                kw_out.append(ap)
            elif k == "accum_out":
                kw_out.append(ap)
            else:
                kw_in.append(ap)
        if method in ("memset", "memzero", "iota"):
            outs = (kw_out + pos_aps)[:1]
            ins = []
        elif method in _RMW_METHODS:
            allaps = kw_out + pos_aps + kw_in
            outs = allaps[:1]
            ins = allaps
        else:
            if kw_out:
                outs = kw_out
                ins = pos_aps + kw_in
            else:
                outs = pos_aps[:1] + [a for a in kw_out]
                ins = pos_aps[1:] + kw_in
        return outs, ins

    def _tok_key(self, tok):
        if tok[0] == "eng":
            return ("e", tok[1])
        return ("d", tok[2].dma_q, tok[2].dma_slot)

    @staticmethod
    def _prange(ap):
        pitch, cnt = ap.ap[0]
        if pitch == 0:
            return (0, cnt)
        return (ap.tensor.base_partition + ap.offset // pitch, cnt)

    def _check_partitions(self, method, outs, ins):
        if method in ("matmul",):
            if len(ins) >= 2:
                a, b = self._prange(ins[0]), self._prange(ins[1])
                assert a == b, f"matmul operand partitions differ: {a} vs {b}"
            return
        if method in ("transpose", "iota", "memset", "memzero",
                      "partition_all_reduce", "partition_broadcast"):
            return
        rr = [self._prange(ap) for ap in outs + ins]
        assert all(r == rr[0] for r in rr), \
            f"{method}: operand partition ranges differ: {rr}"

    def _record(self, engine, method, args, kwargs):
        is_dma = method in ("dma_start", "dma_start_transpose")
        outs, ins = self._classify(method, args, kwargs)
        if not is_dma:
            self._check_partitions(method, outs, ins)
        self._last_real = getattr(self, "_last_real", {})
        op = Op(engine, method, args, kwargs, outs, ins, is_dma,
                _DMA_QUEUE_SEM[engine] if is_dma else None)
        op.seq = self._seq
        self._seq += 1
        if is_dma:
            q = op.dma_q
            c = self._dma_count.get(q, 0)
            self._dma_count[q] = c + 1
            op.dma_slot = c % self.NSLOT
            self.dma_slots.setdefault((q, op.dma_slot), []).append(op)
        for ap in op.ins:
            n = _tname(ap)
            for w in self.writer.get(n, {}).values():
                self._add_dep(op, w)
            if n.startswith("pb"):
                for r in self.readers.get(n, {}).values():
                    self._add_dep(op, r)
        for ap in op.outs:
            n = _tname(ap)
            for w in self.writer.get(n, {}).values():
                self._add_dep(op, w)
            for r in self.readers.get(n, {}).values():
                self._add_dep(op, r)
        tok = self._token_of(op)
        key = self._tok_key(tok)
        for ap in op.outs:
            n = _tname(ap)
            self.writer.setdefault(n, {})[key] = tok
            self.readers[n] = {}
        for ap in op.ins:
            self.readers.setdefault(_tname(ap), {})[key] = tok
        self.eng_ops[engine].append(op)
        return op

    def dma(self, q, out, in_, **kw):
        return getattr(self, q).dma_start(out=out, in_=in_, **kw)

    def barrier(self):
        last = {}
        for e in self.ENGS:
            if self.eng_ops[e]:
                last[e] = self.eng_ops[e][-1]
        dma_last = {k: lst[-1] for k, lst in self.dma_slots.items() if lst}
        for e in self.ENGS:
            nop = Op(e, "nop", (), {}, [], [])
            nop.seq = self._seq
            self._seq += 1
            for e2, lop in last.items():
                if e2 == e:
                    if not lop.is_dma:
                        nop.drain = True
                        continue
                if lop.emitted and not lop.is_dma:
                    continue
                self._add_dep(nop, self._token_of(lop))
            for (q, sl), dop in dma_last.items():
                self._add_dep(nop, ("dma", q, dop))
            self.eng_ops[e].append(nop)
        self.writer = {}
        self.readers = {}

    def _ensure_sems(self):
        if not hasattr(self, "_eng_sems"):
            self._eng_sems = {e: self.nc.alloc_semaphore(name=f"SE_{e}")
                              for e in self.ENGS}
            self._dma_sems = {}
            self._inc_no = {e: 0 for e in self.ENGS}
            self._emitted = {e: 0 for e in self.ENGS}
            self._wait_hw = {}

    def flush(self):
        nc = self.nc
        self._ensure_sems()
        eng_sems, dma_sems = self._eng_sems, self._dma_sems
        for e in self.ENGS:
            for op in self.eng_ops[e][self._emitted[e]:]:
                if op.is_dma:
                    key = (op.dma_q, op.dma_slot)
                    if key not in dma_sems:
                        dma_sems[key] = nc.alloc_semaphore(
                            name=f"SD_{key[0]}_{key[1]}")
                    lst = self.dma_slots[key]
                    op._dma_val = 16 * (lst.index(op) + 1)
                elif op.inc_token:
                    self._inc_no[e] += 1
                    op._inc_val = self._inc_no[e]

        mkself = self

        def emit_stream(engobj, e, start, end):
            for op in mkself.eng_ops[e][start:end]:
                for tok in op.waits:
                    if tok[0] == "eng":
                        engobj.wait_ge(eng_sems[tok[1]], tok[2]._inc_val)
                    else:
                        d = tok[2]
                        engobj.wait_ge(dma_sems[(d.dma_q, d.dma_slot)],
                                       d._dma_val)
                if op.drain:
                    engobj.drain()
                if op.is_dma and op._dma_val > 16:
                    engobj.wait_ge(dma_sems[(op.dma_q, op.dma_slot)],
                                   op._dma_val - 16)
                if op.method == "nop":
                    inst = engobj.nop()
                else:
                    inst = getattr(engobj, op.method)(*op.args, **op.kwargs)
                if op.is_dma:
                    inst.then_inc(dma_sems[(op.dma_q, op.dma_slot)], 16)
                elif op.inc_token:
                    inst.then_inc(eng_sems[e], 1)

        spans = {e: (self._emitted[e], len(self.eng_ops[e])) for e in self.ENGS}
        if all(a == b for a, b in spans.values()):
            return
        with nc.Block() as blk:
            for e in self.ENGS:
                a, b = spans[e]
                if a == b:
                    continue
                getattr(blk, self.ENG_ATTR[e])(
                    lambda engobj, e=e, a=a, b=b: emit_stream(engobj, e, a, b))
        for e in self.ENGS:
            for op in self.eng_ops[e][self._emitted[e]:spans[e][1]]:
                op.emitted = True
            self._emitted[e] = spans[e][1]

    def realize(self):
        self.barrier()
        self.flush()

    def n_ops(self):
        return sum(len(v) for v in self.eng_ops.values())



def cdiv(a, b):
    return -(-a // b)


class Builder:
    SBUF_BYTES = 207 * 1024

    def __init__(self):
        self.nc = bass.Bass("TRN2", target_bir_lowering=False)
        self.mk = MK(self.nc)
        self._dram = {}
        self._persist_bytes = 0
        self._arena_handle = None
        self._guards = []
        self._arena_off = 0
        self._arena_size = 0
        self._arena_peak = 0
        self._uid = 0

    def uid(self):
        self._uid += 1
        return self._uid

    def dram(self, name, shape, dtype=F32, kind="Internal"):
        h = self.nc.dram_tensor(name, list(shape), dtype, kind=kind)
        self._dram[name] = h
        return h

    def sb(self, name, shape, dtype=F32):
        h = self.nc.alloc_sbuf_tensor(name, list(shape), dtype)
        per_part = int(np.prod(shape[1:])) * mybir.dt.size(dtype)
        self._persist_bytes += ((per_part + 31) // 32) * 32
        return h

    def finish_persist(self):
        pass

    def ar(self, name, shape, dtype=F32):
        g = self.nc.sbuf_tensor(f"{name}_{self.uid()}", list(shape), dtype)
        h = g.__enter__()
        self._guards.append(g)
        per_part = int(np.prod(shape[1:])) * mybir.dt.size(dtype)
        self._arena_off += ((per_part + 31) // 32) * 32
        self._arena_peak = max(self._arena_peak, self._arena_off)
        return h

    def arena_mark(self):
        return len(self._guards)

    def arena_reset(self, mark=0):
        self.mk.barrier()
        self.mk.flush()
        while len(self._guards) > mark:
            self._guards.pop().__exit__(None, None, None)

    def psum(self, name, banks=1, part=128):
        return self.nc.alloc_psum_tensor(name, [part, 512 * banks], F32)

    def constcol(self, val):
        val = float(val)
        if not hasattr(self, "_cpool"):
            self._cpool = self.sb("CPOOL", [128, 8])
            self._cvals = {}
        if val not in self._cvals:
            i = len(self._cvals)
            assert i < 8
            self._cvals[val] = i
            self.mk.g.memset(self._cpool[:, i:i + 1], val)
        i = self._cvals[val]
        return self._cpool[:, i:i + 1]



class Kern(Builder):

    def __init__(self):
        super().__init__()
        self.declare_io()
        self.declare_static()
        self.constcol(2.0)
        self.psb = [self.psum(f"pb{i}") for i in range(8)]
        self.finish_persist()

    def declare_io(self):
        d = self.dram
        self.x_in = d("x", [L, D], kind="ExternalInput")
        self.WinT = d("WinT", [D, IN_DIM], kind="ExternalInput")
        self.dw_w = d("dw_w", [DI, 4], kind="ExternalInput")
        self.dw_b = d("dw_b", [1, DI], kind="ExternalInput")
        self.WoutT = d("WoutT", [DI, D], kind="ExternalInput")
        self.skip_in = d("skip", [1, DI], kind="ExternalInput")
        self.dt_bias = d("dt_bias", [1, NH], kind="ExternalInput")
        self.thetaT = d("thetaT", [NS, NH], kind="ExternalInput")
        self.krawT = d("krawT", [NS, NH], kind="ExternalInput")
        self.wqkv = d("wqkv", [2, D, 3 * FI], kind="ExternalInput")
        self.qscale = d("qscale", [2, 1, FD], kind="ExternalInput")
        self.kscale = d("kscale", [2, 1, FD], kind="ExternalInput")
        self.wo = d("wo", [2, FI, D], kind="ExternalInput")
        self.ln_g = d("ln_g", [2, 1, D], kind="ExternalInput")
        self.ln_b = d("ln_b", [2, 1, D], kind="ExternalInput")
        self.w1 = d("w1", [2, D, 2 * HID], kind="ExternalInput")
        self.b1 = d("b1", [2, 1, 2 * HID], kind="ExternalInput")
        self.w2 = d("w2", [2, HID, D], kind="ExternalInput")
        self.b2 = d("b2", [2, 1, D], kind="ExternalInput")
        self.out = d("out", [L, D], kind="ExternalOutput")
        self.sc_bc = d("sc_bc", [NH * 4 * NS, L])
        self.sc_u = d("sc_u", [DI, L])
        self.sc_ut = d("sc_ut", [NH * L, PD])
        self.sc_z = d("sc_z", [DI, L])
        self.sc_yg = d("sc_yg", [DI, L])
        self.sc_s = d("sc_s", [D, L])
        self.sc_g = d("sc_g", [D, L])
        self.sc_xn = d("sc_xn", [D, L])
        self.sc_dbg = d("sc_dbg", [1024, 2048])

    def declare_static(self):
        sb = self.sb
        self.ones_col = sb("ones_col", [128, 1])
        self.ones_row1 = sb("ones_row1", [1, 512])
        self.ident = sb("ident", [128, 128])
        self.tri_incl = sb("tri_incl", [128, 128])
        self.hpar = sb("hpar", [NS, 16 * NH])
        self.dwW = sb("dwW", [128, 16 * 4])
        self.dwB = sb("dwB", [128, 16])
        self.skip_c = sb("skip_c", [128, 16])
        self.dtb_col = sb("dtb_col", [NH, 1])

    HP = {name: i for i, name in enumerate(
        ["A", "k2", "sig", "rs", "mos", "mhy",
         "irs_o", "irs_h2", "negA", "k2n", "mos_h", "mhy_h", "kf"])}

    def hp(self, name):
        i = self.HP[name]
        return self.hpar[:, i:i + 1]

    def ts(self, out, in0, s1, s2=None, op0=ALU.mult, op1=ALU.add, eng="v"):
        e = getattr(self.mk, eng)
        if s2 is None:
            return e.tensor_scalar(out, in0, s1, None, op0)
        return e.tensor_scalar(out, in0, s1, s2, op0, op1)

    def stt(self, out, in0, s, in1, op0=ALU.mult, op1=ALU.add, eng="v"):
        e = getattr(self.mk, eng)
        return e.scalar_tensor_tensor(out, in0, s, in1, op0, op1)

    def tt(self, out, a, b, op=ALU.mult, eng="v"):
        e = getattr(self.mk, eng)
        return e.tensor_tensor(out, a, b, op)

    def act(self, out, in_, func, bias=0.0, scale=1.0, accum_out=None):
        kw = {}
        if accum_out is not None:
            kw["accum_out"] = accum_out
        if isinstance(bias, float) and bias not in (0.0, 1.0):
            p = out.ap[0][1] if isinstance(out, bass.AP) else 128
            bias = self.constcol(bias)[0:p, 0:1]
        return self.mk.s.activation(out=out, in_=in_, func=func, bias=bias,
                                    scale=scale, **kw)

    def bcast_free(self, colap, n):
        t = colap.tensor
        psz = colap.ap[0]
        return bass.AP(t, colap.offset, [list(psz), [0, n]])

    def phase_alloc0(self):
        self.stripe01 = self.ar("stripe01", [128, L])
        self.dt_fm = self.ar("dt_fm", [NH, L])
        self.rn_sq = self.ar("rn_sq", [64, L])
        self.rn_tm = self.ar("rn_tm", [128, NCH * 64])
        g = self.mk.g
        g.memset(self.stripe01[:], 1.0)
        g.memset(bass.AP(self.stripe01, 0, [[L, 128], [TC, NCH]]), 0.0)
        return self.arena_mark()

    def build_all(self):
        self.phase_prep()
        self.arena_reset()
        mA = self.phase_alloc0()
        self.phase_ln()
        mB = mA + 1
        self.arena_reset(mB)
        self.phase_dt()
        self.phase_bc_norms()
        self.phase_norm_finish()
        self.arena_reset(mB)
        self.phase_z_xc_u()
        self.arena_reset(mA)
        self.phase_ssm_core()
        self.arena_reset(0)
        self.phase_wout()
        self.arena_reset(0)
        self.phase_ffn()
        self.arena_reset(0)
        self.phase_final()
        self.mk.realize()

    def hpc(self, name, h):
        i = self.HP[name]
        return self.hpar[:, i * NH + h:i * NH + h + 1]

    def hpall(self, name):
        i = self.HP[name]
        return self.hpar[:, i * NH:(i + 1) * NH]

    def phase_prep(self):
        mk, nc = self.mk, self.nc
        g = mk.g
        g.memset(self.ones_col[:], 1.0)
        g.memset(self.ones_row1[:], 1.0)
        g.memset(self.tri_incl[:], 1.0)
        g.affine_select(self.ident[:], self.tri_incl[:], [[1, 128]],
                        ALU.is_ge, 0.0, base=0, channel_multiplier=-1)
        g.affine_select(self.ident[:], self.ident[:], [[-1, 128]],
                        ALU.is_ge, 0.0, base=0, channel_multiplier=1)
        g.affine_select(self.tri_incl[:], self.tri_incl[:], [[1, 128]],
                        ALU.is_ge, 0.0, base=0, channel_multiplier=-1)

        th_raw = self.ar("th_raw", [NS, NH])
        k_raw = self.ar("k_raw", [NS, NH])
        mk.dma("y", th_raw[:], self.thetaT.ap())
        mk.dma("y", k_raw[:], self.krawT.ap())
        A, k2, sig, rs = (self.hpall(n) for n in ("A", "k2", "sig", "rs"))
        def sigmoid_chain(dst, src_ap):
            self.act(dst, src_ap, AF.Exp, scale=-1.0)
            self.act(dst, dst, AF.Ln, bias=1.0)
            self.act(dst, dst, AF.Exp, scale=-1.0)
        sigmoid_chain(A, th_raw[:])
        self.act(A, A, AF.Square, scale=PI)
        sigmoid_chain(k2, k_raw[:])
        self.ts(k2, k2, 0.25)
        self.act(sig, k2, AF.Square)
        self.tt(sig, sig, A, ALU.subtract)
        ab_t = self.ar("abs_sig", [NS, NH])
        self.ts(ab_t[:], sig, -1.0)
        self.tt(ab_t[:], ab_t[:], sig, ALU.max)
        self.ts(ab_t[:], ab_t[:], 1e-30, op0=ALU.max)
        self.act(rs, ab_t[:], AF.Ln)
        self.act(rs, rs, AF.Exp, scale=0.5)
        mos, mhy = self.hpall("mos"), self.hpall("mhy")
        self.ts(mos, sig, 0.0, op0=ALU.is_lt)
        self.ts(mhy, mos, -1.0, 1.0, op0=ALU.mult, op1=ALU.add)
        irs = self.ar("irs", [NS, NH])
        mk.v.reciprocal(irs[:], rs)
        self.tt(self.hpall("irs_o"), mos, irs[:], ALU.mult)
        t2 = self.ar("t2", [NS, NH])
        self.ts(t2[:], irs[:], 0.5)
        self.tt(self.hpall("irs_h2"), mhy, t2[:], ALU.mult)
        self.ts(self.hpall("negA"), A, -1.0)
        self.ts(self.hpall("k2n"), k2, -1.0)
        self.ts(self.hpall("mos_h"), mos, 0.5)
        self.ts(self.hpall("mhy_h"), mhy, 0.5)
        self.ts(self.hpall("kf"), k2, 2.0)

        for c in range(16):
            mk.dma("y", self.dwW[:, c * 4:(c + 1) * 4],
                   self.dw_w.ap()[c * 128:(c + 1) * 128, :])
        self.load_cols16(self.dwB, self.dw_b)
        self.load_cols16(self.skip_c, self.skip_in)
        mk.dma("y", self.dtb_col[:, 0:1],
               bass.AP(self.dt_bias, 0, [[1, NH], [1, 1]]))

    def load_cols16(self, dst, dram_row):
        mk = self.mk
        tmp = self.ar(f"lc16_{self.uid()}", [16, 128])
        mk.dma("y", tmp[:], dram_row.ap().rearrange("o (c p) -> (o c) p", p=128))
        ps = self.psb[7]
        mk.t.transpose(ps[0:128, 0:16], tmp[:], self.ident[0:16, 0:16])
        mk.v.tensor_copy(dst[:, 0:16], ps[0:128, 0:16])

    def trans128(self, dst_sb_ap, src_sb_ap, bank, rows, cols, eng="v", bp=0):
        ps = self.psb[bank]
        self.mk.t.transpose(ps[0:cols, 0:rows], src_sb_ap,
                            self.ident[bp:bp + rows, bp:bp + rows])
        if eng == "s":
            self.mk.s.copy(dst_sb_ap, ps[0:cols, 0:rows])
        else:
            getattr(self.mk, eng).tensor_copy(dst_sb_ap, ps[0:cols, 0:rows])

    def phase_ln(self):
        mk = self.mk
        self.xn = self.ar("xn", [128, 8 * L])
        xtm = self.ar("xtm", [128, D])
        xsq = self.ar("xsq", [128, D])
        cols = self.ar("lncols", [128, 8])
        for ti in range(16):
            mk.dma("y", xtm[:], self.x_in.ap()[ti * 128:(ti + 1) * 128, :])
            ssum = cols[:, 0:1]
            ssq = cols[:, 1:2]
            mu = cols[:, 2:3]
            musq = cols[:, 3:4]
            var = cols[:, 4:5]
            isd = cols[:, 5:6]
            mk.v.reduce_sum(ssum, xtm[:], axis=AX.X)
            self.act(xsq[:], xtm[:], AF.Square, accum_out=ssq)
            self.ts(mu, ssum, 1.0 / D)
            self.act(musq, mu, AF.Square)
            self.stt(var, ssq, 1.0 / D, musq, op0=ALU.mult, op1=ALU.subtract)
            self.act(var, var, AF.Ln, bias=1e-5)
            self.act(isd, var, AF.Exp, scale=-0.5)
            self.ts(xtm[:], xtm[:], mu, op0=ALU.subtract)
            self.ts(xtm[:], xtm[:], isd, op0=ALU.mult)
            for dc in range(8):
                self.trans128(self.xn[:, dc * L + ti * 128: dc * L + ti * 128 + 128],
                              xtm[:, dc * 128:(dc + 1) * 128], bank=dc % 2,
                              rows=128, cols=128)
        for dc in range(8):
            mk.dma("s", self.sc_xn.ap()[dc * 128:(dc + 1) * 128, :],
                   self.xn[:, dc * L:(dc + 1) * L])

    def load_wslab(self, dst, wT_dram, m0, mw, name="w"):
        for kc in range(8):
            self.mk.dma("y", dst[:, kc * mw:(kc + 1) * mw],
                        wT_dram.ap()[kc * 128:(kc + 1) * 128, m0:m0 + mw])

    def proj_mtile(self, wbuf, mw, bank, evac):
        mk = self.mk
        ps = self.psb[bank]
        for tg in range(4):
            for kc in range(8):
                mk.t.matmul(ps[0:mw, 0:512],
                            wbuf[:, kc * mw:(kc + 1) * mw],
                            self.xn[:, kc * L + tg * 512: kc * L + (tg + 1) * 512],
                            start=(kc == 0), stop=(kc == 7))
            evac(tg, ps[0:mw, 0:512])

    def phase_dt(self):
        mk = self.mk
        wdt = self.ar("wdt", [128, 8 * NH])
        self.load_wslab(wdt, self.WinT, 2 * DI, NH)
        dtr = self.ar("dtraw", [NH, 512])

        def evac(tg, ps):
            sl = self.dt_fm[:, tg * 512:(tg + 1) * 512]
            self.ts(dtr[:], ps, self.dtb_col[0:NH, 0:1], 0.0,
                    op0=ALU.add, op1=ALU.min)
            self.act(dtr[:], dtr[:], AF.Exp)
            self.act(dtr[:], dtr[:], AF.Ln, bias=1.0)
            self.ts(sl, dtr[:], 1e-4, 0.1, op0=ALU.max, op1=ALU.min)

        self.proj_mtile(wdt, NH, 6, evac)

    def phase_bc_norms(self):
        mk = self.mk
        wb = [self.ar(f"wbc{i}", [128, 8 * 128]) for i in range(2)]
        bcb = [self.ar("bcb0", [128, 4 * L])]
        sqf = self.ar("sqf", [128, L])
        sqb = self.ar("sqb", [128, 512])
        nstage = self.ar("nstage", [1, 512])
        BC0 = 2 * DI + NH
        for h in range(NH):
            wbuf = wb[h % 2]
            bcbuf = bcb[0]
            for mt in range(4):
                self.load_wslab(wbuf, self.WinT, BC0 + h * 512 + mt * 128, 128)
                ps = self.psb[mt % 2]
                nps = self.psb[2]
                for tg in range(4):
                    for kc in range(8):
                        mk.t.matmul(ps[0:128, 0:512],
                                    wbuf[:, kc * 128:(kc + 1) * 128],
                                    self.xn[:, kc * L + tg * 512: kc * L + (tg + 1) * 512],
                                    start=(kc == 0), stop=(kc == 7))
                    pa = ps[0:128, 0:512]
                    mk.v.tensor_copy(bcbuf[:, mt * L + tg * 512: mt * L + (tg + 1) * 512], pa)
                    if mt % 2 == 0:
                        self.act(sqf[:, tg * 512:(tg + 1) * 512], pa, AF.Square)
                    else:
                        self.act(sqb[:], pa, AF.Square)
                        mk.t.matmul(nps[0:1, 0:512], self.ones_col[:],
                                    sqf[:, tg * 512:(tg + 1) * 512],
                                    start=True, stop=False)
                        mk.t.matmul(nps[0:1, 0:512], self.ones_col[:], sqb[:],
                                    start=False, stop=True)
                        j = mt // 2
                        mk.s.copy(nstage[0:1, 0:512], nps[0:1, 0:512])
                        mk.dma("g", self.rn_sq[2 * h + j:2 * h + j + 1,
                                               tg * 512:(tg + 1) * 512],
                               nstage[0:1, 0:512])
            mk.dma("s", self.sc_bc.ap()[h * 512:(h + 1) * 512, :]
                   .rearrange("(m p) t -> p m t", p=128),
                   bass.AP(bcbuf, 0, [[4 * L, 128], [L, 4], [1, L]]))

    def phase_norm_finish(self):
        mk = self.mk
        self.act(self.rn_sq[:], self.rn_sq[:], AF.Ln, bias=1e-8)
        self.act(self.rn_sq[:], self.rn_sq[:], AF.Exp, scale=-0.5)
        for ti in range(16):
            self.trans128(self.rn_tm[:, ti * 64:(ti + 1) * 64],
                          self.rn_sq[:, ti * 128:(ti + 1) * 128], bank=6,
                          rows=64, cols=128)

    def phase_z_xc_u(self):
        mk = self.mk
        wz = self.ar("wz", [128, 8 * 128])
        wx = self.ar("wx", [128, 8 * 128])
        zb = self.ar("zb", [128, L])
        xcb = self.ar("xcb", [128, 4 + L])
        ub = self.ar("ub", [128, L])
        utb = self.ar("utb", [128, 2 * 16 * 64])
        mk.g.memset(xcb[:, 0:3], 0.0)
        for c in range(16):
            self.load_wslab(wz, self.WinT, c * 128, 128)
            self.load_wslab(wx, self.WinT, DI + c * 128, 128)

            def evz(tg, pa):
                self.act(zb[:, tg * 512:(tg + 1) * 512], pa, AF.Silu)

            def evx(tg, pa):
                mk.v.tensor_copy(xcb[:, 3 + tg * 512: 3 + (tg + 1) * 512], pa)

            self.proj_mtile(wz, 128, 3, evz)
            self.proj_mtile(wx, 128, 4, evx)
            mk.dma("s", self.sc_z.ap()[c * 128:(c + 1) * 128, :], zb[:])
            w4 = self.dwW[:, c * 4:(c + 1) * 4]
            self.ts(ub[:], xcb[:, 0:L], w4[:, 0:1])
            for r in range(1, 4):
                self.stt(ub[:], xcb[:, r:r + L], w4[:, r:r + 1], ub[:])
            self.act(ub[:], ub[:], AF.Silu, bias=self.dwB[:, c:c + 1])
            mk.dma("s", self.sc_u.ap()[c * 128:(c + 1) * 128, :], ub[:])
            for hi in range(2):
                for ci in range(16):
                    self.trans128(utb[:, hi * 1024 + ci * 64: hi * 1024 + (ci + 1) * 64],
                                  ub[hi * 64:(hi + 1) * 64, ci * 128:(ci + 1) * 128],
                                  bank=5, rows=64, cols=128, bp=hi * 64)
                h = 2 * c + hi
                mk.dma("s", self.sc_ut.ap()[h * L:(h + 1) * L, :]
                       .rearrange("(c p) d -> p c d", p=128),
                       bass.AP(utb, hi * 1024, [[2048, 128], [64, 16], [1, 64]]))

    def phase_ssm_core(self):
        mk = self.mk
        ar = self.ar
        bch = ar("bch", [128, L])
        dt_bc = ar("dt_bc", [128, L])
        rd = ar("rd", [128, L])
        ab = ar("abb", [128, L])
        bb = ar("bbb", [128, L])
        idet = ar("idet", [128, L])
        Kf = [ar(f"Kf{j}", [128, L]) for j in range(4)]
        Qf = [ar(f"Qf{j}", [128, L]) for j in range(4)]
        w1 = ar("w1", [128, L]); w2 = ar("w2", [128, L]); w3 = ar("w3", [128, L])
        ibuf = ar("ibuf", [128, L // 4], I32)
        ebuf = ar("ebuf", [128, 16 * 16])
        dtrow = ar("dtrow", [1, L])
        rcrow = ar("rcrow", [1, L])
        ygb = ar("ygb", [64, L])
        rCb = ar("rCb", [64, L])
        utc = ar("utc", [128, PD])
        gsb = ar("gsb", [128, 128])
        kysb = ar("kysb", [128, 128])
        ktmp = ar("ktmp", [128, 128])
        Yst = [ar("Yst0", [128, PD]), ar("Yst1", [128, PD])]
        Vst = [ar("Vst0", [128, PD]), ar("Vst1", [128, PD])]
        sYt = ar("sYt", [128, PD])

        def eslot(i):
            return ebuf[:, i * 16:(i + 1) * 16]

        def ecol(i, c):
            return ebuf[:, i * 16 + c:i * 16 + c + 1]

        PG, PO, PKY, PKV, PYA, PVA, PRP = 0, 1, 2, 3, 4, 5, 6

        for h in range(NH):
            hc = lambda name: self.hpc(name, h)
            mk.dma("g", dtrow[0:1, :], self.dt_fm[h:h + 1, :])
            mk.dma("g", rcrow[0:1, :], self.rn_sq[2 * h + 1:2 * h + 2, :])
            prp = self.psb[PRP]
            for tg in range(4):
                mk.t.matmul(prp[0:128, 0:512], self.ones_row1[0:1, 0:128],
                            dtrow[0:1, tg * 512:(tg + 1) * 512],
                            start=True, stop=True)
                mk.s.copy(dt_bc[:, tg * 512:(tg + 1) * 512], prp[0:128, 0:512])
            for tg in range(4):
                mk.t.matmul(prp[0:64, 0:512], self.ones_row1[0:1, 0:64],
                            rcrow[0:1, tg * 512:(tg + 1) * 512],
                            start=True, stop=True)
                mk.s.copy(rCb[0:64, tg * 512:(tg + 1) * 512], prp[0:64, 0:512])
            mk.dma("y", bch[:], self.sc_bc.ap()[h * 512 + 128:h * 512 + 256, :])

            self.ts(w1[:], dt_bc[:], hc("k2"), 1.0)
            self.ts(w2[:], dt_bc[:], hc("rs"))
            self.ts(w3[:], dt_bc[:], hc("A"), hc("kf"))
            self.tt(w3[:], w3[:], dt_bc[:])
            self.act(w3[:], w3[:], AF.Ln, bias=1.0)
            self.act(rd[:], w3[:], AF.Exp, scale=-1.0)
            self.tt(bb[:], w1[:], w2[:], ALU.add)
            self.act(bb[:], bb[:], AF.Ln)
            self.ts(idet[:], w3[:], hc("mos_h"))
            self.stt(idet[:], bb[:], hc("mhy"), idet[:])
            self.tt(ab[:], w1[:], w2[:], ALU.subtract)
            self.act(ab[:], ab[:], AF.Ln)
            mk.v.reciprocal(w1[:], w1[:])
            self.tt(w2[:], w2[:], w1[:])
            self.act(w2[:], w2[:], AF.Arctan)
            self.ts(w2[:], w2[:], hc("mos"))
            self.stt(ab[:], ab[:], hc("mhy"), w2[:])
            mk.v.tensor_tensor_scan(w1[:], self.stripe01[:], idet[:], 0.0,
                                    ALU.mult, ALU.add)
            mk.v.tensor_tensor_scan(w2[:], self.stripe01[:], ab[:], 0.0,
                                    ALU.mult, ALU.add)
            mk.v.tensor_tensor_scan(idet[:], self.stripe01[:], w3[:], 0.0,
                                    ALU.mult, ALU.add)
            self.act(idet[:], idet[:], AF.Exp, scale=-1.0)
            self.act(w1[:], w1[:], AF.Exp)
            self.act(w3[:], w2[:], AF.Exp)
            self.ts(bb[:], w2[:], 1.0 / (2 * PI))
            for hh in range(4):
                hsl = slice(hh * (L // 4), (hh + 1) * (L // 4))
                mk.v.tensor_copy(ibuf[:], bb[:, hsl])
                mk.v.tensor_copy(bb[:, hsl], ibuf[:])
            self.stt(bb[:], bb[:], -2 * PI, w2[:])
            self.ts(Kf[1][:], bb[:], PI32, op0=ALU.is_gt)
            self.stt(bb[:], Kf[1][:], -2 * PI, bb[:])
            self.ts(bb[:], bb[:], -PI32, PI32, op0=ALU.max, op1=ALU.min)
            self.act(bb[:], bb[:], AF.Sin)
            self.ts(Kf[0][:], w2[:], 1.0 / (2 * PI), 0.25)
            for hh in range(4):
                hsl = slice(hh * (L // 4), (hh + 1) * (L // 4))
                mk.v.tensor_copy(ibuf[:], Kf[0][:, hsl])
                mk.v.tensor_copy(Kf[0][:, hsl], ibuf[:])
            self.stt(Kf[0][:], Kf[0][:], -2 * PI, w2[:])
            self.ts(Kf[1][:], Kf[0][:], PI32 / 2, op0=ALU.is_gt)
            self.stt(Kf[0][:], Kf[1][:], -2 * PI, Kf[0][:])
            self.ts(Kf[0][:], Kf[0][:], -PI32 - PI / 2, PI32 - PI / 2,
                    op0=ALU.max, op1=ALU.min)
            self.act(Kf[0][:], Kf[0][:], AF.Sin, bias=PI / 2)
            self.tt(w2[:], w1[:], Kf[0][:])
            self.tt(Kf[1][:], w1[:], w3[:], ALU.add)
            self.ts(w2[:], w2[:], hc("mos"))
            self.stt(ab[:], Kf[1][:], hc("mhy_h"), w2[:])
            self.tt(Kf[1][:], w1[:], w3[:], ALU.subtract)
            self.tt(w1[:], w1[:], bb[:])
            self.ts(w1[:], w1[:], hc("irs_o"))
            self.stt(bb[:], Kf[1][:], hc("irs_h2"), w1[:])

            self.tt(w1[:], dt_bc[:], rd[:])
            self.tt(w2[:], w1[:], bch[:])
            self.tt(w1[:], dt_bc[:], w2[:])
            mk.dma("y", bch[:], self.sc_bc.ap()[h * 512:h * 512 + 128, :])
            self.tt(w3[:], dt_bc[:], bch[:])
            self.tt(w3[:], w3[:], w1[:], ALU.add)
            self.tt(Kf[0][:], ab[:], w3[:])
            self.tt(Kf[1][:], ab[:], w2[:])
            self.ts(w1[:], bb[:], -1.0)
            self.tt(Kf[2][:], w1[:], w3[:])
            self.tt(Kf[3][:], w1[:], w2[:])
            mk.dma("y", bch[:], self.sc_bc.ap()[h * 512 + 384:h * 512 + 512, :])
            self.ts(w1[:], bch[:], hc("negA"))
            self.ts(w2[:], bch[:], hc("k2n"))
            self.tt(w3[:], ab[:], bch[:])
            self.ts(Qf[3][:], bb[:], hc("sig"))
            self.tt(Qf[3][:], Qf[3][:], bch[:])
            mk.dma("y", bch[:], self.sc_bc.ap()[h * 512 + 256:h * 512 + 384, :])
            self.stt(w1[:], bch[:], hc("k2"), w1[:])
            self.tt(w2[:], bch[:], w2[:], ALU.add)
            self.tt(Qf[1][:], bb[:], w2[:])
            self.tt(Qf[1][:], w3[:], Qf[1][:], ALU.add)
            self.tt(Qf[1][:], Qf[1][:], idet[:])
            self.tt(w3[:], ab[:], bch[:])
            self.tt(Qf[0][:], bb[:], w1[:])
            self.tt(Qf[0][:], w3[:], Qf[0][:], ALU.add)
            self.tt(Qf[0][:], Qf[0][:], idet[:])
            self.ts(w3[:], bb[:], hc("sig"))
            self.tt(Qf[2][:], w3[:], bch[:])
            self.tt(w1[:], ab[:], w1[:])
            self.tt(Qf[2][:], Qf[2][:], w1[:], ALU.add)
            self.tt(Qf[2][:], Qf[2][:], idet[:])
            self.tt(w2[:], ab[:], w2[:])
            self.tt(Qf[3][:], Qf[3][:], w2[:], ALU.add)
            self.tt(Qf[3][:], Qf[3][:], idet[:])

            mk.v.tensor_copy(eslot(0), bass.AP(ab, TC - 1, [[L, 128], [TC, 16]]))
            mk.v.tensor_copy(eslot(1), bass.AP(bb, TC - 1, [[L, 128], [TC, 16]]))
            mk.v.tensor_copy(eslot(2), bass.AP(idet, TC - 1, [[L, 128], [TC, 16]]))
            self.tt(eslot(3), eslot(0), eslot(2))
            self.tt(eslot(4), eslot(1), eslot(2))
            self.stt(eslot(5), eslot(4), hc("k2"), eslot(3))
            self.ts(eslot(6), eslot(4), hc("sig"))
            self.stt(eslot(7), eslot(3), hc("k2"), eslot(6))
            self.ts(eslot(8), eslot(4), hc("negA"))
            self.stt(eslot(9), eslot(4), hc("k2n"), eslot(3))
            self.ts(eslot(10), eslot(3), hc("negA"))
            self.stt(eslot(11), eslot(3), hc("k2n"), eslot(6))
            self.stt(eslot(12), eslot(1), hc("k2"), eslot(0))
            self.tt(eslot(12), eslot(12), eslot(2))
            self.ts(eslot(13), eslot(4), hc("negA"))
            self.stt(eslot(14), eslot(1), hc("k2n"), eslot(0))
            self.tt(eslot(14), eslot(14), eslot(2))
            DQ = {1: 5, 2: 4, 3: 7, 4: 3}
            FQ = {1: 8, 2: 9, 3: 10, 4: 11}

            Y, V = Yst[0], Vst[0]
            Y2, V2 = Yst[1], Vst[1]
            for c in range(NCH):
                sl = slice(c * TC, (c + 1) * TC)
                mk.dma("y", utc[:], self.sc_ut.ap()[h * L + c * TC:
                                                    h * L + (c + 1) * TC, :])
                self.ts(utc[:], utc[:],
                        self.rn_tm[:, c * 64 + 2 * h:c * 64 + 2 * h + 1])
                pg = self.psb[PG]
                for j in range(4):
                    mk.t.matmul(pg[0:128, 0:128], Kf[j][:, sl], Qf[j][:, sl],
                                start=(j == 0), stop=(j == 3))
                self.tt(gsb[:], pg[0:128, 0:128], self.tri_incl[:])
                po = self.psb[PO]
                oc = (c % 4) * 128
                mk.t.matmul(po[0:64, oc:oc + 128], utc[:], gsb[:],
                            start=True, stop=(c == 0))
                if c > 0:
                    mk.t.matmul(po[0:64, oc:oc + 128], Y[:, 0:64], Qf[0][:, sl],
                                start=False, stop=False)
                    mk.t.matmul(po[0:64, oc:oc + 128], V[:, 0:64], Qf[1][:, sl],
                                start=False, stop=True)
                pky, pkv = self.psb[PKY], self.psb[PKV]
                for j in (1, 2, 3, 4):
                    self.ts(ktmp[:], Kf[j - 1][:, sl], ecol(DQ[j], c))
                    mk.t.matmul(pky[0:128, 0:128], ktmp[:], self.ident[:],
                                is_transpose=True, start=(j == 1), stop=(j == 4))
                mk.s.copy(kysb[:], pky[0:128, 0:128])
                pya, pva = self.psb[PYA], self.psb[PVA]
                mk.t.matmul(pya[0:128, 0:64], kysb[:], utc[:],
                            start=True, stop=True)
                for j in (1, 2, 3, 4):
                    self.ts(ktmp[:], Kf[j - 1][:, sl], ecol(FQ[j], c))
                    mk.t.matmul(pkv[0:128, 0:128], ktmp[:], self.ident[:],
                                is_transpose=True, start=(j == 1), stop=(j == 4))
                mk.s.copy(kysb[:], pkv[0:128, 0:128])
                mk.t.matmul(pva[0:128, 0:64], kysb[:], utc[:],
                            start=True, stop=True)
                if c == 0:
                    mk.s.copy(Y2[:], pya[0:128, 0:64])
                    mk.s.copy(V2[:], pva[0:128, 0:64])
                else:
                    self.stt(sYt[:], Y[:], ecol(12, c), pya[0:128, 0:64])
                    self.stt(Y2[:], V[:], ecol(4, c), sYt[:])
                    self.stt(sYt[:], Y[:], ecol(13, c), pva[0:128, 0:64])
                    self.stt(V2[:], V[:], ecol(14, c), sYt[:])
                Y, Y2 = Y2, Y
                V, V2 = V2, V
                if c % 4 == 3:
                    g0 = (c - 3) * TC
                    self.tt(ygb[0:64, g0:g0 + 512], po[0:64, 0:512],
                            rCb[0:64, g0:g0 + 512])
            mk.dma("s", self.sc_yg.ap()[h * 64:(h + 1) * 64, :], ygb[0:64, :])

    def phase_wout(self):
        mk = self.mk
        ygs = [self.ar(f"ygs{i}", [128, 512]) for i in range(16)]
        uzt = self.ar("uzt", [128, 512])
        wsl = [self.ar(f"wsl{i}", [128, 16 * 128]) for i in range(2)]
        sout = self.ar("sout", [128, 512])
        for tg in range(4):
            for kc in range(16):
                mk.dma("y", ygs[kc][:],
                       self.sc_yg.ap()[kc * 128:(kc + 1) * 128,
                                       tg * 512:(tg + 1) * 512])
                mk.dma("y", uzt[:], self.sc_u.ap()[kc * 128:(kc + 1) * 128,
                                                   tg * 512:(tg + 1) * 512])
                self.stt(ygs[kc][:], uzt[:], self.skip_c[:, kc:kc + 1],
                         ygs[kc][:])
                mk.dma("y", uzt[:], self.sc_z.ap()[kc * 128:(kc + 1) * 128,
                                                   tg * 512:(tg + 1) * 512])
                self.tt(ygs[kc][:], ygs[kc][:], uzt[:])
            for m in range(8):
                wb = wsl[m % 2]
                if tg == 0 or True:
                    for kc in range(16):
                        mk.dma("y", wb[:, kc * 128:(kc + 1) * 128],
                               self.WoutT.ap()[kc * 128:(kc + 1) * 128,
                                               m * 128:(m + 1) * 128])
                ps = self.psb[m % 2]
                for kc in range(16):
                    mk.t.matmul(ps[0:128, 0:512], wb[:, kc * 128:(kc + 1) * 128],
                                ygs[kc][:], start=(kc == 0), stop=(kc == 15))
                mk.v.tensor_copy(sout[:], ps[0:128, 0:512])
                mk.dma("s", self.sc_s.ap()[m * 128:(m + 1) * 128,
                                           tg * 512:(tg + 1) * 512], sout[:])

    def phase_final(self):
        mk = self.mk
        xt = self.ar("fxt", [128, D])
        st = self.ar("fst", [128, D])
        gt = self.ar("fgt", [128, D])
        sfm = self.ar("sfm", [128, D])
        gfm = self.ar("gfm", [128, D])
        for ti in range(16):
            mk.dma("y", xt[:], self.x_in.ap()[ti * 128:(ti + 1) * 128, :])
            for dc in range(8):
                mk.dma("y", sfm[:, dc * 128:(dc + 1) * 128],
                       self.sc_s.ap()[dc * 128:(dc + 1) * 128,
                                      ti * 128:(ti + 1) * 128])
                mk.dma("y", gfm[:, dc * 128:(dc + 1) * 128],
                       self.sc_g.ap()[dc * 128:(dc + 1) * 128,
                                      ti * 128:(ti + 1) * 128])
            for dc in range(8):
                self.trans128(st[:, dc * 128:(dc + 1) * 128],
                              sfm[:, dc * 128:(dc + 1) * 128], bank=dc % 2,
                              rows=128, cols=128)
                self.trans128(gt[:, dc * 128:(dc + 1) * 128],
                              gfm[:, dc * 128:(dc + 1) * 128], bank=2 + dc % 2,
                              rows=128, cols=128, eng="s")
            self.act(gt[:], gt[:], AF.Silu)
            self.tt(st[:], st[:], gt[:])
            self.tt(st[:], st[:], xt[:], ALU.add)
            mk.dma("s", self.out.ap()[ti * 128:(ti + 1) * 128, :], st[:])

    def ffn_load_cols(self, dst, dram_row_ap, n, off=0):
        tmp = self.ar(f"flc_{self.uid()}", [max(n, 1), 128])
        self.mk.dma("y", tmp[0:n, :],
                    dram_row_ap.rearrange("o (c p) -> (o c) p", p=128))
        ps = self.psb[7]
        self.mk.t.transpose(ps[0:128, 0:n], tmp[0:n, :], self.ident[0:n, 0:n])
        self.mk.v.tensor_copy(dst[:, off:off + n], ps[0:128, 0:n])

    def phase_ffn_prep(self):
        mk = self.mk
        g = mk.g
        ar = self.ar
        self.t_row = ar("t_row", [1, 512])
        self.pw_row = ar("pw_row", [1, 512])
        self.if_row = ar("if_row", [1, 128])
        self.lb_row = ar("lb_row", [1, 128])
        dr = ar("dr", [1, 128])
        g.iota(dr[:], [[1, 128]], channel_multiplier=0,
               allow_small_or_imprecise_dtypes=True)
        tmp = ar("drt", [1, 128])
        ib2 = ar("ib2", [1, 128], I32)
        self.ts(tmp[:], dr[:], 1.0 / 32.0)
        mk.v.tensor_copy(ib2[:], tmp[:])
        mk.v.tensor_copy(tmp[:], ib2[:])
        self.stt(dr[:], tmp[:], -32.0, dr[:])
        self.ts(tmp[:], dr[:], 0.0, op0=ALU.is_lt)
        self.stt(dr[:], tmp[:], 32.0, dr[:])
        self.ts(self.if_row[:], dr[:], -2.0 / 64.0 * float(np.log(10000.0)))
        self.act(self.if_row[:], self.if_row[:], AF.Exp)
        self.ts(self.lb_row[:], dr[:], 2.0 / 89.6, 25.6 / 89.6)
        self.act(self.lb_row[:], self.lb_row[:], AF.Ln)
        self.emask = ar("emask", [128, 1408])
        g.memset(self.emask[:], 0.0)
        g.affine_select(self.emask[:], self.emask[:], [[1, 1408]], ALU.is_ge,
                        NEG_BIG, base=-384, channel_multiplier=-1)
        g.affine_select(self.emask[:], self.emask[:], [[-1, 1408]], ALU.is_ge,
                        NEG_BIG, base=896, channel_multiplier=1)
        self.esel = ar("esel", [8, 4 * 128])
        g.memset(self.esel[:], 0.0)
        for dc in range(4):
            sl = self.esel[:, dc * 128:(dc + 1) * 128]
            g.memset(sl, 1.0)
            g.affine_select(sl, sl, [[1, 128]], ALU.is_ge, 0.0,
                            base=128 * dc, channel_multiplier=-64)
            g.affine_select(sl, sl, [[-1, 128]], ALU.is_ge, 0.0,
                            base=63 - 128 * dc, channel_multiplier=64)
        self.sgnc = ar("sgnc", [128, 1])
        g.memset(self.sgnc[:], 1.0)
        g.memset(self.sgnc[0:32, 0:1], -1.0)
        g.memset(self.sgnc[64:96, 0:1], -1.0)

    def phase_ffn(self, cut=None):
        mk = self.mk
        xf = self.ar("xf", [128, 8 * L])
        for dc in range(8):
            mk.dma("y", xf[:, dc * L:(dc + 1) * L],
                   self.sc_xn.ap()[dc * 128:(dc + 1) * 128, :])
        self.phase_ffn_prep()
        ffn_mark = self.arena_mark()
        done = False
        for l in range(2):
            self.ffn_attn(l, xf)
            self.arena_reset(ffn_mark)
            if cut == f"attn{l}":
                done = True
                break
            self.ffn_mlp(l, xf)
            self.arena_reset(ffn_mark)
            if cut == f"mlp{l}":
                done = True
                break
        for dc in range(8):
            mk.dma("s", self.sc_g.ap()[dc * 128:(dc + 1) * 128, :],
                   xf[:, dc * L:(dc + 1) * L])

    def ffn_attn(self, l, xf):
        mk = self.mk
        ar = self.ar
        qs_c = ar("qs_c", [128, 2])
        qs_s = ar("qs_s", [128, 2])
        t2 = ar("t2r", [2, 64])
        mk.dma("y", t2[0:1, :], self.qscale.ap()[l])
        mk.dma("y", t2[1:2, :], self.kscale.ap()[l])
        ps7 = self.psb[7]
        mk.t.transpose(ps7[0:64, 0:2], t2[0:2, 0:64], self.ident[0:2, 0:2])
        mk.v.tensor_copy(qs_c[0:64, 0:2], ps7[0:64, 0:2])
        mk.dma("g", qs_c[64:128, 0:2], qs_c[0:64, 0:2])
        for base in (0, 64):
            mk.dma("g", qs_s[base:base + 32, 0:2], qs_c[base + 32:base + 64, 0:2])
            mk.dma("g", qs_s[base + 32:base + 64, 0:2], qs_c[base:base + 32, 0:2])
        qw = ar("qw", [128, 4 * 512])
        kw = [ar("kw0", [128, 4 * 512]), ar("kw1", [128, 4 * 512])]
        qsh = ar("qsh", [128, 512])
        vtm = ar("vtm", [128, 8 * 520])
        tabs = ar("tabs", [128, 4 * 512])
        ssb = ar("ssb", [128, 8 * 512])
        sq = ar("sqf2", [128, 512])
        nrq = ar("nrq", [8, 512])
        nrk = ar("nrk", [8, 512])
        nst = ar("nst", [1, 512])
        m1 = ar("m1w", [128, 512])
        mxc = ar("mxc", [128, 4])
        mxe = ar("mxe", [4, 128])
        mxr = ar("mxr", [1, 512])
        dnr = ar("dnr", [8, 512])
        obuf = ar("obuf", [128, 4 * 512])
        wsl = ar("wslf", [128, 8 * 128])
        vsl = ar("vslf", [128, 2 * 512])
        ib3 = ar("ib3", [128, 512], I32)
        wf = m1
        bstash = ar("bstash", [128, 8 * 3])
        x0s = obuf
        PA, PB, PC, PS0, PS1, PV, PR = 0, 1, 2, 3, 4, 5, 6
        for ti in range(8):
            for hh in range(FH):
                mk.g.memset(vtm[:, ti * 520 + hh * 65 + 64:
                                ti * 520 + hh * 65 + 65], 1.0)

        W0 = self.wqkv.ap()[l]
        for w in range(4):
            t0 = w * FW
            kcur = kw[w % 2]
            kprev = kw[(w + 1) % 2]
            for i, dc in enumerate(range(4, 8)):
                if w == 0:
                    mk.g.memset(x0s[:, i * 512:i * 512 + 1], 0.0)
                else:
                    mk.v.tensor_copy(x0s[:, i * 512:i * 512 + 1],
                                     bstash[:, dc * 3 + (w - 1):dc * 3 + w])
                mk.v.tensor_copy(x0s[:, i * 512 + 1:(i + 1) * 512],
                                 xf[:, dc * L + t0: dc * L + t0 + 511])

            def rhs_x(dc, fr0, frw):
                if dc < 4:
                    return xf[:, dc * L + t0 + fr0: dc * L + t0 + fr0 + frw]
                return x0s[:, (dc - 4) * 512 + fr0:(dc - 4) * 512 + fr0 + frw]

            mk.g.iota(self.t_row[:], [[1, 512]], base=t0, channel_multiplier=0,
                      allow_small_or_imprecise_dtypes=True)
            self.ts(self.pw_row[:], self.t_row[:], 1.0 / 256.0, -1024.0 / 256.0)
            pt = self.psb[PA]
            mk.t.matmul(pt[0:128, 0:512], self.if_row[:],
                        self.t_row[0:1, 0:512], start=True, stop=True)
            for which, bias in (("sin", 0.0), ("cos", 0.25)):
                self.ts(wf[:], pt[0:128, 0:512], 1.0 / (2 * PI), bias)
                mk.v.tensor_copy(ib3[:], wf[:])
                mk.v.tensor_copy(wf[:], ib3[:])
                self.stt(wf[:], wf[:], -2 * PI, pt[0:128, 0:512])
                thr = PI32 if which == "sin" else PI32 / 2
                self.ts(sq[:], wf[:], thr, op0=ALU.is_gt)
                self.stt(wf[:], sq[:], -2 * PI, wf[:])
                if which == "sin":
                    self.ts(wf[:], wf[:], -PI32, PI32, op0=ALU.max, op1=ALU.min)
                    self.act(qsh[:], wf[:], AF.Sin)
                else:
                    self.ts(wf[:], wf[:], -PI32 - PI / 2, PI32 - PI / 2,
                            op0=ALU.max, op1=ALU.min)
                    self.act(tabs[:, 3 * 512:4 * 512], wf[:], AF.Sin,
                             bias=PI / 2)
            mk.t.matmul(pt[0:128, 0:512], self.lb_row[:],
                        self.pw_row[0:1, 0:512], start=True, stop=True)
            self.act(wf[:], pt[0:128, 0:512], AF.Exp)
            self.tt(tabs[:, 0:512], tabs[:, 3 * 512:4 * 512], wf[:])
            self.tt(tabs[:, 512:1024], qsh[:], wf[:])
            self.act(wf[:], pt[0:128, 0:512], AF.Exp, scale=-1.0)
            self.tt(tabs[:, 1024:1536], tabs[:, 3 * 512:4 * 512], wf[:])
            self.tt(tabs[:, 3 * 512:4 * 512], qsh[:], wf[:])
            self.ts(tabs[:, 0:512], tabs[:, 0:512], qs_c[:, 0:1])
            self.ts(tabs[:, 0:512], tabs[:, 0:512], 8.0)
            self.tt(tabs[:, 512:1024], tabs[:, 512:1024],
                    self.bcast_free(self.sgnc[:, 0:1], 512))
            self.ts(tabs[:, 512:1024], tabs[:, 512:1024], qs_s[:, 0:1])
            self.ts(tabs[:, 512:1024], tabs[:, 512:1024], 8.0)
            self.ts(tabs[:, 1024:1536], tabs[:, 1024:1536], qs_c[:, 1:2])
            self.tt(tabs[:, 3 * 512:4 * 512], tabs[:, 3 * 512:4 * 512],
                    self.bcast_free(self.sgnc[:, 0:1], 512))
            self.ts(tabs[:, 3 * 512:4 * 512], tabs[:, 3 * 512:4 * 512],
                    qs_s[:, 1:2])

            for side, dst in (("q", qw), ("k", kcur)):
                for mc in range(4):
                    col0 = (0 if side == "q" else FI) + mc * 128
                    for kc in range(8):
                        mk.dma("y", wsl[:, kc * 128:(kc + 1) * 128],
                               W0[kc * 128:(kc + 1) * 128, col0:col0 + 128])
                    ps = self.psb[PB]
                    for kc in range(8):
                        mk.t.matmul(ps[0:128, 0:512],
                                    wsl[:, kc * 128:(kc + 1) * 128],
                                    rhs_x(kc, 0, 512), start=(kc == 0),
                                    stop=(kc == 7))
                    pa = ps[0:128, 0:512]
                    mk.v.tensor_copy(dst[:, mc * 512:(mc + 1) * 512], pa)
                    self.act(sq[:], pa, AF.Square)
                    nps = self.psb[PC]
                    nrt = nrq if side == "q" else nrk
                    for hh in range(2):
                        b = hh * 64
                        mk.t.matmul(nps[0:1, 0:512],
                                    self.ones_col[b:b + 64, 0:1],
                                    sq[b:b + 64, :], start=True, stop=True)
                        mk.s.copy(nst[0:1, 0:512], nps[0:1, 0:512])
                        mk.dma("g", nrt[2 * mc + hh:2 * mc + hh + 1, :],
                               nst[0:1, 0:512])
            vpb = [PA, PS0, PS1, PV]
            for kc in range(8):
                vs = vsl[(kc % 2) * 512:(kc % 2) * 512 + 512] if False else \
                    vsl[:, (kc % 2) * 512:(kc % 2) * 512 + 512]
                mk.dma("y", vs, W0[kc * 128:(kc + 1) * 128, 2 * FI:3 * FI])
                for i in range(4):
                    mk.t.matmul(self.psb[vpb[i]][0:128, 0:512],
                                rhs_x(kc, i * 128, 128), vs,
                                start=(kc == 0), stop=(kc == 7))
            for i in range(4):
                ti = (4 * w + i) % 8
                mk.v.tensor_copy(
                    bass.AP(vtm, ti * 520, [[8 * 520, 128], [65, 8], [1, 64]]),
                    self.psb[vpb[i]][0:128, 0:512])
            for nrt in (nrq, nrk):
                self.ts(nrt[:], nrt[:], 1e-24, op0=ALU.max)
                self.act(nrt[:], nrt[:], AF.Ln)
                self.act(nrt[:], nrt[:], AF.Exp, scale=-0.5)
            for side, dst in (("q", qw), ("k", kcur)):
                nrt = nrq if side == "q" else nrk
                tb = 0 if side == "q" else 1024
                for mc in range(4):
                    csl = slice(mc * 512, (mc + 1) * 512)
                    pr = self.psb[PR]
                    mk.t.matmul(pr[0:128, 0:512],
                                self.esel[:, mc * 128:(mc + 1) * 128],
                                nrt[0:8, :], start=True, stop=True)
                    self.tt(dst[:, csl], dst[:, csl], pr[0:128, 0:512])
                    for b in (0, 64):
                        mk.dma("g", qsh[b:b + 32, :], dst[b + 32:b + 64, csl])
                        mk.dma("g", qsh[b + 32:b + 64, :], dst[b:b + 32, csl])
                    self.tt(qsh[:], qsh[:], tabs[:, tb + 512:tb + 1024])
                    self.tt(dst[:, csl], dst[:, csl], tabs[:, tb:tb + 512])
                    self.tt(dst[:, csl], dst[:, csl], qsh[:], ALU.add)

            for h in range(FH):
                hb = (h % 2) * 64
                mc = h // 2
                csl = slice(mc * 512, (mc + 1) * 512)
                rlist = list(range(4, 8)) if w == 0 else list(range(8))
                qs_ap = qw[hb:hb + 64, csl]
                for idx, r in enumerate(rlist):
                    kbuf = kprev if r < 4 else kcur
                    co = mc * 512 + (r % 4) * 128
                    pss = self.psb[[PS0, PS1][idx % 2]]
                    mk.t.matmul(pss[0:128, 0:512], kbuf[hb:hb + 64, co:co + 128],
                                qs_ap, start=True, stop=True)
                    self.tt(ssb[:, r * 512:(r + 1) * 512], pss[0:128, 0:512],
                            self.emask[:, 896 - 128 * r:1408 - 128 * r],
                            ALU.add)
                nr_ = len(rlist)
                r0_ = rlist[0]
                view = bass.AP(ssb, r0_ * 512,
                               [[8 * 512, 128], [1, 512], [512, nr_]])
                mk.v.reduce_max(m1[:, 0:512], view, axis=AX.X)
                for qsub in range(4):
                    ps = self.psb[PR]
                    mk.t.transpose(ps[0:128, 0:128],
                                   m1[:, qsub * 128:(qsub + 1) * 128],
                                   self.ident[:])
                    mk.v.reduce_max(mxc[:, qsub:qsub + 1], ps[0:128, 0:128],
                                    axis=AX.X)
                ps = self.psb[PR]
                mk.t.transpose(ps[0:4, 0:128], mxc[:, 0:4], self.ident[:])
                mk.v.tensor_copy(mxe[0:4, 0:128], ps[0:4, 0:128])
                mk.dma("g", bass.AP(mxr, 0, [[512, 1], [128, 4], [1, 128]]),
                       bass.AP(mxe, 0, [[128, 4], [1, 128]]))
                mk.t.matmul(ps[0:128, 0:512], self.ones_row1[0:1, 0:128],
                            mxr[0:1, 0:512], start=True, stop=True)
                for r in rlist:
                    self.tt(ssb[:, r * 512:(r + 1) * 512],
                            ssb[:, r * 512:(r + 1) * 512], ps[0:128, 0:512],
                            ALU.subtract)
                self.act(bass.AP(ssb, r0_ * 512, [[8 * 512, 128], [1, nr_ * 512]]),
                         bass.AP(ssb, r0_ * 512, [[8 * 512, 128], [1, nr_ * 512]]),
                         AF.Exp)
                pv = self.psb[PV]
                for idx, r in enumerate(rlist):
                    ti = (4 * (w - 1) + r) % 8
                    mk.t.matmul(pv[0:65, 0:512],
                                bass.AP(vtm, ti * 520 + h * 65,
                                        [[8 * 520, 128], [1, 65]]),
                                ssb[:, r * 512:(r + 1) * 512],
                                start=(idx == 0), stop=(idx == nr_ - 1))
                if hb == 0:
                    mk.v.tensor_copy(obuf[0:64, csl], pv[0:64, 0:512])
                else:
                    mk.v.tensor_copy(sq[0:64, 0:512], pv[0:64, 0:512])
                    mk.dma("g", obuf[64:128, csl], sq[0:64, 0:512])
                mk.s.copy(m1[64:65, 0:512], pv[64:65, 0:512])
                mk.dma("g", dnr[h:h + 1, :], m1[64:65, 0:512])
            mk.v.reciprocal(dnr[:], dnr[:])
            for mc in range(4):
                csl = slice(mc * 512, (mc + 1) * 512)
                pr = self.psb[PR]
                mk.t.matmul(pr[0:128, 0:512],
                            self.esel[:, mc * 128:(mc + 1) * 128],
                            dnr[0:8, :], start=True, stop=True)
                self.tt(obuf[:, csl], obuf[:, csl], pr[0:128, 0:512])
            import os as _os
            if w == 0 and l == 0 and _os.environ.get("FFN_DBG"):
                mk.dma("s", self.sc_dbg.ap()[0:128, :], qw[:])
                mk.dma("s", self.sc_dbg.ap()[128:256, :], kcur[:])
                mk.dma("s", self.sc_dbg.ap()[256:384, :], obuf[:])
                mk.dma("s", self.sc_dbg.ap()[384:512, 0:1408], self.emask[:])
                mk.dma("s", self.sc_dbg.ap()[512:640, :], tabs[:])
                mk.dma("s", self.sc_dbg.ap()[640:648, 0:512], dnr[:])
                mk.dma("s", self.sc_dbg.ap()[648:656, 0:512], nrq[:])
                mk.dma("s", self.sc_dbg.ap()[656:664, 0:512], nrk[:])
                mk.dma("s", self.sc_dbg.ap()[792:920, :],
                       bass.AP(ssb, 2048, [[8 * 512, 128], [1, 2048]]))
                mk.dma("s", self.sc_dbg.ap()[920:1024, 0:520],
                       bass.AP(vtm, 0, [[8 * 520, 104], [1, 520]]))
            WOv = self.wo.ap()[l]
            for m in range(8):
                if w < 3:
                    mk.v.tensor_copy(bstash[:, m * 3 + w:m * 3 + w + 1],
                                     xf[:, m * L + t0 + 511:m * L + t0 + 512])
                for kc in range(4):
                    mk.dma("y", wsl[:, kc * 128:(kc + 1) * 128],
                           WOv[kc * 128:(kc + 1) * 128, m * 128:(m + 1) * 128])
                ps = self.psb[PB]
                for kc in range(4):
                    mk.t.matmul(ps[0:128, 0:512], wsl[:, kc * 128:(kc + 1) * 128],
                                obuf[:, kc * 512:(kc + 1) * 512],
                                start=(kc == 0), stop=(kc == 3))
                self.tt(xf[:, m * L + t0:m * L + t0 + 512],
                        xf[:, m * L + t0:m * L + t0 + 512], ps[0:128, 0:512],
                        ALU.add)

    def ffn_mlp(self, l, xf):
        mk = self.mk
        ar = self.ar
        lngb = ar("lngb", [128, 16])
        self.ffn_load_cols(lngb, self.ln_g.ap()[l], 8, off=0)
        self.ffn_load_cols(lngb, self.ln_b.ap()[l], 8, off=8)
        b1row = ar("b1row", [1, 2 * HID])
        b2row = ar("b2row", [1, D])
        mk.dma("y", b1row[:], self.b1.ap()[l])
        mk.dma("y", b2row[:], self.b2.ap()[l])
        xln = ar("xlnb", [128, 8 * 512])
        h1 = ar("h1b", [128, 22 * 512])
        gel = ar("gelb", [128, 512])
        sqm = ar("sqmb", [128, 512])
        rows = ar("rowsb", [1, 8 * 512])
        wsl = ar("wslm", [128, 8 * 128])
        w2s = ar("w2sb", [128, 22 * 128])
        mstash = ar("mstash", [128, 8 * 3])
        x0m = ar("x0m", [128, 4])
        PB0, PB1, PRA, PRB, PRC = 0, 1, 2, 3, 4
        hch = [(i * 128, 128) for i in range(21)] + [(21 * 128, HID - 21 * 128)]

        for tg in range(4):
            t0 = tg * 512
            for i, dc in enumerate(range(4, 8)):
                if tg == 0:
                    mk.g.memset(x0m[:, i:i + 1], 0.0)
                else:
                    mk.v.tensor_copy(x0m[:, i:i + 1],
                                     mstash[:, dc * 3 + tg - 1:dc * 3 + tg])

            def sx(dc, fr0, frw):
                if dc < 4:
                    return xf[:, dc * L + t0 + fr0: dc * L + t0 + fr0 + frw]
                return xf[:, dc * L + t0 + fr0 - 1: dc * L + t0 + fr0 - 1 + frw]

            pra, prb = self.psb[PRA], self.psb[PRB]
            for kc in range(8):
                dst = xln[:, kc * 512:(kc + 1) * 512]
                if kc < 4:
                    mk.v.tensor_copy(dst, xf[:, kc * L + t0: kc * L + t0 + 512])
                else:
                    mk.v.tensor_copy(xln[:, kc * 512 + 1:(kc + 1) * 512],
                                     xf[:, kc * L + t0: kc * L + t0 + 511])
                    mk.v.tensor_copy(xln[:, kc * 512:kc * 512 + 1],
                                     x0m[:, kc - 4:kc - 3])
            for kc in range(8):
                sl = xln[:, kc * 512:(kc + 1) * 512]
                mk.t.matmul(pra[0:1, 0:512], self.ones_col[:], sl,
                            start=(kc == 0), stop=(kc == 7))
                self.act(sqm[:], sl, AF.Square)
                mk.t.matmul(prb[0:1, 0:512], self.ones_col[:], sqm[:],
                            start=(kc == 0), stop=(kc == 7))
            mu = rows[0:1, 0:512]
            ivr = rows[0:1, 512:1024]
            t1 = rows[0:1, 1024:1536]
            self.ts(mu, pra[0:1, 0:512], 1.0 / D)
            self.act(t1, mu, AF.Square)
            self.stt(ivr, prb[0:1, 0:512], 1.0 / D, t1, op0=ALU.mult,
                     op1=ALU.subtract)
            self.act(ivr, ivr, AF.Ln, bias=1e-5)
            self.act(ivr, ivr, AF.Exp, scale=-0.5)
            prc = self.psb[PRC]
            mk.t.matmul(prc[0:128, 0:512], self.ones_row1[0:1, 0:128], mu,
                        start=True, stop=True)
            pra2 = self.psb[PRA]
            mk.t.matmul(pra2[0:128, 0:512], self.ones_row1[0:1, 0:128], ivr,
                        start=True, stop=True)
            for kc in range(8):
                sl = xln[:, kc * 512:(kc + 1) * 512]
                self.tt(sl, sl, prc[0:128, 0:512], ALU.subtract)
                self.tt(sl, sl, pra2[0:128, 0:512])
                self.ts(sl, sl, lngb[:, kc:kc + 1], lngb[:, 8 + kc:9 + kc],
                        op0=ALU.mult, op1=ALU.add)
            for mi, (m0, mw) in enumerate(hch):
                psA = self.psb[PB0]
                psB = self.psb[PB1]
                for half, ps in ((0, psA), (1, psB)):
                    col0 = m0 + half * HID
                    for kc in range(8):
                        mk.dma("y", wsl[:, kc * 128:kc * 128 + mw],
                               self.w1.ap()[l][kc * 128:(kc + 1) * 128,
                                               col0:col0 + mw])
                    for kc in range(8):
                        mk.t.matmul(ps[0:mw, 0:512],
                                    wsl[:, kc * 128:kc * 128 + mw],
                                    xln[:, kc * 512:(kc + 1) * 512],
                                    start=(kc == 0), stop=False)
                    mk.t.matmul(ps[0:mw, 0:512],
                                b1row[0:1, col0:col0 + mw],
                                self.ones_row1[0:1, 0:512],
                                start=False, stop=True)
                self.act(gel[0:mw, :], psB[0:mw, 0:512], AF.Gelu)
                self.tt(h1[0:mw, mi * 512:(mi + 1) * 512], psA[0:mw, 0:512],
                        gel[0:mw, :])
            for m in range(8):
                if tg < 3:
                    mk.v.tensor_copy(mstash[:, m * 3 + tg:m * 3 + tg + 1],
                                     xf[:, m * L + t0 + 511:m * L + t0 + 512])
                for ki, (k0, kwd) in enumerate(hch):
                    mk.dma("y", w2s[0:kwd, ki * 128:(ki + 1) * 128],
                           self.w2.ap()[l][k0:k0 + kwd, m * 128:(m + 1) * 128])
                ps = self.psb[PB0 if m % 2 == 0 else PB1]
                for ki, (k0, kwd) in enumerate(hch):
                    mk.t.matmul(ps[0:128, 0:512],
                                w2s[0:kwd, ki * 128:ki * 128 + 128],
                                h1[0:kwd, ki * 512:(ki + 1) * 512],
                                start=(ki == 0), stop=False)
                mk.t.matmul(ps[0:128, 0:512], b2row[0:1, m * 128:(m + 1) * 128],
                            self.ones_row1[0:1, 0:512], start=False, stop=True)
                self.tt(xf[:, m * L + t0:m * L + t0 + 512],
                        xf[:, m * L + t0:m * L + t0 + 512], ps[0:128, 0:512],
                        ALU.add)



_PROG_CACHE = {}


def _build_program():
    if "k" not in _PROG_CACHE:
        k = Kern()
        k.build_all()
        _PROG_CACHE["k"] = k
    return _PROG_CACHE["k"]


def kernel(L0_attn_x, L0_attn_W_in, L0_attn_dw_weight, L0_attn_dw_bias,
           L0_attn_W_out, L0_attn_skip, L0_attn_dt_bias, L0_attn_theta_raw,
           L0_attn_k_raw, L0_ffn_wqkv, L0_ffn_q_scale, L0_ffn_k_scale,
           L0_ffn_wo, L0_ffn_ln_g, L0_ffn_ln_b, L0_ffn_w1, L0_ffn_b1,
           L0_ffn_w2, L0_ffn_b2):
    k = _build_program()
    f32 = np.float32
    common = {
        "WinT": np.ascontiguousarray(L0_attn_W_in.T).astype(f32, copy=False),
        "dw_w": np.ascontiguousarray(L0_attn_dw_weight).astype(f32, copy=False),
        "dw_b": np.ascontiguousarray(L0_attn_dw_bias[None, :]).astype(f32, copy=False),
        "WoutT": np.ascontiguousarray(L0_attn_W_out.T).astype(f32, copy=False),
        "skip": np.ascontiguousarray(L0_attn_skip[None, :]).astype(f32, copy=False),
        "dt_bias": np.ascontiguousarray(L0_attn_dt_bias[None, :]).astype(f32, copy=False),
        "thetaT": np.ascontiguousarray(L0_attn_theta_raw.T).astype(f32, copy=False),
        "krawT": np.ascontiguousarray(L0_attn_k_raw.T).astype(f32, copy=False),
        "wqkv": np.ascontiguousarray(L0_ffn_wqkv).astype(f32, copy=False),
        "qscale": np.ascontiguousarray(L0_ffn_q_scale[:, None, :]).astype(f32, copy=False),
        "kscale": np.ascontiguousarray(L0_ffn_k_scale[:, None, :]).astype(f32, copy=False),
        "wo": np.ascontiguousarray(L0_ffn_wo).astype(f32, copy=False),
        "ln_g": np.ascontiguousarray(L0_ffn_ln_g[:, None, :]).astype(f32, copy=False),
        "ln_b": np.ascontiguousarray(L0_ffn_ln_b[:, None, :]).astype(f32, copy=False),
        "w1": np.ascontiguousarray(L0_ffn_w1).astype(f32, copy=False),
        "b1": np.ascontiguousarray(L0_ffn_b1[:, None, :]).astype(f32, copy=False),
        "w2": np.ascontiguousarray(L0_ffn_w2).astype(f32, copy=False),
        "b2": np.ascontiguousarray(L0_ffn_b2[:, None, :]).astype(f32, copy=False),
    }
    in_maps = []
    for b in range(8):
        m = dict(common)
        m["x"] = np.ascontiguousarray(L0_attn_x[b]).astype(f32, copy=False)
        in_maps.append(m)
    res = run_bass_kernel_spmd(k.nc, in_maps, core_ids=list(range(8)))
    return np.stack([res.results[b]["out"] for b in range(8)], axis=0)


# version 07
# speedup vs baseline: 1.2361x; 1.2361x over previous
import numpy as np

import concourse.bass as bass
import concourse.mybir as mybir
from concourse.bass_utils import run_bass_kernel_spmd

F32 = mybir.dt.float32
I32 = mybir.dt.int32
AF = mybir.ActivationFunctionType
ALU = mybir.AluOpType
AX = mybir.AxisListType

L = 2048
D = 1024
DI = 2048
NH = 32
NS = 128
PD = 64
TC = 128
NCH = L // TC
IN_DIM = 2 * DI + NH + NH * 4 * NS
HID = 2730
FH = 8
FD = 64
FI = 512
FW = 512

PI = float(np.pi)
PI32 = float(np.float32(np.pi))
NEG_BIG = -1e9



_RMW_METHODS = {"select", "copy_predicated"}
_DMA_QUEUE_SEM = {"t": "dmt", "v": "dmv", "s": "dms", "g": "dmg", "y": "dmy"}


def _ap_of(x):
    return x if isinstance(x, bass.AP) else None


def _tname(ap):
    return ap.tensor.name


class Op:
    __slots__ = ("engine", "method", "args", "kwargs", "outs", "ins",
                 "inc_token", "waits", "drain", "pos", "is_dma", "dma_q",
                 "_inc_val", "_dma_val", "seq", "dma_slot", "emitted")

    def __init__(self, engine, method, args, kwargs, outs, ins, is_dma=False,
                 dma_q=None):
        self.engine = engine
        self.method = method
        self.args = args
        self.kwargs = kwargs
        self.outs = outs
        self.ins = ins
        self.inc_token = False
        self.waits = []
        self.drain = False
        self.pos = -1
        self.is_dma = is_dma
        self.dma_q = dma_q
        self.emitted = False


class Eng:
    def __init__(self, mk, engine):
        self._mk = mk
        self._e = engine

    def __getattr__(self, method):
        def call(*args, **kwargs):
            return self._mk._record(self._e, method, args, kwargs)
        return call


class MK:
    ENGS = ("t", "v", "s", "g", "y")
    ENG_ATTR = {"t": "tensor", "v": "vector", "s": "scalar", "g": "gpsimd",
                "y": "sync"}

    def __init__(self, nc):
        self.nc = nc
        self.eng_ops = {e: [] for e in self.ENGS}
        self.writer = {}
        self.readers = {}
        self._seq = 0
        self._dma_count = {}
        self.dma_slots = {}
        self._rec_hw = {}
        self._drain_cov = {}
        self.NSLOT = 8
        self.t = Eng(self, "t")
        self.v = Eng(self, "v")
        self.s = Eng(self, "s")
        self.g = Eng(self, "g")
        self.y = Eng(self, "y")

    def _token_of(self, op):
        return ("dma", op.dma_q, op) if op.is_dma else ("eng", op.engine, op)

    def _add_dep(self, op, tok):
        if tok[0] == "eng" and tok[1] == op.engine:
            e = op.engine
            cov = self._drain_cov.get(e, -1)
            if tok[2].seq > cov:
                op.drain = True
                prev = self.eng_ops[e][-1] if self.eng_ops[e] else None
                self._drain_cov[e] = prev.seq if prev is not None else -1
            return
        hw = self._rec_hw.setdefault(op.engine, {})
        key = self._tok_key(tok)
        prev = hw.get(key)
        if prev is not None and prev.seq >= tok[2].seq:
            return
        hw[key] = tok[2]
        if tok[0] == "eng":
            tok[2].inc_token = True
        for t2 in op.waits:
            if t2[2] is tok[2]:
                return
        op.waits.append(tok)

    def _classify(self, method, args, kwargs):
        pos_aps = [_ap_of(a) for a in args]
        pos_aps = [a for a in pos_aps if a is not None]
        kw_out = []
        kw_in = []
        for k, a in kwargs.items():
            ap = _ap_of(a)
            if ap is None:
                continue
            if k in ("out",):
                kw_out.append(ap)
            elif k == "accum_out":
                kw_out.append(ap)
            else:
                kw_in.append(ap)
        if method in ("memset", "memzero", "iota"):
            outs = (kw_out + pos_aps)[:1]
            ins = []
        elif method in _RMW_METHODS:
            allaps = kw_out + pos_aps + kw_in
            outs = allaps[:1]
            ins = allaps
        else:
            if kw_out:
                outs = kw_out
                ins = pos_aps + kw_in
            else:
                outs = pos_aps[:1] + [a for a in kw_out]
                ins = pos_aps[1:] + kw_in
        return outs, ins

    def _tok_key(self, tok):
        if tok[0] == "eng":
            return ("e", tok[1])
        return ("d", tok[2].dma_q, tok[2].dma_slot)

    @staticmethod
    def _prange(ap):
        pitch, cnt = ap.ap[0]
        if pitch == 0:
            return (0, cnt)
        return (ap.tensor.base_partition + ap.offset // pitch, cnt)

    def _check_partitions(self, method, outs, ins):
        if method in ("matmul",):
            if len(ins) >= 2:
                a, b = self._prange(ins[0]), self._prange(ins[1])
                assert a == b, f"matmul operand partitions differ: {a} vs {b}"
            return
        if method in ("transpose", "iota", "memset", "memzero",
                      "partition_all_reduce", "partition_broadcast"):
            return
        rr = [self._prange(ap) for ap in outs + ins]
        assert all(r == rr[0] for r in rr), \
            f"{method}: operand partition ranges differ: {rr}"

    def _record(self, engine, method, args, kwargs):
        is_dma = method in ("dma_start", "dma_start_transpose")
        outs, ins = self._classify(method, args, kwargs)
        if not is_dma:
            self._check_partitions(method, outs, ins)
        self._last_real = getattr(self, "_last_real", {})
        op = Op(engine, method, args, kwargs, outs, ins, is_dma,
                _DMA_QUEUE_SEM[engine] if is_dma else None)
        op.seq = self._seq
        self._seq += 1
        if is_dma:
            q = op.dma_q
            c = self._dma_count.get(q, 0)
            self._dma_count[q] = c + 1
            op.dma_slot = c % self.NSLOT
            self.dma_slots.setdefault((q, op.dma_slot), []).append(op)
        for ap in op.ins:
            n = _tname(ap)
            for w in self.writer.get(n, {}).values():
                self._add_dep(op, w)
            if n.startswith("pb"):
                for r in self.readers.get(n, {}).values():
                    self._add_dep(op, r)
        for ap in op.outs:
            n = _tname(ap)
            for w in self.writer.get(n, {}).values():
                self._add_dep(op, w)
            for r in self.readers.get(n, {}).values():
                self._add_dep(op, r)
        tok = self._token_of(op)
        key = self._tok_key(tok)
        for ap in op.outs:
            n = _tname(ap)
            self.writer.setdefault(n, {})[key] = tok
            self.readers[n] = {}
        for ap in op.ins:
            self.readers.setdefault(_tname(ap), {})[key] = tok
        self.eng_ops[engine].append(op)
        return op

    def dma(self, q, out, in_, **kw):
        return getattr(self, q).dma_start(out=out, in_=in_, **kw)

    def barrier(self):
        last = {}
        for e in self.ENGS:
            if self.eng_ops[e]:
                last[e] = self.eng_ops[e][-1]
        dma_last = {k: lst[-1] for k, lst in self.dma_slots.items() if lst}
        for e in self.ENGS:
            nop = Op(e, "nop", (), {}, [], [])
            nop.seq = self._seq
            self._seq += 1
            for e2, lop in last.items():
                if e2 == e:
                    if not lop.is_dma:
                        nop.drain = True
                        continue
                if lop.emitted and not lop.is_dma:
                    continue
                self._add_dep(nop, self._token_of(lop))
            for (q, sl), dop in dma_last.items():
                self._add_dep(nop, ("dma", q, dop))
            self.eng_ops[e].append(nop)
        self.writer = {}
        self.readers = {}

    def _ensure_sems(self):
        if not hasattr(self, "_eng_sems"):
            self._eng_sems = {e: self.nc.alloc_semaphore(name=f"SE_{e}")
                              for e in self.ENGS}
            self._dma_sems = {}
            self._inc_no = {e: 0 for e in self.ENGS}
            self._emitted = {e: 0 for e in self.ENGS}
            self._wait_hw = {}

    def flush(self):
        nc = self.nc
        self._ensure_sems()
        eng_sems, dma_sems = self._eng_sems, self._dma_sems
        for e in self.ENGS:
            for op in self.eng_ops[e][self._emitted[e]:]:
                if op.is_dma:
                    key = (op.dma_q, op.dma_slot)
                    if key not in dma_sems:
                        dma_sems[key] = nc.alloc_semaphore(
                            name=f"SD_{key[0]}_{key[1]}")
                    lst = self.dma_slots[key]
                    op._dma_val = 16 * (lst.index(op) + 1)
                elif op.inc_token:
                    self._inc_no[e] += 1
                    op._inc_val = self._inc_no[e]

        mkself = self

        def emit_stream(engobj, e, start, end):
            for op in mkself.eng_ops[e][start:end]:
                for tok in op.waits:
                    if tok[0] == "eng":
                        engobj.wait_ge(eng_sems[tok[1]], tok[2]._inc_val)
                    else:
                        d = tok[2]
                        engobj.wait_ge(dma_sems[(d.dma_q, d.dma_slot)],
                                       d._dma_val)
                if op.drain:
                    engobj.drain()
                if op.is_dma and op._dma_val > 16:
                    engobj.wait_ge(dma_sems[(op.dma_q, op.dma_slot)],
                                   op._dma_val - 16)
                if op.method == "nop":
                    inst = engobj.nop()
                else:
                    inst = getattr(engobj, op.method)(*op.args, **op.kwargs)
                if op.is_dma:
                    inst.then_inc(dma_sems[(op.dma_q, op.dma_slot)], 16)
                elif op.inc_token:
                    inst.then_inc(eng_sems[e], 1)

        spans = {e: (self._emitted[e], len(self.eng_ops[e])) for e in self.ENGS}
        if all(a == b for a, b in spans.values()):
            return
        with nc.Block() as blk:
            for e in self.ENGS:
                a, b = spans[e]
                if a == b:
                    continue
                getattr(blk, self.ENG_ATTR[e])(
                    lambda engobj, e=e, a=a, b=b: emit_stream(engobj, e, a, b))
        for e in self.ENGS:
            for op in self.eng_ops[e][self._emitted[e]:spans[e][1]]:
                op.emitted = True
            self._emitted[e] = spans[e][1]

    def realize(self):
        self.barrier()
        self.flush()

    def n_ops(self):
        return sum(len(v) for v in self.eng_ops.values())



def cdiv(a, b):
    return -(-a // b)


class Builder:
    SBUF_BYTES = 207 * 1024

    def __init__(self):
        self.nc = bass.Bass("TRN2", target_bir_lowering=False)
        self.mk = MK(self.nc)
        self._dram = {}
        self._persist_bytes = 0
        self._arena_handle = None
        self._guards = []
        self._arena_off = 0
        self._arena_size = 0
        self._arena_peak = 0
        self._uid = 0

    def uid(self):
        self._uid += 1
        return self._uid

    def dram(self, name, shape, dtype=F32, kind="Internal"):
        h = self.nc.dram_tensor(name, list(shape), dtype, kind=kind)
        self._dram[name] = h
        return h

    def sb(self, name, shape, dtype=F32):
        h = self.nc.alloc_sbuf_tensor(name, list(shape), dtype)
        per_part = int(np.prod(shape[1:])) * mybir.dt.size(dtype)
        self._persist_bytes += ((per_part + 31) // 32) * 32
        return h

    def finish_persist(self):
        pass

    def ar(self, name, shape, dtype=F32):
        g = self.nc.sbuf_tensor(f"{name}_{self.uid()}", list(shape), dtype)
        h = g.__enter__()
        self._guards.append(g)
        per_part = int(np.prod(shape[1:])) * mybir.dt.size(dtype)
        self._arena_off += ((per_part + 31) // 32) * 32
        self._arena_peak = max(self._arena_peak, self._arena_off)
        return h

    def arena_mark(self):
        return len(self._guards)

    def arena_reset(self, mark=0):
        self.mk.barrier()
        self.mk.flush()
        while len(self._guards) > mark:
            self._guards.pop().__exit__(None, None, None)

    def psum(self, name, banks=1, part=128):
        return self.nc.alloc_psum_tensor(name, [part, 512 * banks], F32)

    def constcol(self, val):
        val = float(val)
        if not hasattr(self, "_cpool"):
            self._cpool = self.sb("CPOOL", [128, 8])
            self._cvals = {}
        if val not in self._cvals:
            i = len(self._cvals)
            assert i < 8
            self._cvals[val] = i
            self.mk.g.memset(self._cpool[:, i:i + 1], val)
        i = self._cvals[val]
        return self._cpool[:, i:i + 1]



class Kern(Builder):

    def __init__(self):
        super().__init__()
        self.declare_io()
        self.declare_static()
        self.constcol(2.0)
        self.psb = [self.psum(f"pb{i}") for i in range(8)]
        self.finish_persist()

    def declare_io(self):
        d = self.dram
        self.x_in = d("x", [L, D], kind="ExternalInput")
        self.WinT = d("WinT", [D, IN_DIM], kind="ExternalInput")
        self.dw_w = d("dw_w", [DI, 4], kind="ExternalInput")
        self.dw_b = d("dw_b", [1, DI], kind="ExternalInput")
        self.WoutT = d("WoutT", [DI, D], kind="ExternalInput")
        self.skip_in = d("skip", [1, DI], kind="ExternalInput")
        self.dt_bias = d("dt_bias", [1, NH], kind="ExternalInput")
        self.thetaT = d("thetaT", [NS, NH], kind="ExternalInput")
        self.krawT = d("krawT", [NS, NH], kind="ExternalInput")
        self.wqkv = d("wqkv", [2, D, 3 * FI], kind="ExternalInput")
        self.qscale = d("qscale", [2, 1, FD], kind="ExternalInput")
        self.kscale = d("kscale", [2, 1, FD], kind="ExternalInput")
        self.wo = d("wo", [2, FI, D], kind="ExternalInput")
        self.ln_g = d("ln_g", [2, 1, D], kind="ExternalInput")
        self.ln_b = d("ln_b", [2, 1, D], kind="ExternalInput")
        self.w1t = d("w1t", [88, 128, 1024], kind="ExternalInput")
        self.b1 = d("b1", [2, 1, 2 * HID], kind="ExternalInput")
        self.w2t = d("w2t", [16, 128, 2816], kind="ExternalInput")
        self.b2 = d("b2", [2, 1, D], kind="ExternalInput")
        self.out = d("out", [L, D], kind="ExternalOutput")
        self.sc_bc = d("sc_bc", [NH * 4 * NS, L])
        self.sc_u = d("sc_u", [DI, L])
        self.sc_ut = d("sc_ut", [NH * L, PD])
        self.sc_z = d("sc_z", [DI, L])
        self.sc_yg = d("sc_yg", [DI, L])
        self.sc_s = d("sc_s", [D, L])
        self.sc_g = d("sc_g", [D, L])
        self.sc_xn = d("sc_xn", [D, L])
        self.sc_dbg = d("sc_dbg", [1024, 2048])

    def declare_static(self):
        sb = self.sb
        self.ones_col = sb("ones_col", [128, 1])
        self.ones_row1 = sb("ones_row1", [1, 512])
        self.ident = sb("ident", [128, 128])
        self.tri_incl = sb("tri_incl", [128, 128])
        self.hpar = sb("hpar", [NS, 16 * NH])
        self.dwW = sb("dwW", [128, 16 * 4])
        self.dwB = sb("dwB", [128, 16])
        self.skip_c = sb("skip_c", [128, 16])
        self.dtb_col = sb("dtb_col", [NH, 1])

    HP = {name: i for i, name in enumerate(
        ["A", "k2", "sig", "rs", "mos", "mhy",
         "irs_o", "irs_h2", "negA", "k2n", "mos_h", "mhy_h", "kf"])}

    def hp(self, name):
        i = self.HP[name]
        return self.hpar[:, i:i + 1]

    def ts(self, out, in0, s1, s2=None, op0=ALU.mult, op1=ALU.add, eng="v"):
        e = getattr(self.mk, eng)
        if s2 is None:
            return e.tensor_scalar(out, in0, s1, None, op0)
        return e.tensor_scalar(out, in0, s1, s2, op0, op1)

    def stt(self, out, in0, s, in1, op0=ALU.mult, op1=ALU.add, eng="v"):
        e = getattr(self.mk, eng)
        return e.scalar_tensor_tensor(out, in0, s, in1, op0, op1)

    def tt(self, out, a, b, op=ALU.mult, eng="v"):
        e = getattr(self.mk, eng)
        return e.tensor_tensor(out, a, b, op)

    def act(self, out, in_, func, bias=0.0, scale=1.0, accum_out=None):
        kw = {}
        if accum_out is not None:
            kw["accum_out"] = accum_out
        if isinstance(bias, float) and bias not in (0.0, 1.0):
            p = out.ap[0][1] if isinstance(out, bass.AP) else 128
            bias = self.constcol(bias)[0:p, 0:1]
        return self.mk.s.activation(out=out, in_=in_, func=func, bias=bias,
                                    scale=scale, **kw)

    def bcast_free(self, colap, n):
        t = colap.tensor
        psz = colap.ap[0]
        return bass.AP(t, colap.offset, [list(psz), [0, n]])

    def phase_alloc0(self):
        self.stripe01 = self.ar("stripe01", [128, L])
        self.dt_fm = self.ar("dt_fm", [NH, L])
        self.rn_sq = self.ar("rn_sq", [64, L])
        self.rn_tm = self.ar("rn_tm", [128, NCH * 64])
        g = self.mk.g
        g.memset(self.stripe01[:], 1.0)
        g.memset(bass.AP(self.stripe01, 0, [[L, 128], [TC, NCH]]), 0.0)
        return self.arena_mark()

    def build_all(self):
        self.phase_prep()
        self.arena_reset()
        mA = self.phase_alloc0()
        self.phase_ln()
        mB = mA + 1
        self.arena_reset(mB)
        self.phase_dt()
        self.phase_bc_norms()
        self.phase_norm_finish()
        self.arena_reset(mB)
        self.phase_z_xc_u()
        self.arena_reset(mA)
        self.phase_ssm_core()
        self.arena_reset(0)
        self.phase_wout()
        self.arena_reset(0)
        self.phase_ffn()
        self.arena_reset(0)
        self.phase_final()
        self.mk.realize()

    def hpc(self, name, h):
        i = self.HP[name]
        return self.hpar[:, i * NH + h:i * NH + h + 1]

    def hpall(self, name):
        i = self.HP[name]
        return self.hpar[:, i * NH:(i + 1) * NH]

    def phase_prep(self):
        mk, nc = self.mk, self.nc
        g = mk.g
        g.memset(self.ones_col[:], 1.0)
        g.memset(self.ones_row1[:], 1.0)
        g.memset(self.tri_incl[:], 1.0)
        g.affine_select(self.ident[:], self.tri_incl[:], [[1, 128]],
                        ALU.is_ge, 0.0, base=0, channel_multiplier=-1)
        g.affine_select(self.ident[:], self.ident[:], [[-1, 128]],
                        ALU.is_ge, 0.0, base=0, channel_multiplier=1)
        g.affine_select(self.tri_incl[:], self.tri_incl[:], [[1, 128]],
                        ALU.is_ge, 0.0, base=0, channel_multiplier=-1)

        th_raw = self.ar("th_raw", [NS, NH])
        k_raw = self.ar("k_raw", [NS, NH])
        mk.dma("y", th_raw[:], self.thetaT.ap())
        mk.dma("y", k_raw[:], self.krawT.ap())
        A, k2, sig, rs = (self.hpall(n) for n in ("A", "k2", "sig", "rs"))
        def sigmoid_chain(dst, src_ap):
            self.act(dst, src_ap, AF.Exp, scale=-1.0)
            self.act(dst, dst, AF.Ln, bias=1.0)
            self.act(dst, dst, AF.Exp, scale=-1.0)
        sigmoid_chain(A, th_raw[:])
        self.act(A, A, AF.Square, scale=PI)
        sigmoid_chain(k2, k_raw[:])
        self.ts(k2, k2, 0.25)
        self.act(sig, k2, AF.Square)
        self.tt(sig, sig, A, ALU.subtract)
        ab_t = self.ar("abs_sig", [NS, NH])
        self.ts(ab_t[:], sig, -1.0)
        self.tt(ab_t[:], ab_t[:], sig, ALU.max)
        self.ts(ab_t[:], ab_t[:], 1e-30, op0=ALU.max)
        self.act(rs, ab_t[:], AF.Ln)
        self.act(rs, rs, AF.Exp, scale=0.5)
        mos, mhy = self.hpall("mos"), self.hpall("mhy")
        self.ts(mos, sig, 0.0, op0=ALU.is_lt)
        self.ts(mhy, mos, -1.0, 1.0, op0=ALU.mult, op1=ALU.add)
        irs = self.ar("irs", [NS, NH])
        mk.v.reciprocal(irs[:], rs)
        self.tt(self.hpall("irs_o"), mos, irs[:], ALU.mult)
        t2 = self.ar("t2", [NS, NH])
        self.ts(t2[:], irs[:], 0.5)
        self.tt(self.hpall("irs_h2"), mhy, t2[:], ALU.mult)
        self.ts(self.hpall("negA"), A, -1.0)
        self.ts(self.hpall("k2n"), k2, -1.0)
        self.ts(self.hpall("mos_h"), mos, 0.5)
        self.ts(self.hpall("mhy_h"), mhy, 0.5)
        self.ts(self.hpall("kf"), k2, 2.0)

        for c in range(16):
            mk.dma("y", self.dwW[:, c * 4:(c + 1) * 4],
                   self.dw_w.ap()[c * 128:(c + 1) * 128, :])
        self.load_cols16(self.dwB, self.dw_b)
        self.load_cols16(self.skip_c, self.skip_in)
        mk.dma("y", self.dtb_col[:, 0:1],
               bass.AP(self.dt_bias, 0, [[1, NH], [1, 1]]))

    def load_cols16(self, dst, dram_row):
        mk = self.mk
        tmp = self.ar(f"lc16_{self.uid()}", [16, 128])
        mk.dma("y", tmp[:], dram_row.ap().rearrange("o (c p) -> (o c) p", p=128))
        ps = self.psb[7]
        mk.t.transpose(ps[0:128, 0:16], tmp[:], self.ident[0:16, 0:16])
        mk.v.tensor_copy(dst[:, 0:16], ps[0:128, 0:16])

    def trans128(self, dst_sb_ap, src_sb_ap, bank, rows, cols, eng="v", bp=0):
        ps = self.psb[bank]
        self.mk.t.transpose(ps[0:cols, 0:rows], src_sb_ap,
                            self.ident[bp:bp + rows, bp:bp + rows])
        if eng == "s":
            self.mk.s.copy(dst_sb_ap, ps[0:cols, 0:rows])
        else:
            getattr(self.mk, eng).tensor_copy(dst_sb_ap, ps[0:cols, 0:rows])

    def phase_ln(self):
        mk = self.mk
        self.xn = self.ar("xn", [128, 8 * L])
        xtm = self.ar("xtm", [128, D])
        xsq = self.ar("xsq", [128, D])
        cols = self.ar("lncols", [128, 8])
        for ti in range(16):
            mk.dma("y", xtm[:], self.x_in.ap()[ti * 128:(ti + 1) * 128, :])
            ssum = cols[:, 0:1]
            ssq = cols[:, 1:2]
            mu = cols[:, 2:3]
            musq = cols[:, 3:4]
            var = cols[:, 4:5]
            isd = cols[:, 5:6]
            mk.v.reduce_sum(ssum, xtm[:], axis=AX.X)
            self.act(xsq[:], xtm[:], AF.Square, accum_out=ssq)
            self.ts(mu, ssum, 1.0 / D)
            self.act(musq, mu, AF.Square)
            self.stt(var, ssq, 1.0 / D, musq, op0=ALU.mult, op1=ALU.subtract)
            self.act(var, var, AF.Ln, bias=1e-5)
            self.act(isd, var, AF.Exp, scale=-0.5)
            self.ts(xtm[:], xtm[:], mu, op0=ALU.subtract)
            self.ts(xtm[:], xtm[:], isd, op0=ALU.mult)
            for dc in range(8):
                self.trans128(self.xn[:, dc * L + ti * 128: dc * L + ti * 128 + 128],
                              xtm[:, dc * 128:(dc + 1) * 128], bank=dc % 2,
                              rows=128, cols=128)
        for dc in range(8):
            mk.dma("s", self.sc_xn.ap()[dc * 128:(dc + 1) * 128, :],
                   self.xn[:, dc * L:(dc + 1) * L])

    def load_wslab(self, dst, wT_dram, m0, mw, name="w"):
        for kc in range(8):
            self.mk.dma("y", dst[:, kc * mw:(kc + 1) * mw],
                        wT_dram.ap()[kc * 128:(kc + 1) * 128, m0:m0 + mw])

    def proj_mtile(self, wbuf, mw, bank, evac):
        mk = self.mk
        ps = self.psb[bank]
        for tg in range(4):
            for kc in range(8):
                mk.t.matmul(ps[0:mw, 0:512],
                            wbuf[:, kc * mw:(kc + 1) * mw],
                            self.xn[:, kc * L + tg * 512: kc * L + (tg + 1) * 512],
                            start=(kc == 0), stop=(kc == 7))
            evac(tg, ps[0:mw, 0:512])

    def phase_dt(self):
        mk = self.mk
        wdt = self.ar("wdt", [128, 8 * NH])
        self.load_wslab(wdt, self.WinT, 2 * DI, NH)
        dtr = self.ar("dtraw", [NH, 512])

        def evac(tg, ps):
            sl = self.dt_fm[:, tg * 512:(tg + 1) * 512]
            self.ts(dtr[:], ps, self.dtb_col[0:NH, 0:1], 0.0,
                    op0=ALU.add, op1=ALU.min)
            self.act(dtr[:], dtr[:], AF.Exp)
            self.act(dtr[:], dtr[:], AF.Ln, bias=1.0)
            self.ts(sl, dtr[:], 1e-4, 0.1, op0=ALU.max, op1=ALU.min)

        self.proj_mtile(wdt, NH, 6, evac)

    def phase_bc_norms(self):
        mk = self.mk
        wb = [self.ar(f"wbc{i}", [128, 8 * 128]) for i in range(2)]
        bcb = [self.ar("bcb0", [128, 4 * L])]
        sqf = self.ar("sqf", [128, L])
        sqb = self.ar("sqb", [128, 512])
        nstage = self.ar("nstage", [1, 512])
        BC0 = 2 * DI + NH
        for h in range(NH):
            wbuf = wb[h % 2]
            bcbuf = bcb[0]
            for mt in range(4):
                self.load_wslab(wbuf, self.WinT, BC0 + h * 512 + mt * 128, 128)
                ps = self.psb[mt % 2]
                nps = self.psb[2]
                for tg in range(4):
                    for kc in range(8):
                        mk.t.matmul(ps[0:128, 0:512],
                                    wbuf[:, kc * 128:(kc + 1) * 128],
                                    self.xn[:, kc * L + tg * 512: kc * L + (tg + 1) * 512],
                                    start=(kc == 0), stop=(kc == 7))
                    pa = ps[0:128, 0:512]
                    mk.v.tensor_copy(bcbuf[:, mt * L + tg * 512: mt * L + (tg + 1) * 512], pa)
                    if mt % 2 == 0:
                        self.act(sqf[:, tg * 512:(tg + 1) * 512], pa, AF.Square)
                    else:
                        self.act(sqb[:], pa, AF.Square)
                        mk.t.matmul(nps[0:1, 0:512], self.ones_col[:],
                                    sqf[:, tg * 512:(tg + 1) * 512],
                                    start=True, stop=False)
                        mk.t.matmul(nps[0:1, 0:512], self.ones_col[:], sqb[:],
                                    start=False, stop=True)
                        j = mt // 2
                        mk.s.copy(nstage[0:1, 0:512], nps[0:1, 0:512])
                        mk.dma("g", self.rn_sq[2 * h + j:2 * h + j + 1,
                                               tg * 512:(tg + 1) * 512],
                               nstage[0:1, 0:512])
            mk.dma("s", self.sc_bc.ap()[h * 512:(h + 1) * 512, :]
                   .rearrange("(m p) t -> p m t", p=128),
                   bass.AP(bcbuf, 0, [[4 * L, 128], [L, 4], [1, L]]))

    def phase_norm_finish(self):
        mk = self.mk
        self.act(self.rn_sq[:], self.rn_sq[:], AF.Ln, bias=1e-8)
        self.act(self.rn_sq[:], self.rn_sq[:], AF.Exp, scale=-0.5)
        for ti in range(16):
            self.trans128(self.rn_tm[:, ti * 64:(ti + 1) * 64],
                          self.rn_sq[:, ti * 128:(ti + 1) * 128], bank=6,
                          rows=64, cols=128)

    def phase_z_xc_u(self):
        mk = self.mk
        wz = self.ar("wz", [128, 8 * 128])
        wx = self.ar("wx", [128, 8 * 128])
        zb = self.ar("zb", [128, L])
        xcb = self.ar("xcb", [128, 4 + L])
        ub = self.ar("ub", [128, L])
        utb = self.ar("utb", [128, 2 * 16 * 64])
        mk.g.memset(xcb[:, 0:3], 0.0)
        for c in range(16):
            self.load_wslab(wz, self.WinT, c * 128, 128)
            self.load_wslab(wx, self.WinT, DI + c * 128, 128)

            def evz(tg, pa):
                self.act(zb[:, tg * 512:(tg + 1) * 512], pa, AF.Silu)

            def evx(tg, pa):
                mk.v.tensor_copy(xcb[:, 3 + tg * 512: 3 + (tg + 1) * 512], pa)

            self.proj_mtile(wz, 128, 3, evz)
            self.proj_mtile(wx, 128, 4, evx)
            mk.dma("s", self.sc_z.ap()[c * 128:(c + 1) * 128, :], zb[:])
            w4 = self.dwW[:, c * 4:(c + 1) * 4]
            self.ts(ub[:], xcb[:, 0:L], w4[:, 0:1])
            for r in range(1, 4):
                self.stt(ub[:], xcb[:, r:r + L], w4[:, r:r + 1], ub[:])
            self.act(ub[:], ub[:], AF.Silu, bias=self.dwB[:, c:c + 1])
            mk.dma("s", self.sc_u.ap()[c * 128:(c + 1) * 128, :], ub[:])
            for hi in range(2):
                for ci in range(16):
                    self.trans128(utb[:, hi * 1024 + ci * 64: hi * 1024 + (ci + 1) * 64],
                                  ub[hi * 64:(hi + 1) * 64, ci * 128:(ci + 1) * 128],
                                  bank=5, rows=64, cols=128, bp=hi * 64)
                h = 2 * c + hi
                mk.dma("s", self.sc_ut.ap()[h * L:(h + 1) * L, :]
                       .rearrange("(c p) d -> p c d", p=128),
                       bass.AP(utb, hi * 1024, [[2048, 128], [64, 16], [1, 64]]))

    def phase_ssm_core(self):
        mk = self.mk
        ar = self.ar
        bch = ar("bch", [128, L])
        dt_bc = ar("dt_bc", [128, L])
        rd = ar("rd", [128, L])
        ab = ar("abb", [128, L])
        bb = ar("bbb", [128, L])
        idet = ar("idet", [128, L])
        Kf = [ar(f"Kf{j}", [128, L]) for j in range(4)]
        Qf = [ar(f"Qf{j}", [128, L]) for j in range(4)]
        w1 = ar("w1", [128, L]); w2 = ar("w2", [128, L]); w3 = ar("w3", [128, L])
        ibuf = ar("ibuf", [128, L // 4], I32)
        ebuf = ar("ebuf", [128, 16 * 16])
        dtrow = ar("dtrow", [1, L])
        rcrow = ar("rcrow", [1, L])
        ygb = ar("ygb", [64, L])
        rCb = ar("rCb", [64, L])
        utc = ar("utc", [128, PD])
        gsb = ar("gsb", [128, 128])
        kysb = ar("kysb", [128, 128])
        ktmp = ar("ktmp", [128, 128])
        Yst = [ar("Yst0", [128, PD]), ar("Yst1", [128, PD])]
        Vst = [ar("Vst0", [128, PD]), ar("Vst1", [128, PD])]
        sYt = ar("sYt", [128, PD])

        def eslot(i):
            return ebuf[:, i * 16:(i + 1) * 16]

        def ecol(i, c):
            return ebuf[:, i * 16 + c:i * 16 + c + 1]

        PG, PO, PKY, PKV, PYA, PVA, PRP = 0, 1, 2, 3, 4, 5, 6

        for h in range(NH):
            hc = lambda name: self.hpc(name, h)
            mk.dma("g", dtrow[0:1, :], self.dt_fm[h:h + 1, :])
            mk.dma("g", rcrow[0:1, :], self.rn_sq[2 * h + 1:2 * h + 2, :])
            prp = self.psb[PRP]
            for tg in range(4):
                mk.t.matmul(prp[0:128, 0:512], self.ones_row1[0:1, 0:128],
                            dtrow[0:1, tg * 512:(tg + 1) * 512],
                            start=True, stop=True)
                mk.s.copy(dt_bc[:, tg * 512:(tg + 1) * 512], prp[0:128, 0:512])
            for tg in range(4):
                mk.t.matmul(prp[0:64, 0:512], self.ones_row1[0:1, 0:64],
                            rcrow[0:1, tg * 512:(tg + 1) * 512],
                            start=True, stop=True)
                mk.s.copy(rCb[0:64, tg * 512:(tg + 1) * 512], prp[0:64, 0:512])
            mk.dma("y", bch[:], self.sc_bc.ap()[h * 512 + 128:h * 512 + 256, :])

            self.ts(w1[:], dt_bc[:], hc("k2"), 1.0)
            self.ts(w2[:], dt_bc[:], hc("rs"))
            self.ts(w3[:], dt_bc[:], hc("A"), hc("kf"))
            self.tt(w3[:], w3[:], dt_bc[:])
            self.act(w3[:], w3[:], AF.Ln, bias=1.0)
            self.act(rd[:], w3[:], AF.Exp, scale=-1.0)
            self.tt(bb[:], w1[:], w2[:], ALU.add)
            self.act(bb[:], bb[:], AF.Ln)
            self.ts(idet[:], w3[:], hc("mos_h"))
            self.stt(idet[:], bb[:], hc("mhy"), idet[:])
            self.tt(ab[:], w1[:], w2[:], ALU.subtract)
            self.act(ab[:], ab[:], AF.Ln)
            mk.v.reciprocal(w1[:], w1[:])
            self.tt(w2[:], w2[:], w1[:])
            self.act(w2[:], w2[:], AF.Arctan)
            self.ts(w2[:], w2[:], hc("mos"))
            self.stt(ab[:], ab[:], hc("mhy"), w2[:])
            mk.v.tensor_tensor_scan(w1[:], self.stripe01[:], idet[:], 0.0,
                                    ALU.mult, ALU.add)
            mk.v.tensor_tensor_scan(w2[:], self.stripe01[:], ab[:], 0.0,
                                    ALU.mult, ALU.add)
            mk.v.tensor_tensor_scan(idet[:], self.stripe01[:], w3[:], 0.0,
                                    ALU.mult, ALU.add)
            self.act(idet[:], idet[:], AF.Exp, scale=-1.0)
            self.act(w1[:], w1[:], AF.Exp)
            self.act(w3[:], w2[:], AF.Exp)
            self.ts(bb[:], w2[:], 1.0 / (2 * PI))
            for hh in range(4):
                hsl = slice(hh * (L // 4), (hh + 1) * (L // 4))
                mk.v.tensor_copy(ibuf[:], bb[:, hsl])
                mk.v.tensor_copy(bb[:, hsl], ibuf[:])
            self.stt(bb[:], bb[:], -2 * PI, w2[:])
            self.ts(Kf[1][:], bb[:], PI32, op0=ALU.is_gt)
            self.stt(bb[:], Kf[1][:], -2 * PI, bb[:])
            self.ts(bb[:], bb[:], -PI32, PI32, op0=ALU.max, op1=ALU.min)
            self.act(bb[:], bb[:], AF.Sin)
            self.ts(Kf[0][:], w2[:], 1.0 / (2 * PI), 0.25)
            for hh in range(4):
                hsl = slice(hh * (L // 4), (hh + 1) * (L // 4))
                mk.v.tensor_copy(ibuf[:], Kf[0][:, hsl])
                mk.v.tensor_copy(Kf[0][:, hsl], ibuf[:])
            self.stt(Kf[0][:], Kf[0][:], -2 * PI, w2[:])
            self.ts(Kf[1][:], Kf[0][:], PI32 / 2, op0=ALU.is_gt)
            self.stt(Kf[0][:], Kf[1][:], -2 * PI, Kf[0][:])
            self.ts(Kf[0][:], Kf[0][:], -PI32 - PI / 2, PI32 - PI / 2,
                    op0=ALU.max, op1=ALU.min)
            self.act(Kf[0][:], Kf[0][:], AF.Sin, bias=PI / 2)
            self.tt(w2[:], w1[:], Kf[0][:])
            self.tt(Kf[1][:], w1[:], w3[:], ALU.add)
            self.ts(w2[:], w2[:], hc("mos"))
            self.stt(ab[:], Kf[1][:], hc("mhy_h"), w2[:])
            self.tt(Kf[1][:], w1[:], w3[:], ALU.subtract)
            self.tt(w1[:], w1[:], bb[:])
            self.ts(w1[:], w1[:], hc("irs_o"))
            self.stt(bb[:], Kf[1][:], hc("irs_h2"), w1[:])

            self.tt(w1[:], dt_bc[:], rd[:])
            self.tt(w2[:], w1[:], bch[:])
            self.tt(w1[:], dt_bc[:], w2[:])
            mk.dma("y", bch[:], self.sc_bc.ap()[h * 512:h * 512 + 128, :])
            self.tt(w3[:], dt_bc[:], bch[:])
            self.tt(w3[:], w3[:], w1[:], ALU.add)
            self.tt(Kf[0][:], ab[:], w3[:])
            self.tt(Kf[1][:], ab[:], w2[:])
            self.ts(w1[:], bb[:], -1.0)
            self.tt(Kf[2][:], w1[:], w3[:])
            self.tt(Kf[3][:], w1[:], w2[:])
            mk.dma("y", bch[:], self.sc_bc.ap()[h * 512 + 384:h * 512 + 512, :])
            self.ts(w1[:], bch[:], hc("negA"))
            self.ts(w2[:], bch[:], hc("k2n"))
            self.tt(w3[:], ab[:], bch[:])
            self.ts(Qf[3][:], bb[:], hc("sig"))
            self.tt(Qf[3][:], Qf[3][:], bch[:])
            mk.dma("y", bch[:], self.sc_bc.ap()[h * 512 + 256:h * 512 + 384, :])
            self.stt(w1[:], bch[:], hc("k2"), w1[:])
            self.tt(w2[:], bch[:], w2[:], ALU.add)
            self.tt(Qf[1][:], bb[:], w2[:])
            self.tt(Qf[1][:], w3[:], Qf[1][:], ALU.add)
            self.tt(Qf[1][:], Qf[1][:], idet[:])
            self.tt(w3[:], ab[:], bch[:])
            self.tt(Qf[0][:], bb[:], w1[:])
            self.tt(Qf[0][:], w3[:], Qf[0][:], ALU.add)
            self.tt(Qf[0][:], Qf[0][:], idet[:])
            self.ts(w3[:], bb[:], hc("sig"))
            self.tt(Qf[2][:], w3[:], bch[:])
            self.tt(w1[:], ab[:], w1[:])
            self.tt(Qf[2][:], Qf[2][:], w1[:], ALU.add)
            self.tt(Qf[2][:], Qf[2][:], idet[:])
            self.tt(w2[:], ab[:], w2[:])
            self.tt(Qf[3][:], Qf[3][:], w2[:], ALU.add)
            self.tt(Qf[3][:], Qf[3][:], idet[:])

            mk.v.tensor_copy(eslot(0), bass.AP(ab, TC - 1, [[L, 128], [TC, 16]]))
            mk.v.tensor_copy(eslot(1), bass.AP(bb, TC - 1, [[L, 128], [TC, 16]]))
            mk.v.tensor_copy(eslot(2), bass.AP(idet, TC - 1, [[L, 128], [TC, 16]]))
            self.tt(eslot(3), eslot(0), eslot(2))
            self.tt(eslot(4), eslot(1), eslot(2))
            self.stt(eslot(5), eslot(4), hc("k2"), eslot(3))
            self.ts(eslot(6), eslot(4), hc("sig"))
            self.stt(eslot(7), eslot(3), hc("k2"), eslot(6))
            self.ts(eslot(8), eslot(4), hc("negA"))
            self.stt(eslot(9), eslot(4), hc("k2n"), eslot(3))
            self.ts(eslot(10), eslot(3), hc("negA"))
            self.stt(eslot(11), eslot(3), hc("k2n"), eslot(6))
            self.stt(eslot(12), eslot(1), hc("k2"), eslot(0))
            self.tt(eslot(12), eslot(12), eslot(2))
            self.ts(eslot(13), eslot(4), hc("negA"))
            self.stt(eslot(14), eslot(1), hc("k2n"), eslot(0))
            self.tt(eslot(14), eslot(14), eslot(2))
            DQ = {1: 5, 2: 4, 3: 7, 4: 3}
            FQ = {1: 8, 2: 9, 3: 10, 4: 11}

            Y, V = Yst[0], Vst[0]
            Y2, V2 = Yst[1], Vst[1]
            for c in range(NCH):
                sl = slice(c * TC, (c + 1) * TC)
                mk.dma("y", utc[:], self.sc_ut.ap()[h * L + c * TC:
                                                    h * L + (c + 1) * TC, :])
                self.ts(utc[:], utc[:],
                        self.rn_tm[:, c * 64 + 2 * h:c * 64 + 2 * h + 1])
                pg = self.psb[PG]
                for j in range(4):
                    mk.t.matmul(pg[0:128, 0:128], Kf[j][:, sl], Qf[j][:, sl],
                                start=(j == 0), stop=(j == 3))
                self.tt(gsb[:], pg[0:128, 0:128], self.tri_incl[:])
                po = self.psb[PO]
                oc = (c % 4) * 128
                mk.t.matmul(po[0:64, oc:oc + 128], utc[:], gsb[:],
                            start=True, stop=(c == 0))
                if c > 0:
                    mk.t.matmul(po[0:64, oc:oc + 128], Y[:, 0:64], Qf[0][:, sl],
                                start=False, stop=False)
                    mk.t.matmul(po[0:64, oc:oc + 128], V[:, 0:64], Qf[1][:, sl],
                                start=False, stop=True)
                pky, pkv = self.psb[PKY], self.psb[PKV]
                for j in (1, 2, 3, 4):
                    self.ts(ktmp[:], Kf[j - 1][:, sl], ecol(DQ[j], c))
                    mk.t.matmul(pky[0:128, 0:128], ktmp[:], self.ident[:],
                                is_transpose=True, start=(j == 1), stop=(j == 4))
                mk.s.copy(kysb[:], pky[0:128, 0:128])
                pya, pva = self.psb[PYA], self.psb[PVA]
                mk.t.matmul(pya[0:128, 0:64], kysb[:], utc[:],
                            start=True, stop=True)
                for j in (1, 2, 3, 4):
                    self.ts(ktmp[:], Kf[j - 1][:, sl], ecol(FQ[j], c))
                    mk.t.matmul(pkv[0:128, 0:128], ktmp[:], self.ident[:],
                                is_transpose=True, start=(j == 1), stop=(j == 4))
                mk.s.copy(kysb[:], pkv[0:128, 0:128])
                mk.t.matmul(pva[0:128, 0:64], kysb[:], utc[:],
                            start=True, stop=True)
                if c == 0:
                    mk.s.copy(Y2[:], pya[0:128, 0:64])
                    mk.s.copy(V2[:], pva[0:128, 0:64])
                else:
                    self.stt(sYt[:], Y[:], ecol(12, c), pya[0:128, 0:64])
                    self.stt(Y2[:], V[:], ecol(4, c), sYt[:])
                    self.stt(sYt[:], Y[:], ecol(13, c), pva[0:128, 0:64])
                    self.stt(V2[:], V[:], ecol(14, c), sYt[:])
                Y, Y2 = Y2, Y
                V, V2 = V2, V
                if c % 4 == 3:
                    g0 = (c - 3) * TC
                    self.tt(ygb[0:64, g0:g0 + 512], po[0:64, 0:512],
                            rCb[0:64, g0:g0 + 512])
            mk.dma("s", self.sc_yg.ap()[h * 64:(h + 1) * 64, :], ygb[0:64, :])

    def phase_wout(self):
        mk = self.mk
        ygs = [self.ar(f"ygs{i}", [128, 512]) for i in range(16)]
        uzt = self.ar("uzt", [128, 512])
        wsl = [self.ar(f"wsl{i}", [128, 16 * 128]) for i in range(2)]
        sout = self.ar("sout", [128, 512])
        for tg in range(4):
            for kc in range(16):
                mk.dma("y", ygs[kc][:],
                       self.sc_yg.ap()[kc * 128:(kc + 1) * 128,
                                       tg * 512:(tg + 1) * 512])
                mk.dma("y", uzt[:], self.sc_u.ap()[kc * 128:(kc + 1) * 128,
                                                   tg * 512:(tg + 1) * 512])
                self.stt(ygs[kc][:], uzt[:], self.skip_c[:, kc:kc + 1],
                         ygs[kc][:])
                mk.dma("y", uzt[:], self.sc_z.ap()[kc * 128:(kc + 1) * 128,
                                                   tg * 512:(tg + 1) * 512])
                self.tt(ygs[kc][:], ygs[kc][:], uzt[:])
            for m in range(8):
                wb = wsl[m % 2]
                if tg == 0 or True:
                    for kc in range(16):
                        mk.dma("y", wb[:, kc * 128:(kc + 1) * 128],
                               self.WoutT.ap()[kc * 128:(kc + 1) * 128,
                                               m * 128:(m + 1) * 128])
                ps = self.psb[m % 2]
                for kc in range(16):
                    mk.t.matmul(ps[0:128, 0:512], wb[:, kc * 128:(kc + 1) * 128],
                                ygs[kc][:], start=(kc == 0), stop=(kc == 15))
                mk.v.tensor_copy(sout[:], ps[0:128, 0:512])
                mk.dma("s", self.sc_s.ap()[m * 128:(m + 1) * 128,
                                           tg * 512:(tg + 1) * 512], sout[:])

    def phase_final(self):
        mk = self.mk
        xt = self.ar("fxt", [128, D])
        st = self.ar("fst", [128, D])
        gt = self.ar("fgt", [128, D])
        sfm = self.ar("sfm", [128, D])
        gfm = self.ar("gfm", [128, D])
        for ti in range(16):
            mk.dma("y", xt[:], self.x_in.ap()[ti * 128:(ti + 1) * 128, :])
            for dc in range(8):
                mk.dma("y", sfm[:, dc * 128:(dc + 1) * 128],
                       self.sc_s.ap()[dc * 128:(dc + 1) * 128,
                                      ti * 128:(ti + 1) * 128])
                mk.dma("y", gfm[:, dc * 128:(dc + 1) * 128],
                       self.sc_g.ap()[dc * 128:(dc + 1) * 128,
                                      ti * 128:(ti + 1) * 128])
            for dc in range(8):
                self.trans128(st[:, dc * 128:(dc + 1) * 128],
                              sfm[:, dc * 128:(dc + 1) * 128], bank=dc % 2,
                              rows=128, cols=128)
                self.trans128(gt[:, dc * 128:(dc + 1) * 128],
                              gfm[:, dc * 128:(dc + 1) * 128], bank=2 + dc % 2,
                              rows=128, cols=128, eng="s")
            self.act(gt[:], gt[:], AF.Silu)
            self.tt(st[:], st[:], gt[:])
            self.tt(st[:], st[:], xt[:], ALU.add)
            mk.dma("s", self.out.ap()[ti * 128:(ti + 1) * 128, :], st[:])

    def ffn_load_cols(self, dst, dram_row_ap, n, off=0):
        tmp = self.ar(f"flc_{self.uid()}", [max(n, 1), 128])
        self.mk.dma("y", tmp[0:n, :],
                    dram_row_ap.rearrange("o (c p) -> (o c) p", p=128))
        ps = self.psb[7]
        self.mk.t.transpose(ps[0:128, 0:n], tmp[0:n, :], self.ident[0:n, 0:n])
        self.mk.v.tensor_copy(dst[:, off:off + n], ps[0:128, 0:n])

    def phase_ffn_prep(self):
        mk = self.mk
        g = mk.g
        ar = self.ar
        self.t_row = ar("t_row", [1, 512])
        self.pw_row = ar("pw_row", [1, 512])
        self.if_row = ar("if_row", [1, 128])
        self.lb_row = ar("lb_row", [1, 128])
        dr = ar("dr", [1, 128])
        g.iota(dr[:], [[1, 128]], channel_multiplier=0,
               allow_small_or_imprecise_dtypes=True)
        tmp = ar("drt", [1, 128])
        ib2 = ar("ib2", [1, 128], I32)
        self.ts(tmp[:], dr[:], 1.0 / 32.0)
        mk.v.tensor_copy(ib2[:], tmp[:])
        mk.v.tensor_copy(tmp[:], ib2[:])
        self.stt(dr[:], tmp[:], -32.0, dr[:])
        self.ts(tmp[:], dr[:], 0.0, op0=ALU.is_lt)
        self.stt(dr[:], tmp[:], 32.0, dr[:])
        self.ts(self.if_row[:], dr[:], -2.0 / 64.0 * float(np.log(10000.0)))
        self.act(self.if_row[:], self.if_row[:], AF.Exp)
        self.ts(self.lb_row[:], dr[:], 2.0 / 89.6, 25.6 / 89.6)
        self.act(self.lb_row[:], self.lb_row[:], AF.Ln)
        self.emask = ar("emask", [128, 1408])
        g.memset(self.emask[:], 0.0)
        g.affine_select(self.emask[:], self.emask[:], [[1, 1408]], ALU.is_ge,
                        NEG_BIG, base=-384, channel_multiplier=-1)
        g.affine_select(self.emask[:], self.emask[:], [[-1, 1408]], ALU.is_ge,
                        NEG_BIG, base=896, channel_multiplier=1)
        self.esel = ar("esel", [8, 4 * 128])
        g.memset(self.esel[:], 0.0)
        for dc in range(4):
            sl = self.esel[:, dc * 128:(dc + 1) * 128]
            g.memset(sl, 1.0)
            g.affine_select(sl, sl, [[1, 128]], ALU.is_ge, 0.0,
                            base=128 * dc, channel_multiplier=-64)
            g.affine_select(sl, sl, [[-1, 128]], ALU.is_ge, 0.0,
                            base=63 - 128 * dc, channel_multiplier=64)
        self.sgnc = ar("sgnc", [128, 1])
        g.memset(self.sgnc[:], 1.0)
        g.memset(self.sgnc[0:32, 0:1], -1.0)
        g.memset(self.sgnc[64:96, 0:1], -1.0)

    def phase_ffn(self, cut=None):
        mk = self.mk
        xf = self.ar("xf", [128, 8 * L])
        for dc in range(8):
            mk.dma("y", xf[:, dc * L:(dc + 1) * L],
                   self.sc_xn.ap()[dc * 128:(dc + 1) * 128, :])
        self.phase_ffn_prep()
        ffn_mark = self.arena_mark()
        done = False
        for l in range(2):
            self.ffn_attn(l, xf)
            self.arena_reset(ffn_mark)
            if cut == f"attn{l}":
                done = True
                break
            self.ffn_mlp(l, xf)
            self.arena_reset(ffn_mark)
            if cut == f"mlp{l}":
                done = True
                break
        for dc in range(8):
            mk.dma("s", self.sc_g.ap()[dc * 128:(dc + 1) * 128, :],
                   xf[:, dc * L:(dc + 1) * L])

    def ffn_attn(self, l, xf):
        mk = self.mk
        ar = self.ar
        qs_c = ar("qs_c", [128, 2])
        qs_s = ar("qs_s", [128, 2])
        t2 = ar("t2r", [2, 64])
        mk.dma("y", t2[0:1, :], self.qscale.ap()[l])
        mk.dma("y", t2[1:2, :], self.kscale.ap()[l])
        ps7 = self.psb[7]
        mk.t.transpose(ps7[0:64, 0:2], t2[0:2, 0:64], self.ident[0:2, 0:2])
        mk.v.tensor_copy(qs_c[0:64, 0:2], ps7[0:64, 0:2])
        mk.dma("g", qs_c[64:128, 0:2], qs_c[0:64, 0:2])
        for base in (0, 64):
            mk.dma("g", qs_s[base:base + 32, 0:2], qs_c[base + 32:base + 64, 0:2])
            mk.dma("g", qs_s[base + 32:base + 64, 0:2], qs_c[base:base + 32, 0:2])
        qw = ar("qw", [128, 4 * 512])
        kw = [ar("kw0", [128, 4 * 512]), ar("kw1", [128, 4 * 512])]
        qsh = ar("qsh", [128, 512])
        vtm = ar("vtm", [128, 8 * 520])
        tabs = ar("tabs", [128, 4 * 512])
        ssb = ar("ssb", [128, 8 * 512])
        sq = ar("sqf2", [128, 512])
        nrq = ar("nrq", [8, 512])
        nrk = ar("nrk", [8, 512])
        nst = ar("nst", [1, 512])
        m1 = ar("m1w", [128, 512])
        mxc = ar("mxc", [128, 4])
        mxe = ar("mxe", [4, 128])
        mxr = ar("mxr", [1, 512])
        dnr = ar("dnr", [8, 512])
        obuf = ar("obuf", [128, 4 * 512])
        wsl = ar("wslf", [128, 8 * 128])
        vsl = ar("vslf", [128, 2 * 512])
        ib3 = ar("ib3", [128, 512], I32)
        wf = m1
        bstash = ar("bstash", [128, 8 * 3])
        x0s = obuf
        PA, PB, PC, PS0, PS1, PV, PR = 0, 1, 2, 3, 4, 5, 6
        for ti in range(8):
            for hh in range(FH):
                mk.g.memset(vtm[:, ti * 520 + hh * 65 + 64:
                                ti * 520 + hh * 65 + 65], 1.0)

        W0 = self.wqkv.ap()[l]
        for w in range(4):
            t0 = w * FW
            kcur = kw[w % 2]
            kprev = kw[(w + 1) % 2]
            for i, dc in enumerate(range(4, 8)):
                if w == 0:
                    mk.g.memset(x0s[:, i * 512:i * 512 + 1], 0.0)
                else:
                    mk.v.tensor_copy(x0s[:, i * 512:i * 512 + 1],
                                     bstash[:, dc * 3 + (w - 1):dc * 3 + w])
                mk.v.tensor_copy(x0s[:, i * 512 + 1:(i + 1) * 512],
                                 xf[:, dc * L + t0: dc * L + t0 + 511])

            def rhs_x(dc, fr0, frw):
                if dc < 4:
                    return xf[:, dc * L + t0 + fr0: dc * L + t0 + fr0 + frw]
                return x0s[:, (dc - 4) * 512 + fr0:(dc - 4) * 512 + fr0 + frw]

            mk.g.iota(self.t_row[:], [[1, 512]], base=t0, channel_multiplier=0,
                      allow_small_or_imprecise_dtypes=True)
            self.ts(self.pw_row[:], self.t_row[:], 1.0 / 256.0, -1024.0 / 256.0)
            pt = self.psb[PA]
            mk.t.matmul(pt[0:128, 0:512], self.if_row[:],
                        self.t_row[0:1, 0:512], start=True, stop=True)
            for which, bias in (("sin", 0.0), ("cos", 0.25)):
                self.ts(wf[:], pt[0:128, 0:512], 1.0 / (2 * PI), bias)
                mk.v.tensor_copy(ib3[:], wf[:])
                mk.v.tensor_copy(wf[:], ib3[:])
                self.stt(wf[:], wf[:], -2 * PI, pt[0:128, 0:512])
                thr = PI32 if which == "sin" else PI32 / 2
                self.ts(sq[:], wf[:], thr, op0=ALU.is_gt)
                self.stt(wf[:], sq[:], -2 * PI, wf[:])
                if which == "sin":
                    self.ts(wf[:], wf[:], -PI32, PI32, op0=ALU.max, op1=ALU.min)
                    self.act(qsh[:], wf[:], AF.Sin)
                else:
                    self.ts(wf[:], wf[:], -PI32 - PI / 2, PI32 - PI / 2,
                            op0=ALU.max, op1=ALU.min)
                    self.act(tabs[:, 3 * 512:4 * 512], wf[:], AF.Sin,
                             bias=PI / 2)
            mk.t.matmul(pt[0:128, 0:512], self.lb_row[:],
                        self.pw_row[0:1, 0:512], start=True, stop=True)
            self.act(wf[:], pt[0:128, 0:512], AF.Exp)
            self.tt(tabs[:, 0:512], tabs[:, 3 * 512:4 * 512], wf[:])
            self.tt(tabs[:, 512:1024], qsh[:], wf[:])
            self.act(wf[:], pt[0:128, 0:512], AF.Exp, scale=-1.0)
            self.tt(tabs[:, 1024:1536], tabs[:, 3 * 512:4 * 512], wf[:])
            self.tt(tabs[:, 3 * 512:4 * 512], qsh[:], wf[:])
            self.ts(tabs[:, 0:512], tabs[:, 0:512], qs_c[:, 0:1])
            self.ts(tabs[:, 0:512], tabs[:, 0:512], 8.0)
            self.tt(tabs[:, 512:1024], tabs[:, 512:1024],
                    self.bcast_free(self.sgnc[:, 0:1], 512))
            self.ts(tabs[:, 512:1024], tabs[:, 512:1024], qs_s[:, 0:1])
            self.ts(tabs[:, 512:1024], tabs[:, 512:1024], 8.0)
            self.ts(tabs[:, 1024:1536], tabs[:, 1024:1536], qs_c[:, 1:2])
            self.tt(tabs[:, 3 * 512:4 * 512], tabs[:, 3 * 512:4 * 512],
                    self.bcast_free(self.sgnc[:, 0:1], 512))
            self.ts(tabs[:, 3 * 512:4 * 512], tabs[:, 3 * 512:4 * 512],
                    qs_s[:, 1:2])

            for side, dst in (("q", qw), ("k", kcur)):
                for mc in range(4):
                    col0 = (0 if side == "q" else FI) + mc * 128
                    for kc in range(8):
                        mk.dma("y", wsl[:, kc * 128:(kc + 1) * 128],
                               W0[kc * 128:(kc + 1) * 128, col0:col0 + 128])
                    ps = self.psb[PB]
                    for kc in range(8):
                        mk.t.matmul(ps[0:128, 0:512],
                                    wsl[:, kc * 128:(kc + 1) * 128],
                                    rhs_x(kc, 0, 512), start=(kc == 0),
                                    stop=(kc == 7))
                    pa = ps[0:128, 0:512]
                    mk.v.tensor_copy(dst[:, mc * 512:(mc + 1) * 512], pa)
                    self.act(sq[:], pa, AF.Square)
                    nps = self.psb[PC]
                    nrt = nrq if side == "q" else nrk
                    for hh in range(2):
                        b = hh * 64
                        mk.t.matmul(nps[0:1, 0:512],
                                    self.ones_col[b:b + 64, 0:1],
                                    sq[b:b + 64, :], start=True, stop=True)
                        mk.s.copy(nst[0:1, 0:512], nps[0:1, 0:512])
                        mk.dma("g", nrt[2 * mc + hh:2 * mc + hh + 1, :],
                               nst[0:1, 0:512])
            vpb = [PA, PS0, PS1, PV]
            for kc in range(8):
                vs = vsl[(kc % 2) * 512:(kc % 2) * 512 + 512] if False else \
                    vsl[:, (kc % 2) * 512:(kc % 2) * 512 + 512]
                mk.dma("y", vs, W0[kc * 128:(kc + 1) * 128, 2 * FI:3 * FI])
                for i in range(4):
                    mk.t.matmul(self.psb[vpb[i]][0:128, 0:512],
                                rhs_x(kc, i * 128, 128), vs,
                                start=(kc == 0), stop=(kc == 7))
            for i in range(4):
                ti = (4 * w + i) % 8
                mk.v.tensor_copy(
                    bass.AP(vtm, ti * 520, [[8 * 520, 128], [65, 8], [1, 64]]),
                    self.psb[vpb[i]][0:128, 0:512])
            for nrt in (nrq, nrk):
                self.ts(nrt[:], nrt[:], 1e-24, op0=ALU.max)
                self.act(nrt[:], nrt[:], AF.Ln)
                self.act(nrt[:], nrt[:], AF.Exp, scale=-0.5)
            for side, dst in (("q", qw), ("k", kcur)):
                nrt = nrq if side == "q" else nrk
                tb = 0 if side == "q" else 1024
                for mc in range(4):
                    csl = slice(mc * 512, (mc + 1) * 512)
                    pr = self.psb[PR]
                    mk.t.matmul(pr[0:128, 0:512],
                                self.esel[:, mc * 128:(mc + 1) * 128],
                                nrt[0:8, :], start=True, stop=True)
                    self.tt(dst[:, csl], dst[:, csl], pr[0:128, 0:512])
                    for b in (0, 64):
                        mk.dma("g", qsh[b:b + 32, :], dst[b + 32:b + 64, csl])
                        mk.dma("g", qsh[b + 32:b + 64, :], dst[b:b + 32, csl])
                    self.tt(qsh[:], qsh[:], tabs[:, tb + 512:tb + 1024])
                    self.tt(dst[:, csl], dst[:, csl], tabs[:, tb:tb + 512])
                    self.tt(dst[:, csl], dst[:, csl], qsh[:], ALU.add)

            for h in range(FH):
                hb = (h % 2) * 64
                mc = h // 2
                csl = slice(mc * 512, (mc + 1) * 512)
                rlist = list(range(4, 8)) if w == 0 else list(range(8))
                qs_ap = qw[hb:hb + 64, csl]
                for idx, r in enumerate(rlist):
                    kbuf = kprev if r < 4 else kcur
                    co = mc * 512 + (r % 4) * 128
                    pss = self.psb[[PS0, PS1][idx % 2]]
                    mk.t.matmul(pss[0:128, 0:512], kbuf[hb:hb + 64, co:co + 128],
                                qs_ap, start=True, stop=True)
                    self.tt(ssb[:, r * 512:(r + 1) * 512], pss[0:128, 0:512],
                            self.emask[:, 896 - 128 * r:1408 - 128 * r],
                            ALU.add)
                nr_ = len(rlist)
                r0_ = rlist[0]
                view = bass.AP(ssb, r0_ * 512,
                               [[8 * 512, 128], [1, 512], [512, nr_]])
                mk.v.reduce_max(m1[:, 0:512], view, axis=AX.X)
                for qsub in range(4):
                    ps = self.psb[PR]
                    mk.t.transpose(ps[0:128, 0:128],
                                   m1[:, qsub * 128:(qsub + 1) * 128],
                                   self.ident[:])
                    mk.v.reduce_max(mxc[:, qsub:qsub + 1], ps[0:128, 0:128],
                                    axis=AX.X)
                ps = self.psb[PR]
                mk.t.transpose(ps[0:4, 0:128], mxc[:, 0:4], self.ident[:])
                mk.v.tensor_copy(mxe[0:4, 0:128], ps[0:4, 0:128])
                mk.dma("g", bass.AP(mxr, 0, [[512, 1], [128, 4], [1, 128]]),
                       bass.AP(mxe, 0, [[128, 4], [1, 128]]))
                mk.t.matmul(ps[0:128, 0:512], self.ones_row1[0:1, 0:128],
                            mxr[0:1, 0:512], start=True, stop=True)
                for r in rlist:
                    self.tt(ssb[:, r * 512:(r + 1) * 512],
                            ssb[:, r * 512:(r + 1) * 512], ps[0:128, 0:512],
                            ALU.subtract)
                self.act(bass.AP(ssb, r0_ * 512, [[8 * 512, 128], [1, nr_ * 512]]),
                         bass.AP(ssb, r0_ * 512, [[8 * 512, 128], [1, nr_ * 512]]),
                         AF.Exp)
                pv = self.psb[PV]
                for idx, r in enumerate(rlist):
                    ti = (4 * (w - 1) + r) % 8
                    mk.t.matmul(pv[0:65, 0:512],
                                bass.AP(vtm, ti * 520 + h * 65,
                                        [[8 * 520, 128], [1, 65]]),
                                ssb[:, r * 512:(r + 1) * 512],
                                start=(idx == 0), stop=(idx == nr_ - 1))
                if hb == 0:
                    mk.v.tensor_copy(obuf[0:64, csl], pv[0:64, 0:512])
                else:
                    mk.v.tensor_copy(sq[0:64, 0:512], pv[0:64, 0:512])
                    mk.dma("g", obuf[64:128, csl], sq[0:64, 0:512])
                mk.s.copy(m1[64:65, 0:512], pv[64:65, 0:512])
                mk.dma("g", dnr[h:h + 1, :], m1[64:65, 0:512])
            mk.v.reciprocal(dnr[:], dnr[:])
            for mc in range(4):
                csl = slice(mc * 512, (mc + 1) * 512)
                pr = self.psb[PR]
                mk.t.matmul(pr[0:128, 0:512],
                            self.esel[:, mc * 128:(mc + 1) * 128],
                            dnr[0:8, :], start=True, stop=True)
                self.tt(obuf[:, csl], obuf[:, csl], pr[0:128, 0:512])
            import os as _os
            if w == 0 and l == 0 and _os.environ.get("FFN_DBG"):
                mk.dma("s", self.sc_dbg.ap()[0:128, :], qw[:])
                mk.dma("s", self.sc_dbg.ap()[128:256, :], kcur[:])
                mk.dma("s", self.sc_dbg.ap()[256:384, :], obuf[:])
                mk.dma("s", self.sc_dbg.ap()[384:512, 0:1408], self.emask[:])
                mk.dma("s", self.sc_dbg.ap()[512:640, :], tabs[:])
                mk.dma("s", self.sc_dbg.ap()[640:648, 0:512], dnr[:])
                mk.dma("s", self.sc_dbg.ap()[648:656, 0:512], nrq[:])
                mk.dma("s", self.sc_dbg.ap()[656:664, 0:512], nrk[:])
                mk.dma("s", self.sc_dbg.ap()[792:920, :],
                       bass.AP(ssb, 2048, [[8 * 512, 128], [1, 2048]]))
                mk.dma("s", self.sc_dbg.ap()[920:1024, 0:520],
                       bass.AP(vtm, 0, [[8 * 520, 104], [1, 520]]))
            WOv = self.wo.ap()[l]
            for m in range(8):
                if w < 3:
                    mk.v.tensor_copy(bstash[:, m * 3 + w:m * 3 + w + 1],
                                     xf[:, m * L + t0 + 511:m * L + t0 + 512])
                for kc in range(4):
                    mk.dma("y", wsl[:, kc * 128:(kc + 1) * 128],
                           WOv[kc * 128:(kc + 1) * 128, m * 128:(m + 1) * 128])
                ps = self.psb[PB]
                for kc in range(4):
                    mk.t.matmul(ps[0:128, 0:512], wsl[:, kc * 128:(kc + 1) * 128],
                                obuf[:, kc * 512:(kc + 1) * 512],
                                start=(kc == 0), stop=(kc == 3))
                self.tt(xf[:, m * L + t0:m * L + t0 + 512],
                        xf[:, m * L + t0:m * L + t0 + 512], ps[0:128, 0:512],
                        ALU.add)

    def ffn_mlp(self, l, xf):
        mk = self.mk
        ar = self.ar
        lngb = ar("lngb", [128, 16])
        self.ffn_load_cols(lngb, self.ln_g.ap()[l], 8, off=0)
        self.ffn_load_cols(lngb, self.ln_b.ap()[l], 8, off=8)
        b1row = ar("b1row", [1, 2 * HID])
        b2row = ar("b2row", [1, D])
        mk.dma("y", b1row[:], self.b1.ap()[l])
        mk.dma("y", b2row[:], self.b2.ap()[l])
        xln = ar("xlnb", [128, 8 * 512])
        h1 = ar("h1b", [128, 22 * 512])
        gel = ar("gelb", [128, 512])
        sqm = ar("sqmb", [128, 512])
        rows = ar("rowsb", [1, 3 * 512])
        wsl = [ar("wslm0", [128, 1024]), ar("wslm1", [128, 1024])]
        w2s = [ar("w2sb0", [128, 1408]), ar("w2sb1", [128, 1408])]
        mstash = ar("mstash", [128, 8 * 3])
        x0m = ar("x0m", [128, 4])
        PB0, PB1, PRA, PRB, PRC = 0, 1, 2, 3, 4
        hch = [(i * 128, 128) for i in range(21)] + [(21 * 128, HID - 21 * 128)]

        for tg in range(4):
            t0 = tg * 512
            for i, dc in enumerate(range(4, 8)):
                if tg == 0:
                    mk.g.memset(x0m[:, i:i + 1], 0.0)
                else:
                    mk.v.tensor_copy(x0m[:, i:i + 1],
                                     mstash[:, dc * 3 + tg - 1:dc * 3 + tg])

            def sx(dc, fr0, frw):
                if dc < 4:
                    return xf[:, dc * L + t0 + fr0: dc * L + t0 + fr0 + frw]
                return xf[:, dc * L + t0 + fr0 - 1: dc * L + t0 + fr0 - 1 + frw]

            pra, prb = self.psb[PRA], self.psb[PRB]
            for kc in range(8):
                dst = xln[:, kc * 512:(kc + 1) * 512]
                if kc < 4:
                    mk.v.tensor_copy(dst, xf[:, kc * L + t0: kc * L + t0 + 512])
                else:
                    mk.v.tensor_copy(xln[:, kc * 512 + 1:(kc + 1) * 512],
                                     xf[:, kc * L + t0: kc * L + t0 + 511])
                    mk.v.tensor_copy(xln[:, kc * 512:kc * 512 + 1],
                                     x0m[:, kc - 4:kc - 3])
            for kc in range(8):
                sl = xln[:, kc * 512:(kc + 1) * 512]
                mk.t.matmul(pra[0:1, 0:512], self.ones_col[:], sl,
                            start=(kc == 0), stop=(kc == 7))
                self.act(sqm[:], sl, AF.Square)
                mk.t.matmul(prb[0:1, 0:512], self.ones_col[:], sqm[:],
                            start=(kc == 0), stop=(kc == 7))
            mu = rows[0:1, 0:512]
            ivr = rows[0:1, 512:1024]
            t1 = rows[0:1, 1024:1536]
            self.ts(mu, pra[0:1, 0:512], 1.0 / D)
            self.act(t1, mu, AF.Square)
            self.stt(ivr, prb[0:1, 0:512], 1.0 / D, t1, op0=ALU.mult,
                     op1=ALU.subtract)
            self.act(ivr, ivr, AF.Ln, bias=1e-5)
            self.act(ivr, ivr, AF.Exp, scale=-0.5)
            prc = self.psb[PRC]
            mk.t.matmul(prc[0:128, 0:512], self.ones_row1[0:1, 0:128], mu,
                        start=True, stop=True)
            pra2 = self.psb[PRA]
            mk.t.matmul(pra2[0:128, 0:512], self.ones_row1[0:1, 0:128], ivr,
                        start=True, stop=True)
            for kc in range(8):
                sl = xln[:, kc * 512:(kc + 1) * 512]
                self.tt(sl, sl, prc[0:128, 0:512], ALU.subtract)
                self.tt(sl, sl, pra2[0:128, 0:512])
                self.ts(sl, sl, lngb[:, kc:kc + 1], lngb[:, 8 + kc:9 + kc],
                        op0=ALU.mult, op1=ALU.add)
            slabseq = [(mi, half) for mi in range(22) for half in (0, 1)]
            mk.dma("y", wsl[0][:], self.w1t.ap()[l * 44 + 0 * 22 + 0])
            for idx, (mi, half) in enumerate(slabseq):
                buf = wsl[idx % 2]
                if idx + 1 < len(slabseq):
                    nmi, nhalf = slabseq[idx + 1]
                    mk.dma("y", wsl[(idx + 1) % 2][:],
                           self.w1t.ap()[l * 44 + nhalf * 22 + nmi])
                m0, mw = hch[mi]
                col0 = m0 + half * HID
                ps = self.psb[PB0 if half == 0 else PB1]
                for kc in range(8):
                    mk.t.matmul(ps[0:mw, 0:512],
                                buf[:, kc * 128:kc * 128 + mw],
                                xln[:, kc * 512:(kc + 1) * 512],
                                start=(kc == 0), stop=False)
                mk.t.matmul(ps[0:mw, 0:512],
                            b1row[0:1, col0:col0 + mw],
                            self.ones_row1[0:1, 0:512],
                            start=False, stop=True)
                if half == 1:
                    psA, psB = self.psb[PB0], self.psb[PB1]
                    self.act(gel[0:mw, :], psB[0:mw, 0:512], AF.Gelu)
                    self.tt(h1[0:mw, mi * 512:(mi + 1) * 512],
                            psA[0:mw, 0:512], gel[0:mw, :])
            def w2ap(m, hh):
                return self.w2t.ap()[l * 8 + m][:, hh * 1408:(hh + 1) * 1408]
            mk.dma("y", w2s[0][:], w2ap(0, 0))
            for m in range(8):
                if tg < 3:
                    mk.v.tensor_copy(mstash[:, m * 3 + tg:m * 3 + tg + 1],
                                     xf[:, m * L + t0 + 511:m * L + t0 + 512])
                ps = self.psb[PB0 if m % 2 == 0 else PB1]
                for seg in (0, 1):
                    sidx = m * 2 + seg
                    buf = w2s[sidx % 2]
                    if sidx + 1 < 16:
                        nm, nh = divmod(sidx + 1, 2)
                        mk.dma("y", w2s[(sidx + 1) % 2][:], w2ap(nm, nh))
                    for ki in range(seg * 11, seg * 11 + 11):
                        k0, kwd = hch[ki]
                        c0 = (ki - seg * 11) * 128
                        mk.t.matmul(ps[0:128, 0:512],
                                    buf[0:kwd, c0:c0 + 128],
                                    h1[0:kwd, ki * 512:(ki + 1) * 512],
                                    start=(ki == 0), stop=False)
                mk.t.matmul(ps[0:128, 0:512], b2row[0:1, m * 128:(m + 1) * 128],
                            self.ones_row1[0:1, 0:512], start=False, stop=True)
                self.tt(xf[:, m * L + t0:m * L + t0 + 512],
                        xf[:, m * L + t0:m * L + t0 + 512], ps[0:128, 0:512],
                        ALU.add)



_PROG_CACHE = {}

_HCH = [(i * 128, 128) for i in range(21)] + [(21 * 128, HID - 21 * 128)]


def _tile_w1(w1):
    out = np.zeros((2, 44, 128, 1024), np.float32)
    for l in range(2):
        for half in range(2):
            for mi, (m0, mw) in enumerate(_HCH):
                col0 = half * HID + m0
                blk = w1[l, :, col0:col0 + mw].reshape(8, 128, mw)
                out[l, half * 22 + mi].reshape(128, 8, 128)[:, :, :mw] = \
                    blk.transpose(1, 0, 2)
    return out.reshape(88, 128, 1024)


def _tile_w2(w2):
    out = np.zeros((2, 8, 128, 2816), np.float32)
    for l in range(2):
        for m in range(8):
            for ki, (k0, kwd) in enumerate(_HCH):
                out[l, m, :kwd, ki * 128:(ki + 1) * 128] = \
                    w2[l, k0:k0 + kwd, m * 128:(m + 1) * 128]
    return out.reshape(16, 128, 2816)


def _build_program():
    if "k" not in _PROG_CACHE:
        k = Kern()
        k.build_all()
        _PROG_CACHE["k"] = k
    return _PROG_CACHE["k"]


def kernel(L0_attn_x, L0_attn_W_in, L0_attn_dw_weight, L0_attn_dw_bias,
           L0_attn_W_out, L0_attn_skip, L0_attn_dt_bias, L0_attn_theta_raw,
           L0_attn_k_raw, L0_ffn_wqkv, L0_ffn_q_scale, L0_ffn_k_scale,
           L0_ffn_wo, L0_ffn_ln_g, L0_ffn_ln_b, L0_ffn_w1, L0_ffn_b1,
           L0_ffn_w2, L0_ffn_b2):
    k = _build_program()
    f32 = np.float32
    common = {
        "WinT": np.ascontiguousarray(L0_attn_W_in.T).astype(f32, copy=False),
        "dw_w": np.ascontiguousarray(L0_attn_dw_weight).astype(f32, copy=False),
        "dw_b": np.ascontiguousarray(L0_attn_dw_bias[None, :]).astype(f32, copy=False),
        "WoutT": np.ascontiguousarray(L0_attn_W_out.T).astype(f32, copy=False),
        "skip": np.ascontiguousarray(L0_attn_skip[None, :]).astype(f32, copy=False),
        "dt_bias": np.ascontiguousarray(L0_attn_dt_bias[None, :]).astype(f32, copy=False),
        "thetaT": np.ascontiguousarray(L0_attn_theta_raw.T).astype(f32, copy=False),
        "krawT": np.ascontiguousarray(L0_attn_k_raw.T).astype(f32, copy=False),
        "wqkv": np.ascontiguousarray(L0_ffn_wqkv).astype(f32, copy=False),
        "qscale": np.ascontiguousarray(L0_ffn_q_scale[:, None, :]).astype(f32, copy=False),
        "kscale": np.ascontiguousarray(L0_ffn_k_scale[:, None, :]).astype(f32, copy=False),
        "wo": np.ascontiguousarray(L0_ffn_wo).astype(f32, copy=False),
        "ln_g": np.ascontiguousarray(L0_ffn_ln_g[:, None, :]).astype(f32, copy=False),
        "ln_b": np.ascontiguousarray(L0_ffn_ln_b[:, None, :]).astype(f32, copy=False),
        "w1t": _tile_w1(np.asarray(L0_ffn_w1, dtype=f32)),
        "b1": np.ascontiguousarray(L0_ffn_b1[:, None, :]).astype(f32, copy=False),
        "w2t": _tile_w2(np.asarray(L0_ffn_w2, dtype=f32)),
        "b2": np.ascontiguousarray(L0_ffn_b2[:, None, :]).astype(f32, copy=False),
    }
    in_maps = []
    for b in range(8):
        m = dict(common)
        m["x"] = np.ascontiguousarray(L0_attn_x[b]).astype(f32, copy=False)
        in_maps.append(m)
    res = run_bass_kernel_spmd(k.nc, in_maps, core_ids=list(range(8)))
    return np.stack([res.results[b]["out"] for b in range(8)], axis=0)


# version 08
# speedup vs baseline: 1.2724x; 1.0293x over previous
import numpy as np

import concourse.bass as bass
import concourse.mybir as mybir
from concourse.bass_utils import run_bass_kernel_spmd

F32 = mybir.dt.float32
I32 = mybir.dt.int32
AF = mybir.ActivationFunctionType
ALU = mybir.AluOpType
AX = mybir.AxisListType

L = 2048
D = 1024
DI = 2048
NH = 32
NS = 128
PD = 64
TC = 128
NCH = L // TC
IN_DIM = 2 * DI + NH + NH * 4 * NS
HID = 2730
FH = 8
FD = 64
FI = 512
FW = 512

PI = float(np.pi)
PI32 = float(np.float32(np.pi))
NEG_BIG = -1e9



_RMW_METHODS = {"select", "copy_predicated"}
_DMA_QUEUE_SEM = {"t": "dmt", "v": "dmv", "s": "dms", "g": "dmg", "y": "dmy"}


def _ap_of(x):
    return x if isinstance(x, bass.AP) else None


def _tname(ap):
    return ap.tensor.name


class Op:
    __slots__ = ("engine", "method", "args", "kwargs", "outs", "ins",
                 "inc_token", "waits", "drain", "pos", "is_dma", "dma_q",
                 "_inc_val", "_dma_val", "seq", "dma_slot", "emitted")

    def __init__(self, engine, method, args, kwargs, outs, ins, is_dma=False,
                 dma_q=None):
        self.engine = engine
        self.method = method
        self.args = args
        self.kwargs = kwargs
        self.outs = outs
        self.ins = ins
        self.inc_token = False
        self.waits = []
        self.drain = False
        self.pos = -1
        self.is_dma = is_dma
        self.dma_q = dma_q
        self.emitted = False


class Eng:
    def __init__(self, mk, engine):
        self._mk = mk
        self._e = engine

    def __getattr__(self, method):
        def call(*args, **kwargs):
            return self._mk._record(self._e, method, args, kwargs)
        return call


class MK:
    ENGS = ("t", "v", "s", "g", "y")
    ENG_ATTR = {"t": "tensor", "v": "vector", "s": "scalar", "g": "gpsimd",
                "y": "sync"}

    def __init__(self, nc):
        self.nc = nc
        self.eng_ops = {e: [] for e in self.ENGS}
        self.writer = {}
        self.readers = {}
        self._seq = 0
        self._dma_count = {}
        self.dma_slots = {}
        self._rec_hw = {}
        self._drain_cov = {}
        self.NSLOT = 8
        self.t = Eng(self, "t")
        self.v = Eng(self, "v")
        self.s = Eng(self, "s")
        self.g = Eng(self, "g")
        self.y = Eng(self, "y")

    def _token_of(self, op):
        return ("dma", op.dma_q, op) if op.is_dma else ("eng", op.engine, op)

    def _add_dep(self, op, tok):
        if tok[0] == "eng" and tok[1] == op.engine:
            e = op.engine
            cov = self._drain_cov.get(e, -1)
            if tok[2].seq > cov:
                op.drain = True
                prev = self.eng_ops[e][-1] if self.eng_ops[e] else None
                self._drain_cov[e] = prev.seq if prev is not None else -1
            return
        hw = self._rec_hw.setdefault(op.engine, {})
        key = self._tok_key(tok)
        prev = hw.get(key)
        if prev is not None and prev.seq >= tok[2].seq:
            return
        hw[key] = tok[2]
        if tok[0] == "eng":
            tok[2].inc_token = True
        for t2 in op.waits:
            if t2[2] is tok[2]:
                return
        op.waits.append(tok)

    def _classify(self, method, args, kwargs):
        pos_aps = [_ap_of(a) for a in args]
        pos_aps = [a for a in pos_aps if a is not None]
        kw_out = []
        kw_in = []
        for k, a in kwargs.items():
            ap = _ap_of(a)
            if ap is None:
                continue
            if k in ("out",):
                kw_out.append(ap)
            elif k == "accum_out":
                kw_out.append(ap)
            else:
                kw_in.append(ap)
        if method in ("memset", "memzero", "iota"):
            outs = (kw_out + pos_aps)[:1]
            ins = []
        elif method in _RMW_METHODS:
            allaps = kw_out + pos_aps + kw_in
            outs = allaps[:1]
            ins = allaps
        else:
            if kw_out:
                outs = kw_out
                ins = pos_aps + kw_in
            else:
                outs = pos_aps[:1] + [a for a in kw_out]
                ins = pos_aps[1:] + kw_in
        return outs, ins

    def _tok_key(self, tok):
        if tok[0] == "eng":
            return ("e", tok[1])
        return ("d", tok[2].dma_q, tok[2].dma_slot)

    @staticmethod
    def _prange(ap):
        pitch, cnt = ap.ap[0]
        if pitch == 0:
            return (0, cnt)
        return (ap.tensor.base_partition + ap.offset // pitch, cnt)

    def _check_partitions(self, method, outs, ins):
        if method in ("matmul",):
            if len(ins) >= 2:
                a, b = self._prange(ins[0]), self._prange(ins[1])
                assert a == b, f"matmul operand partitions differ: {a} vs {b}"
            return
        if method in ("transpose", "iota", "memset", "memzero",
                      "partition_all_reduce", "partition_broadcast"):
            return
        rr = [self._prange(ap) for ap in outs + ins]
        assert all(r == rr[0] for r in rr), \
            f"{method}: operand partition ranges differ: {rr}"

    def _record(self, engine, method, args, kwargs):
        is_dma = method in ("dma_start", "dma_start_transpose")
        outs, ins = self._classify(method, args, kwargs)
        if not is_dma:
            self._check_partitions(method, outs, ins)
        self._last_real = getattr(self, "_last_real", {})
        op = Op(engine, method, args, kwargs, outs, ins, is_dma,
                _DMA_QUEUE_SEM[engine] if is_dma else None)
        op.seq = self._seq
        self._seq += 1
        if is_dma:
            q = op.dma_q
            c = self._dma_count.get(q, 0)
            self._dma_count[q] = c + 1
            op.dma_slot = c % self.NSLOT
            self.dma_slots.setdefault((q, op.dma_slot), []).append(op)
        for ap in op.ins:
            n = _tname(ap)
            for w in self.writer.get(n, {}).values():
                self._add_dep(op, w)
            if n.startswith("pb"):
                for r in self.readers.get(n, {}).values():
                    self._add_dep(op, r)
        for ap in op.outs:
            n = _tname(ap)
            for w in self.writer.get(n, {}).values():
                self._add_dep(op, w)
            for r in self.readers.get(n, {}).values():
                self._add_dep(op, r)
        tok = self._token_of(op)
        key = self._tok_key(tok)
        for ap in op.outs:
            n = _tname(ap)
            self.writer.setdefault(n, {})[key] = tok
            self.readers[n] = {}
        for ap in op.ins:
            self.readers.setdefault(_tname(ap), {})[key] = tok
        self.eng_ops[engine].append(op)
        return op

    def dma(self, q, out, in_, **kw):
        return getattr(self, q).dma_start(out=out, in_=in_, **kw)

    def barrier(self):
        last = {}
        for e in self.ENGS:
            if self.eng_ops[e]:
                last[e] = self.eng_ops[e][-1]
        dma_last = {k: lst[-1] for k, lst in self.dma_slots.items() if lst}
        for e in self.ENGS:
            nop = Op(e, "nop", (), {}, [], [])
            nop.seq = self._seq
            self._seq += 1
            for e2, lop in last.items():
                if e2 == e:
                    if not lop.is_dma:
                        nop.drain = True
                        continue
                if lop.emitted and not lop.is_dma:
                    continue
                self._add_dep(nop, self._token_of(lop))
            for (q, sl), dop in dma_last.items():
                self._add_dep(nop, ("dma", q, dop))
            self.eng_ops[e].append(nop)
        self.writer = {}
        self.readers = {}

    def _ensure_sems(self):
        if not hasattr(self, "_eng_sems"):
            self._eng_sems = {e: self.nc.alloc_semaphore(name=f"SE_{e}")
                              for e in self.ENGS}
            self._dma_sems = {}
            self._inc_no = {e: 0 for e in self.ENGS}
            self._emitted = {e: 0 for e in self.ENGS}
            self._wait_hw = {}

    def flush(self):
        nc = self.nc
        self._ensure_sems()
        eng_sems, dma_sems = self._eng_sems, self._dma_sems
        for e in self.ENGS:
            for op in self.eng_ops[e][self._emitted[e]:]:
                if op.is_dma:
                    key = (op.dma_q, op.dma_slot)
                    if key not in dma_sems:
                        dma_sems[key] = nc.alloc_semaphore(
                            name=f"SD_{key[0]}_{key[1]}")
                    lst = self.dma_slots[key]
                    op._dma_val = 16 * (lst.index(op) + 1)
                elif op.inc_token:
                    self._inc_no[e] += 1
                    op._inc_val = self._inc_no[e]

        mkself = self

        def emit_stream(engobj, e, start, end):
            for op in mkself.eng_ops[e][start:end]:
                for tok in op.waits:
                    if tok[0] == "eng":
                        engobj.wait_ge(eng_sems[tok[1]], tok[2]._inc_val)
                    else:
                        d = tok[2]
                        engobj.wait_ge(dma_sems[(d.dma_q, d.dma_slot)],
                                       d._dma_val)
                if op.drain:
                    engobj.drain()
                if op.is_dma and op._dma_val > 16:
                    engobj.wait_ge(dma_sems[(op.dma_q, op.dma_slot)],
                                   op._dma_val - 16)
                if op.method == "nop":
                    inst = engobj.nop()
                else:
                    inst = getattr(engobj, op.method)(*op.args, **op.kwargs)
                if op.is_dma:
                    inst.then_inc(dma_sems[(op.dma_q, op.dma_slot)], 16)
                elif op.inc_token:
                    inst.then_inc(eng_sems[e], 1)

        spans = {e: (self._emitted[e], len(self.eng_ops[e])) for e in self.ENGS}
        if all(a == b for a, b in spans.values()):
            return
        with nc.Block() as blk:
            for e in self.ENGS:
                a, b = spans[e]
                if a == b:
                    continue
                getattr(blk, self.ENG_ATTR[e])(
                    lambda engobj, e=e, a=a, b=b: emit_stream(engobj, e, a, b))
        for e in self.ENGS:
            for op in self.eng_ops[e][self._emitted[e]:spans[e][1]]:
                op.emitted = True
            self._emitted[e] = spans[e][1]

    def realize(self):
        self.barrier()
        self.flush()

    def n_ops(self):
        return sum(len(v) for v in self.eng_ops.values())



def cdiv(a, b):
    return -(-a // b)


class Builder:
    SBUF_BYTES = 207 * 1024

    def __init__(self):
        self.nc = bass.Bass("TRN2", target_bir_lowering=False)
        self.mk = MK(self.nc)
        self._dram = {}
        self._persist_bytes = 0
        self._arena_handle = None
        self._guards = []
        self._arena_off = 0
        self._arena_size = 0
        self._arena_peak = 0
        self._uid = 0

    def uid(self):
        self._uid += 1
        return self._uid

    def dram(self, name, shape, dtype=F32, kind="Internal"):
        h = self.nc.dram_tensor(name, list(shape), dtype, kind=kind)
        self._dram[name] = h
        return h

    def sb(self, name, shape, dtype=F32):
        h = self.nc.alloc_sbuf_tensor(name, list(shape), dtype)
        per_part = int(np.prod(shape[1:])) * mybir.dt.size(dtype)
        self._persist_bytes += ((per_part + 31) // 32) * 32
        return h

    def finish_persist(self):
        pass

    def ar(self, name, shape, dtype=F32):
        g = self.nc.sbuf_tensor(f"{name}_{self.uid()}", list(shape), dtype)
        h = g.__enter__()
        self._guards.append(g)
        per_part = int(np.prod(shape[1:])) * mybir.dt.size(dtype)
        self._arena_off += ((per_part + 31) // 32) * 32
        self._arena_peak = max(self._arena_peak, self._arena_off)
        return h

    def arena_mark(self):
        return len(self._guards)

    def arena_reset(self, mark=0):
        self.mk.barrier()
        self.mk.flush()
        while len(self._guards) > mark:
            self._guards.pop().__exit__(None, None, None)

    def psum(self, name, banks=1, part=128):
        return self.nc.alloc_psum_tensor(name, [part, 512 * banks], F32)

    def constcol(self, val):
        val = float(val)
        if not hasattr(self, "_cpool"):
            self._cpool = self.sb("CPOOL", [128, 8])
            self._cvals = {}
        if val not in self._cvals:
            i = len(self._cvals)
            assert i < 8
            self._cvals[val] = i
            self.mk.g.memset(self._cpool[:, i:i + 1], val)
        i = self._cvals[val]
        return self._cpool[:, i:i + 1]



class Kern(Builder):

    def __init__(self):
        super().__init__()
        self.declare_io()
        self.declare_static()
        self.constcol(2.0)
        self.psb = [self.psum(f"pb{i}") for i in range(8)]
        self.finish_persist()

    def declare_io(self):
        d = self.dram
        self.x_in = d("x", [L, D], kind="ExternalInput")
        self.WinT = d("WinT", [D, IN_DIM], kind="ExternalInput")
        self.dw_w = d("dw_w", [DI, 4], kind="ExternalInput")
        self.dw_b = d("dw_b", [1, DI], kind="ExternalInput")
        self.WoutT = d("WoutT", [DI, D], kind="ExternalInput")
        self.skip_in = d("skip", [1, DI], kind="ExternalInput")
        self.dt_bias = d("dt_bias", [1, NH], kind="ExternalInput")
        self.thetaT = d("thetaT", [NS, NH], kind="ExternalInput")
        self.krawT = d("krawT", [NS, NH], kind="ExternalInput")
        self.wqkt = d("wqkt", [16, 128, 1024], kind="ExternalInput")
        self.wvt = d("wvt", [16, 128, 512], kind="ExternalInput")
        self.wot = d("wot", [16, 128, 512], kind="ExternalInput")
        self.qscale = d("qscale", [2, 1, FD], kind="ExternalInput")
        self.kscale = d("kscale", [2, 1, FD], kind="ExternalInput")
        self.ln_g = d("ln_g", [2, 1, D], kind="ExternalInput")
        self.ln_b = d("ln_b", [2, 1, D], kind="ExternalInput")
        self.w1t = d("w1t", [88, 128, 1024], kind="ExternalInput")
        self.b1 = d("b1", [2, 1, 2 * HID], kind="ExternalInput")
        self.w2t = d("w2t", [16, 128, 2816], kind="ExternalInput")
        self.b2 = d("b2", [2, 1, D], kind="ExternalInput")
        self.out = d("out", [L, D], kind="ExternalOutput")
        self.sc_bc = d("sc_bc", [NH * 4 * NS, L])
        self.sc_u = d("sc_u", [DI, L])
        self.sc_ut = d("sc_ut", [NH * L, PD])
        self.sc_z = d("sc_z", [DI, L])
        self.sc_yg = d("sc_yg", [DI, L])
        self.sc_s = d("sc_s", [D, L])
        self.sc_g = d("sc_g", [D, L])
        self.sc_xn = d("sc_xn", [D, L])
        self.sc_dbg = d("sc_dbg", [1024, 2048])

    def declare_static(self):
        sb = self.sb
        self.ones_col = sb("ones_col", [128, 1])
        self.ones_row1 = sb("ones_row1", [1, 512])
        self.ident = sb("ident", [128, 128])
        self.tri_incl = sb("tri_incl", [128, 128])
        self.hpar = sb("hpar", [NS, 16 * NH])
        self.dwW = sb("dwW", [128, 16 * 4])
        self.dwB = sb("dwB", [128, 16])
        self.skip_c = sb("skip_c", [128, 16])
        self.dtb_col = sb("dtb_col", [NH, 1])

    HP = {name: i for i, name in enumerate(
        ["A", "k2", "sig", "rs", "mos", "mhy",
         "irs_o", "irs_h2", "negA", "k2n", "mos_h", "mhy_h", "kf"])}

    def hp(self, name):
        i = self.HP[name]
        return self.hpar[:, i:i + 1]

    def ts(self, out, in0, s1, s2=None, op0=ALU.mult, op1=ALU.add, eng="v"):
        e = getattr(self.mk, eng)
        if s2 is None:
            return e.tensor_scalar(out, in0, s1, None, op0)
        return e.tensor_scalar(out, in0, s1, s2, op0, op1)

    def stt(self, out, in0, s, in1, op0=ALU.mult, op1=ALU.add, eng="v"):
        e = getattr(self.mk, eng)
        return e.scalar_tensor_tensor(out, in0, s, in1, op0, op1)

    def tt(self, out, a, b, op=ALU.mult, eng="v"):
        e = getattr(self.mk, eng)
        return e.tensor_tensor(out, a, b, op)

    def act(self, out, in_, func, bias=0.0, scale=1.0, accum_out=None):
        kw = {}
        if accum_out is not None:
            kw["accum_out"] = accum_out
        if isinstance(bias, float) and bias not in (0.0, 1.0):
            p = out.ap[0][1] if isinstance(out, bass.AP) else 128
            bias = self.constcol(bias)[0:p, 0:1]
        return self.mk.s.activation(out=out, in_=in_, func=func, bias=bias,
                                    scale=scale, **kw)

    def bcast_free(self, colap, n):
        t = colap.tensor
        psz = colap.ap[0]
        return bass.AP(t, colap.offset, [list(psz), [0, n]])

    def phase_alloc0(self):
        self.stripe01 = self.ar("stripe01", [128, L])
        self.dt_fm = self.ar("dt_fm", [NH, L])
        self.rn_sq = self.ar("rn_sq", [64, L])
        self.rn_tm = self.ar("rn_tm", [128, NCH * 64])
        g = self.mk.g
        g.memset(self.stripe01[:], 1.0)
        g.memset(bass.AP(self.stripe01, 0, [[L, 128], [TC, NCH]]), 0.0)
        return self.arena_mark()

    def build_all(self):
        self.phase_prep()
        self.arena_reset()
        mA = self.phase_alloc0()
        self.phase_ln()
        mB = mA + 1
        self.arena_reset(mB)
        self.phase_dt()
        self.phase_bc_norms()
        self.phase_norm_finish()
        self.arena_reset(mB)
        self.phase_z_xc_u()
        self.arena_reset(mA)
        self.phase_ssm_core()
        self.arena_reset(0)
        self.phase_wout()
        self.arena_reset(0)
        self.phase_ffn()
        self.arena_reset(0)
        self.phase_final()
        self.mk.realize()

    def hpc(self, name, h):
        i = self.HP[name]
        return self.hpar[:, i * NH + h:i * NH + h + 1]

    def hpall(self, name):
        i = self.HP[name]
        return self.hpar[:, i * NH:(i + 1) * NH]

    def phase_prep(self):
        mk, nc = self.mk, self.nc
        g = mk.g
        g.memset(self.ones_col[:], 1.0)
        g.memset(self.ones_row1[:], 1.0)
        g.memset(self.tri_incl[:], 1.0)
        g.affine_select(self.ident[:], self.tri_incl[:], [[1, 128]],
                        ALU.is_ge, 0.0, base=0, channel_multiplier=-1)
        g.affine_select(self.ident[:], self.ident[:], [[-1, 128]],
                        ALU.is_ge, 0.0, base=0, channel_multiplier=1)
        g.affine_select(self.tri_incl[:], self.tri_incl[:], [[1, 128]],
                        ALU.is_ge, 0.0, base=0, channel_multiplier=-1)

        th_raw = self.ar("th_raw", [NS, NH])
        k_raw = self.ar("k_raw", [NS, NH])
        mk.dma("y", th_raw[:], self.thetaT.ap())
        mk.dma("y", k_raw[:], self.krawT.ap())
        A, k2, sig, rs = (self.hpall(n) for n in ("A", "k2", "sig", "rs"))
        def sigmoid_chain(dst, src_ap):
            self.act(dst, src_ap, AF.Exp, scale=-1.0)
            self.act(dst, dst, AF.Ln, bias=1.0)
            self.act(dst, dst, AF.Exp, scale=-1.0)
        sigmoid_chain(A, th_raw[:])
        self.act(A, A, AF.Square, scale=PI)
        sigmoid_chain(k2, k_raw[:])
        self.ts(k2, k2, 0.25)
        self.act(sig, k2, AF.Square)
        self.tt(sig, sig, A, ALU.subtract)
        ab_t = self.ar("abs_sig", [NS, NH])
        self.ts(ab_t[:], sig, -1.0)
        self.tt(ab_t[:], ab_t[:], sig, ALU.max)
        self.ts(ab_t[:], ab_t[:], 1e-30, op0=ALU.max)
        self.act(rs, ab_t[:], AF.Ln)
        self.act(rs, rs, AF.Exp, scale=0.5)
        mos, mhy = self.hpall("mos"), self.hpall("mhy")
        self.ts(mos, sig, 0.0, op0=ALU.is_lt)
        self.ts(mhy, mos, -1.0, 1.0, op0=ALU.mult, op1=ALU.add)
        irs = self.ar("irs", [NS, NH])
        mk.v.reciprocal(irs[:], rs)
        self.tt(self.hpall("irs_o"), mos, irs[:], ALU.mult)
        t2 = self.ar("t2", [NS, NH])
        self.ts(t2[:], irs[:], 0.5)
        self.tt(self.hpall("irs_h2"), mhy, t2[:], ALU.mult)
        self.ts(self.hpall("negA"), A, -1.0)
        self.ts(self.hpall("k2n"), k2, -1.0)
        self.ts(self.hpall("mos_h"), mos, 0.5)
        self.ts(self.hpall("mhy_h"), mhy, 0.5)
        self.ts(self.hpall("kf"), k2, 2.0)

        for c in range(16):
            mk.dma("y", self.dwW[:, c * 4:(c + 1) * 4],
                   self.dw_w.ap()[c * 128:(c + 1) * 128, :])
        self.load_cols16(self.dwB, self.dw_b)
        self.load_cols16(self.skip_c, self.skip_in)
        mk.dma("y", self.dtb_col[:, 0:1],
               bass.AP(self.dt_bias, 0, [[1, NH], [1, 1]]))

    def load_cols16(self, dst, dram_row):
        mk = self.mk
        tmp = self.ar(f"lc16_{self.uid()}", [16, 128])
        mk.dma("y", tmp[:], dram_row.ap().rearrange("o (c p) -> (o c) p", p=128))
        ps = self.psb[7]
        mk.t.transpose(ps[0:128, 0:16], tmp[:], self.ident[0:16, 0:16])
        mk.v.tensor_copy(dst[:, 0:16], ps[0:128, 0:16])

    def trans128(self, dst_sb_ap, src_sb_ap, bank, rows, cols, eng="v", bp=0):
        ps = self.psb[bank]
        self.mk.t.transpose(ps[0:cols, 0:rows], src_sb_ap,
                            self.ident[bp:bp + rows, bp:bp + rows])
        if eng == "s":
            self.mk.s.copy(dst_sb_ap, ps[0:cols, 0:rows])
        else:
            getattr(self.mk, eng).tensor_copy(dst_sb_ap, ps[0:cols, 0:rows])

    def phase_ln(self):
        mk = self.mk
        self.xn = self.ar("xn", [128, 8 * L])
        xtm = self.ar("xtm", [128, D])
        xsq = self.ar("xsq", [128, D])
        cols = self.ar("lncols", [128, 8])
        for ti in range(16):
            mk.dma("y", xtm[:], self.x_in.ap()[ti * 128:(ti + 1) * 128, :])
            ssum = cols[:, 0:1]
            ssq = cols[:, 1:2]
            mu = cols[:, 2:3]
            musq = cols[:, 3:4]
            var = cols[:, 4:5]
            isd = cols[:, 5:6]
            mk.v.reduce_sum(ssum, xtm[:], axis=AX.X)
            self.act(xsq[:], xtm[:], AF.Square, accum_out=ssq)
            self.ts(mu, ssum, 1.0 / D)
            self.act(musq, mu, AF.Square)
            self.stt(var, ssq, 1.0 / D, musq, op0=ALU.mult, op1=ALU.subtract)
            self.act(var, var, AF.Ln, bias=1e-5)
            self.act(isd, var, AF.Exp, scale=-0.5)
            self.ts(xtm[:], xtm[:], mu, op0=ALU.subtract)
            self.ts(xtm[:], xtm[:], isd, op0=ALU.mult)
            for dc in range(8):
                self.trans128(self.xn[:, dc * L + ti * 128: dc * L + ti * 128 + 128],
                              xtm[:, dc * 128:(dc + 1) * 128], bank=dc % 2,
                              rows=128, cols=128)
        for dc in range(8):
            mk.dma("s", self.sc_xn.ap()[dc * 128:(dc + 1) * 128, :],
                   self.xn[:, dc * L:(dc + 1) * L])

    def load_wslab(self, dst, wT_dram, m0, mw, name="w"):
        for kc in range(8):
            self.mk.dma("y", dst[:, kc * mw:(kc + 1) * mw],
                        wT_dram.ap()[kc * 128:(kc + 1) * 128, m0:m0 + mw])

    def proj_mtile(self, wbuf, mw, bank, evac):
        mk = self.mk
        ps = self.psb[bank]
        for tg in range(4):
            for kc in range(8):
                mk.t.matmul(ps[0:mw, 0:512],
                            wbuf[:, kc * mw:(kc + 1) * mw],
                            self.xn[:, kc * L + tg * 512: kc * L + (tg + 1) * 512],
                            start=(kc == 0), stop=(kc == 7))
            evac(tg, ps[0:mw, 0:512])

    def phase_dt(self):
        mk = self.mk
        wdt = self.ar("wdt", [128, 8 * NH])
        self.load_wslab(wdt, self.WinT, 2 * DI, NH)
        dtr = self.ar("dtraw", [NH, 512])

        def evac(tg, ps):
            sl = self.dt_fm[:, tg * 512:(tg + 1) * 512]
            self.ts(dtr[:], ps, self.dtb_col[0:NH, 0:1], 0.0,
                    op0=ALU.add, op1=ALU.min)
            self.act(dtr[:], dtr[:], AF.Exp)
            self.act(dtr[:], dtr[:], AF.Ln, bias=1.0)
            self.ts(sl, dtr[:], 1e-4, 0.1, op0=ALU.max, op1=ALU.min)

        self.proj_mtile(wdt, NH, 6, evac)

    def phase_bc_norms(self):
        mk = self.mk
        wb = [self.ar(f"wbc{i}", [128, 8 * 128]) for i in range(2)]
        bcb = [self.ar("bcb0", [128, 4 * L])]
        sqf = self.ar("sqf", [128, L])
        sqb = self.ar("sqb", [128, 512])
        nstage = self.ar("nstage", [1, 512])
        BC0 = 2 * DI + NH
        for h in range(NH):
            wbuf = wb[h % 2]
            bcbuf = bcb[0]
            for mt in range(4):
                self.load_wslab(wbuf, self.WinT, BC0 + h * 512 + mt * 128, 128)
                ps = self.psb[mt % 2]
                nps = self.psb[2]
                for tg in range(4):
                    for kc in range(8):
                        mk.t.matmul(ps[0:128, 0:512],
                                    wbuf[:, kc * 128:(kc + 1) * 128],
                                    self.xn[:, kc * L + tg * 512: kc * L + (tg + 1) * 512],
                                    start=(kc == 0), stop=(kc == 7))
                    pa = ps[0:128, 0:512]
                    mk.v.tensor_copy(bcbuf[:, mt * L + tg * 512: mt * L + (tg + 1) * 512], pa)
                    if mt % 2 == 0:
                        self.act(sqf[:, tg * 512:(tg + 1) * 512], pa, AF.Square)
                    else:
                        self.act(sqb[:], pa, AF.Square)
                        mk.t.matmul(nps[0:1, 0:512], self.ones_col[:],
                                    sqf[:, tg * 512:(tg + 1) * 512],
                                    start=True, stop=False)
                        mk.t.matmul(nps[0:1, 0:512], self.ones_col[:], sqb[:],
                                    start=False, stop=True)
                        j = mt // 2
                        mk.s.copy(nstage[0:1, 0:512], nps[0:1, 0:512])
                        mk.dma("g", self.rn_sq[2 * h + j:2 * h + j + 1,
                                               tg * 512:(tg + 1) * 512],
                               nstage[0:1, 0:512])
            mk.dma("s", self.sc_bc.ap()[h * 512:(h + 1) * 512, :]
                   .rearrange("(m p) t -> p m t", p=128),
                   bass.AP(bcbuf, 0, [[4 * L, 128], [L, 4], [1, L]]))

    def phase_norm_finish(self):
        mk = self.mk
        self.act(self.rn_sq[:], self.rn_sq[:], AF.Ln, bias=1e-8)
        self.act(self.rn_sq[:], self.rn_sq[:], AF.Exp, scale=-0.5)
        for ti in range(16):
            self.trans128(self.rn_tm[:, ti * 64:(ti + 1) * 64],
                          self.rn_sq[:, ti * 128:(ti + 1) * 128], bank=6,
                          rows=64, cols=128)

    def phase_z_xc_u(self):
        mk = self.mk
        wz = self.ar("wz", [128, 8 * 128])
        wx = self.ar("wx", [128, 8 * 128])
        zb = self.ar("zb", [128, L])
        xcb = self.ar("xcb", [128, 4 + L])
        ub = self.ar("ub", [128, L])
        utb = self.ar("utb", [128, 2 * 16 * 64])
        mk.g.memset(xcb[:, 0:3], 0.0)
        for c in range(16):
            self.load_wslab(wz, self.WinT, c * 128, 128)
            self.load_wslab(wx, self.WinT, DI + c * 128, 128)

            def evz(tg, pa):
                self.act(zb[:, tg * 512:(tg + 1) * 512], pa, AF.Silu)

            def evx(tg, pa):
                mk.v.tensor_copy(xcb[:, 3 + tg * 512: 3 + (tg + 1) * 512], pa)

            self.proj_mtile(wz, 128, 3, evz)
            self.proj_mtile(wx, 128, 4, evx)
            mk.dma("s", self.sc_z.ap()[c * 128:(c + 1) * 128, :], zb[:])
            w4 = self.dwW[:, c * 4:(c + 1) * 4]
            self.ts(ub[:], xcb[:, 0:L], w4[:, 0:1])
            for r in range(1, 4):
                self.stt(ub[:], xcb[:, r:r + L], w4[:, r:r + 1], ub[:])
            self.act(ub[:], ub[:], AF.Silu, bias=self.dwB[:, c:c + 1])
            mk.dma("s", self.sc_u.ap()[c * 128:(c + 1) * 128, :], ub[:])
            for hi in range(2):
                for ci in range(16):
                    self.trans128(utb[:, hi * 1024 + ci * 64: hi * 1024 + (ci + 1) * 64],
                                  ub[hi * 64:(hi + 1) * 64, ci * 128:(ci + 1) * 128],
                                  bank=5, rows=64, cols=128, bp=hi * 64)
                h = 2 * c + hi
                mk.dma("s", self.sc_ut.ap()[h * L:(h + 1) * L, :]
                       .rearrange("(c p) d -> p c d", p=128),
                       bass.AP(utb, hi * 1024, [[2048, 128], [64, 16], [1, 64]]))

    def phase_ssm_core(self):
        mk = self.mk
        ar = self.ar
        bch = ar("bch", [128, L])
        dt_bc = ar("dt_bc", [128, L])
        rd = ar("rd", [128, L])
        ab = ar("abb", [128, L])
        bb = ar("bbb", [128, L])
        idet = ar("idet", [128, L])
        Kf = [ar(f"Kf{j}", [128, L]) for j in range(4)]
        Qf = [ar(f"Qf{j}", [128, L]) for j in range(4)]
        w1 = ar("w1", [128, L]); w2 = ar("w2", [128, L]); w3 = ar("w3", [128, L])
        ibuf = ar("ibuf", [128, L // 4], I32)
        ebuf = ar("ebuf", [128, 16 * 16])
        dtrow = ar("dtrow", [1, L])
        rcrow = ar("rcrow", [1, L])
        ygb = ar("ygb", [64, L])
        rCb = ar("rCb", [64, L])
        utc = ar("utc", [128, PD])
        gsb = ar("gsb", [128, 128])
        kysb = ar("kysb", [128, 128])
        ktmp = ar("ktmp", [128, 128])
        Yst = [ar("Yst0", [128, PD]), ar("Yst1", [128, PD])]
        Vst = [ar("Vst0", [128, PD]), ar("Vst1", [128, PD])]
        sYt = ar("sYt", [128, PD])

        def eslot(i):
            return ebuf[:, i * 16:(i + 1) * 16]

        def ecol(i, c):
            return ebuf[:, i * 16 + c:i * 16 + c + 1]

        PG, PO, PKY, PKV, PYA, PVA, PRP = 0, 1, 2, 3, 4, 5, 6

        for h in range(NH):
            hc = lambda name: self.hpc(name, h)
            mk.dma("g", dtrow[0:1, :], self.dt_fm[h:h + 1, :])
            mk.dma("g", rcrow[0:1, :], self.rn_sq[2 * h + 1:2 * h + 2, :])
            prp = self.psb[PRP]
            for tg in range(4):
                mk.t.matmul(prp[0:128, 0:512], self.ones_row1[0:1, 0:128],
                            dtrow[0:1, tg * 512:(tg + 1) * 512],
                            start=True, stop=True)
                mk.s.copy(dt_bc[:, tg * 512:(tg + 1) * 512], prp[0:128, 0:512])
            for tg in range(4):
                mk.t.matmul(prp[0:64, 0:512], self.ones_row1[0:1, 0:64],
                            rcrow[0:1, tg * 512:(tg + 1) * 512],
                            start=True, stop=True)
                mk.s.copy(rCb[0:64, tg * 512:(tg + 1) * 512], prp[0:64, 0:512])
            mk.dma("y", bch[:], self.sc_bc.ap()[h * 512 + 128:h * 512 + 256, :])

            self.ts(w1[:], dt_bc[:], hc("k2"), 1.0)
            self.ts(w2[:], dt_bc[:], hc("rs"))
            self.ts(w3[:], dt_bc[:], hc("A"), hc("kf"))
            self.tt(w3[:], w3[:], dt_bc[:])
            self.act(w3[:], w3[:], AF.Ln, bias=1.0)
            self.act(rd[:], w3[:], AF.Exp, scale=-1.0)
            self.tt(bb[:], w1[:], w2[:], ALU.add)
            self.act(bb[:], bb[:], AF.Ln)
            self.ts(idet[:], w3[:], hc("mos_h"))
            self.stt(idet[:], bb[:], hc("mhy"), idet[:])
            self.tt(ab[:], w1[:], w2[:], ALU.subtract)
            self.act(ab[:], ab[:], AF.Ln)
            mk.v.reciprocal(w1[:], w1[:])
            self.tt(w2[:], w2[:], w1[:])
            self.act(w2[:], w2[:], AF.Arctan)
            self.ts(w2[:], w2[:], hc("mos"))
            self.stt(ab[:], ab[:], hc("mhy"), w2[:])
            mk.v.tensor_tensor_scan(w1[:], self.stripe01[:], idet[:], 0.0,
                                    ALU.mult, ALU.add)
            mk.v.tensor_tensor_scan(w2[:], self.stripe01[:], ab[:], 0.0,
                                    ALU.mult, ALU.add)
            mk.v.tensor_tensor_scan(idet[:], self.stripe01[:], w3[:], 0.0,
                                    ALU.mult, ALU.add)
            self.act(idet[:], idet[:], AF.Exp, scale=-1.0)
            self.act(w1[:], w1[:], AF.Exp)
            self.act(w3[:], w2[:], AF.Exp)
            self.ts(bb[:], w2[:], 1.0 / (2 * PI))
            for hh in range(4):
                hsl = slice(hh * (L // 4), (hh + 1) * (L // 4))
                mk.v.tensor_copy(ibuf[:], bb[:, hsl])
                mk.v.tensor_copy(bb[:, hsl], ibuf[:])
            self.stt(bb[:], bb[:], -2 * PI, w2[:])
            self.ts(Kf[1][:], bb[:], PI32, op0=ALU.is_gt)
            self.stt(bb[:], Kf[1][:], -2 * PI, bb[:])
            self.ts(bb[:], bb[:], -PI32, PI32, op0=ALU.max, op1=ALU.min)
            self.act(bb[:], bb[:], AF.Sin)
            self.ts(Kf[0][:], w2[:], 1.0 / (2 * PI), 0.25)
            for hh in range(4):
                hsl = slice(hh * (L // 4), (hh + 1) * (L // 4))
                mk.v.tensor_copy(ibuf[:], Kf[0][:, hsl])
                mk.v.tensor_copy(Kf[0][:, hsl], ibuf[:])
            self.stt(Kf[0][:], Kf[0][:], -2 * PI, w2[:])
            self.ts(Kf[1][:], Kf[0][:], PI32 / 2, op0=ALU.is_gt)
            self.stt(Kf[0][:], Kf[1][:], -2 * PI, Kf[0][:])
            self.ts(Kf[0][:], Kf[0][:], -PI32 - PI / 2, PI32 - PI / 2,
                    op0=ALU.max, op1=ALU.min)
            self.act(Kf[0][:], Kf[0][:], AF.Sin, bias=PI / 2)
            self.tt(w2[:], w1[:], Kf[0][:])
            self.tt(Kf[1][:], w1[:], w3[:], ALU.add)
            self.ts(w2[:], w2[:], hc("mos"))
            self.stt(ab[:], Kf[1][:], hc("mhy_h"), w2[:])
            self.tt(Kf[1][:], w1[:], w3[:], ALU.subtract)
            self.tt(w1[:], w1[:], bb[:])
            self.ts(w1[:], w1[:], hc("irs_o"))
            self.stt(bb[:], Kf[1][:], hc("irs_h2"), w1[:])

            self.tt(w1[:], dt_bc[:], rd[:])
            self.tt(w2[:], w1[:], bch[:])
            self.tt(w1[:], dt_bc[:], w2[:])
            mk.dma("y", bch[:], self.sc_bc.ap()[h * 512:h * 512 + 128, :])
            self.tt(w3[:], dt_bc[:], bch[:])
            self.tt(w3[:], w3[:], w1[:], ALU.add)
            self.tt(Kf[0][:], ab[:], w3[:])
            self.tt(Kf[1][:], ab[:], w2[:])
            self.ts(w1[:], bb[:], -1.0)
            self.tt(Kf[2][:], w1[:], w3[:])
            self.tt(Kf[3][:], w1[:], w2[:])
            mk.dma("y", bch[:], self.sc_bc.ap()[h * 512 + 384:h * 512 + 512, :])
            self.ts(w1[:], bch[:], hc("negA"))
            self.ts(w2[:], bch[:], hc("k2n"))
            self.tt(w3[:], ab[:], bch[:])
            self.ts(Qf[3][:], bb[:], hc("sig"))
            self.tt(Qf[3][:], Qf[3][:], bch[:])
            mk.dma("y", bch[:], self.sc_bc.ap()[h * 512 + 256:h * 512 + 384, :])
            self.stt(w1[:], bch[:], hc("k2"), w1[:])
            self.tt(w2[:], bch[:], w2[:], ALU.add)
            self.tt(Qf[1][:], bb[:], w2[:])
            self.tt(Qf[1][:], w3[:], Qf[1][:], ALU.add)
            self.tt(Qf[1][:], Qf[1][:], idet[:])
            self.tt(w3[:], ab[:], bch[:])
            self.tt(Qf[0][:], bb[:], w1[:])
            self.tt(Qf[0][:], w3[:], Qf[0][:], ALU.add)
            self.tt(Qf[0][:], Qf[0][:], idet[:])
            self.ts(w3[:], bb[:], hc("sig"))
            self.tt(Qf[2][:], w3[:], bch[:])
            self.tt(w1[:], ab[:], w1[:])
            self.tt(Qf[2][:], Qf[2][:], w1[:], ALU.add)
            self.tt(Qf[2][:], Qf[2][:], idet[:])
            self.tt(w2[:], ab[:], w2[:])
            self.tt(Qf[3][:], Qf[3][:], w2[:], ALU.add)
            self.tt(Qf[3][:], Qf[3][:], idet[:])

            mk.v.tensor_copy(eslot(0), bass.AP(ab, TC - 1, [[L, 128], [TC, 16]]))
            mk.v.tensor_copy(eslot(1), bass.AP(bb, TC - 1, [[L, 128], [TC, 16]]))
            mk.v.tensor_copy(eslot(2), bass.AP(idet, TC - 1, [[L, 128], [TC, 16]]))
            self.tt(eslot(3), eslot(0), eslot(2))
            self.tt(eslot(4), eslot(1), eslot(2))
            self.stt(eslot(5), eslot(4), hc("k2"), eslot(3))
            self.ts(eslot(6), eslot(4), hc("sig"))
            self.stt(eslot(7), eslot(3), hc("k2"), eslot(6))
            self.ts(eslot(8), eslot(4), hc("negA"))
            self.stt(eslot(9), eslot(4), hc("k2n"), eslot(3))
            self.ts(eslot(10), eslot(3), hc("negA"))
            self.stt(eslot(11), eslot(3), hc("k2n"), eslot(6))
            self.stt(eslot(12), eslot(1), hc("k2"), eslot(0))
            self.tt(eslot(12), eslot(12), eslot(2))
            self.ts(eslot(13), eslot(4), hc("negA"))
            self.stt(eslot(14), eslot(1), hc("k2n"), eslot(0))
            self.tt(eslot(14), eslot(14), eslot(2))
            DQ = {1: 5, 2: 4, 3: 7, 4: 3}
            FQ = {1: 8, 2: 9, 3: 10, 4: 11}

            Y, V = Yst[0], Vst[0]
            Y2, V2 = Yst[1], Vst[1]
            for c in range(NCH):
                sl = slice(c * TC, (c + 1) * TC)
                mk.dma("y", utc[:], self.sc_ut.ap()[h * L + c * TC:
                                                    h * L + (c + 1) * TC, :])
                self.ts(utc[:], utc[:],
                        self.rn_tm[:, c * 64 + 2 * h:c * 64 + 2 * h + 1])
                pg = self.psb[PG]
                for j in range(4):
                    mk.t.matmul(pg[0:128, 0:128], Kf[j][:, sl], Qf[j][:, sl],
                                start=(j == 0), stop=(j == 3))
                self.tt(gsb[:], pg[0:128, 0:128], self.tri_incl[:])
                po = self.psb[PO]
                oc = (c % 4) * 128
                mk.t.matmul(po[0:64, oc:oc + 128], utc[:], gsb[:],
                            start=True, stop=(c == 0))
                if c > 0:
                    mk.t.matmul(po[0:64, oc:oc + 128], Y[:, 0:64], Qf[0][:, sl],
                                start=False, stop=False)
                    mk.t.matmul(po[0:64, oc:oc + 128], V[:, 0:64], Qf[1][:, sl],
                                start=False, stop=True)
                pky, pkv = self.psb[PKY], self.psb[PKV]
                for j in (1, 2, 3, 4):
                    self.ts(ktmp[:], Kf[j - 1][:, sl], ecol(DQ[j], c))
                    mk.t.matmul(pky[0:128, 0:128], ktmp[:], self.ident[:],
                                is_transpose=True, start=(j == 1), stop=(j == 4))
                mk.s.copy(kysb[:], pky[0:128, 0:128])
                pya, pva = self.psb[PYA], self.psb[PVA]
                mk.t.matmul(pya[0:128, 0:64], kysb[:], utc[:],
                            start=True, stop=True)
                for j in (1, 2, 3, 4):
                    self.ts(ktmp[:], Kf[j - 1][:, sl], ecol(FQ[j], c))
                    mk.t.matmul(pkv[0:128, 0:128], ktmp[:], self.ident[:],
                                is_transpose=True, start=(j == 1), stop=(j == 4))
                mk.s.copy(kysb[:], pkv[0:128, 0:128])
                mk.t.matmul(pva[0:128, 0:64], kysb[:], utc[:],
                            start=True, stop=True)
                if c == 0:
                    mk.s.copy(Y2[:], pya[0:128, 0:64])
                    mk.s.copy(V2[:], pva[0:128, 0:64])
                else:
                    self.stt(sYt[:], Y[:], ecol(12, c), pya[0:128, 0:64])
                    self.stt(Y2[:], V[:], ecol(4, c), sYt[:])
                    self.stt(sYt[:], Y[:], ecol(13, c), pva[0:128, 0:64])
                    self.stt(V2[:], V[:], ecol(14, c), sYt[:])
                Y, Y2 = Y2, Y
                V, V2 = V2, V
                if c % 4 == 3:
                    g0 = (c - 3) * TC
                    self.tt(ygb[0:64, g0:g0 + 512], po[0:64, 0:512],
                            rCb[0:64, g0:g0 + 512])
            mk.dma("s", self.sc_yg.ap()[h * 64:(h + 1) * 64, :], ygb[0:64, :])

    def phase_wout(self):
        mk = self.mk
        ygs = [self.ar(f"ygs{i}", [128, 512]) for i in range(16)]
        uzt = self.ar("uzt", [128, 512])
        wsl = [self.ar(f"wsl{i}", [128, 16 * 128]) for i in range(2)]
        sout = self.ar("sout", [128, 512])
        for tg in range(4):
            for kc in range(16):
                mk.dma("y", ygs[kc][:],
                       self.sc_yg.ap()[kc * 128:(kc + 1) * 128,
                                       tg * 512:(tg + 1) * 512])
                mk.dma("y", uzt[:], self.sc_u.ap()[kc * 128:(kc + 1) * 128,
                                                   tg * 512:(tg + 1) * 512])
                self.stt(ygs[kc][:], uzt[:], self.skip_c[:, kc:kc + 1],
                         ygs[kc][:])
                mk.dma("y", uzt[:], self.sc_z.ap()[kc * 128:(kc + 1) * 128,
                                                   tg * 512:(tg + 1) * 512])
                self.tt(ygs[kc][:], ygs[kc][:], uzt[:])
            for m in range(8):
                wb = wsl[m % 2]
                if tg == 0 or True:
                    for kc in range(16):
                        mk.dma("y", wb[:, kc * 128:(kc + 1) * 128],
                               self.WoutT.ap()[kc * 128:(kc + 1) * 128,
                                               m * 128:(m + 1) * 128])
                ps = self.psb[m % 2]
                for kc in range(16):
                    mk.t.matmul(ps[0:128, 0:512], wb[:, kc * 128:(kc + 1) * 128],
                                ygs[kc][:], start=(kc == 0), stop=(kc == 15))
                mk.v.tensor_copy(sout[:], ps[0:128, 0:512])
                mk.dma("s", self.sc_s.ap()[m * 128:(m + 1) * 128,
                                           tg * 512:(tg + 1) * 512], sout[:])

    def phase_final(self):
        mk = self.mk
        xt = self.ar("fxt", [128, D])
        st = self.ar("fst", [128, D])
        gt = self.ar("fgt", [128, D])
        sfm = self.ar("sfm", [128, D])
        gfm = self.ar("gfm", [128, D])
        for ti in range(16):
            mk.dma("y", xt[:], self.x_in.ap()[ti * 128:(ti + 1) * 128, :])
            for dc in range(8):
                mk.dma("y", sfm[:, dc * 128:(dc + 1) * 128],
                       self.sc_s.ap()[dc * 128:(dc + 1) * 128,
                                      ti * 128:(ti + 1) * 128])
                mk.dma("y", gfm[:, dc * 128:(dc + 1) * 128],
                       self.sc_g.ap()[dc * 128:(dc + 1) * 128,
                                      ti * 128:(ti + 1) * 128])
            for dc in range(8):
                self.trans128(st[:, dc * 128:(dc + 1) * 128],
                              sfm[:, dc * 128:(dc + 1) * 128], bank=dc % 2,
                              rows=128, cols=128)
                self.trans128(gt[:, dc * 128:(dc + 1) * 128],
                              gfm[:, dc * 128:(dc + 1) * 128], bank=2 + dc % 2,
                              rows=128, cols=128, eng="s")
            self.act(gt[:], gt[:], AF.Silu)
            self.tt(st[:], st[:], gt[:])
            self.tt(st[:], st[:], xt[:], ALU.add)
            mk.dma("s", self.out.ap()[ti * 128:(ti + 1) * 128, :], st[:])

    def ffn_load_cols(self, dst, dram_row_ap, n, off=0):
        tmp = self.ar(f"flc_{self.uid()}", [max(n, 1), 128])
        self.mk.dma("y", tmp[0:n, :],
                    dram_row_ap.rearrange("o (c p) -> (o c) p", p=128))
        ps = self.psb[7]
        self.mk.t.transpose(ps[0:128, 0:n], tmp[0:n, :], self.ident[0:n, 0:n])
        self.mk.v.tensor_copy(dst[:, off:off + n], ps[0:128, 0:n])

    def phase_ffn_prep(self):
        mk = self.mk
        g = mk.g
        ar = self.ar
        self.t_row = ar("t_row", [1, 512])
        self.pw_row = ar("pw_row", [1, 512])
        self.if_row = ar("if_row", [1, 128])
        self.lb_row = ar("lb_row", [1, 128])
        dr = ar("dr", [1, 128])
        g.iota(dr[:], [[1, 128]], channel_multiplier=0,
               allow_small_or_imprecise_dtypes=True)
        tmp = ar("drt", [1, 128])
        ib2 = ar("ib2", [1, 128], I32)
        self.ts(tmp[:], dr[:], 1.0 / 32.0)
        mk.v.tensor_copy(ib2[:], tmp[:])
        mk.v.tensor_copy(tmp[:], ib2[:])
        self.stt(dr[:], tmp[:], -32.0, dr[:])
        self.ts(tmp[:], dr[:], 0.0, op0=ALU.is_lt)
        self.stt(dr[:], tmp[:], 32.0, dr[:])
        self.ts(self.if_row[:], dr[:], -2.0 / 64.0 * float(np.log(10000.0)))
        self.act(self.if_row[:], self.if_row[:], AF.Exp)
        self.ts(self.lb_row[:], dr[:], 2.0 / 89.6, 25.6 / 89.6)
        self.act(self.lb_row[:], self.lb_row[:], AF.Ln)
        self.emask = ar("emask", [128, 1408])
        g.memset(self.emask[:], 0.0)
        g.affine_select(self.emask[:], self.emask[:], [[1, 1408]], ALU.is_ge,
                        NEG_BIG, base=-384, channel_multiplier=-1)
        g.affine_select(self.emask[:], self.emask[:], [[-1, 1408]], ALU.is_ge,
                        NEG_BIG, base=896, channel_multiplier=1)
        self.esel = ar("esel", [8, 4 * 128])
        g.memset(self.esel[:], 0.0)
        for dc in range(4):
            sl = self.esel[:, dc * 128:(dc + 1) * 128]
            g.memset(sl, 1.0)
            g.affine_select(sl, sl, [[1, 128]], ALU.is_ge, 0.0,
                            base=128 * dc, channel_multiplier=-64)
            g.affine_select(sl, sl, [[-1, 128]], ALU.is_ge, 0.0,
                            base=63 - 128 * dc, channel_multiplier=64)
        self.sgnc = ar("sgnc", [128, 1])
        g.memset(self.sgnc[:], 1.0)
        g.memset(self.sgnc[0:32, 0:1], -1.0)
        g.memset(self.sgnc[64:96, 0:1], -1.0)

    def phase_ffn(self, cut=None):
        mk = self.mk
        xf = self.ar("xf", [128, 8 * L])
        for dc in range(8):
            mk.dma("y", xf[:, dc * L:(dc + 1) * L],
                   self.sc_xn.ap()[dc * 128:(dc + 1) * 128, :])
        self.phase_ffn_prep()
        ffn_mark = self.arena_mark()
        done = False
        for l in range(2):
            self.ffn_attn(l, xf)
            self.arena_reset(ffn_mark)
            if cut == f"attn{l}":
                done = True
                break
            self.ffn_mlp(l, xf)
            self.arena_reset(ffn_mark)
            if cut == f"mlp{l}":
                done = True
                break
        for dc in range(8):
            mk.dma("s", self.sc_g.ap()[dc * 128:(dc + 1) * 128, :],
                   xf[:, dc * L:(dc + 1) * L])

    def ffn_attn(self, l, xf):
        mk = self.mk
        ar = self.ar
        qs_c = ar("qs_c", [128, 2])
        qs_s = ar("qs_s", [128, 2])
        t2 = ar("t2r", [2, 64])
        mk.dma("y", t2[0:1, :], self.qscale.ap()[l])
        mk.dma("y", t2[1:2, :], self.kscale.ap()[l])
        ps7 = self.psb[7]
        mk.t.transpose(ps7[0:64, 0:2], t2[0:2, 0:64], self.ident[0:2, 0:2])
        mk.v.tensor_copy(qs_c[0:64, 0:2], ps7[0:64, 0:2])
        mk.dma("g", qs_c[64:128, 0:2], qs_c[0:64, 0:2])
        for base in (0, 64):
            mk.dma("g", qs_s[base:base + 32, 0:2], qs_c[base + 32:base + 64, 0:2])
            mk.dma("g", qs_s[base + 32:base + 64, 0:2], qs_c[base:base + 32, 0:2])
        qw = ar("qw", [128, 4 * 512])
        kw = [ar("kw0", [128, 4 * 512]), ar("kw1", [128, 4 * 512])]
        qsh = ar("qsh", [128, 512])
        vtm = ar("vtm", [128, 8 * 520])
        tabs = ar("tabs", [128, 4 * 512])
        ssb = ar("ssb", [128, 8 * 512])
        sq = ar("sqf2", [128, 512])
        nrq = ar("nrq", [8, 512])
        nrk = ar("nrk", [8, 512])
        nst = ar("nst", [1, 512])
        m1 = ar("m1w", [128, 512])
        mxc = ar("mxc", [128, 4])
        mxe = ar("mxe", [4, 128])
        mxr = ar("mxr", [1, 512])
        dnr = ar("dnr", [8, 512])
        obuf = ar("obuf", [128, 4 * 512])
        wsl = [ar("wslf0", [128, 1024]), ar("wslf1", [128, 1024])]
        vsl = ar("vslf", [128, 2 * 512])
        ib3 = ar("ib3", [128, 512], I32)
        wf = m1
        bstash = ar("bstash", [128, 8 * 3])
        x0s = obuf
        PA, PB, PC, PS0, PS1, PV, PR = 0, 1, 2, 3, 4, 5, 6
        for ti in range(8):
            for hh in range(FH):
                mk.g.memset(vtm[:, ti * 520 + hh * 65 + 64:
                                ti * 520 + hh * 65 + 65], 1.0)

        for w in range(4):
            t0 = w * FW
            kcur = kw[w % 2]
            kprev = kw[(w + 1) % 2]
            for i, dc in enumerate(range(4, 8)):
                if w == 0:
                    mk.g.memset(x0s[:, i * 512:i * 512 + 1], 0.0)
                else:
                    mk.v.tensor_copy(x0s[:, i * 512:i * 512 + 1],
                                     bstash[:, dc * 3 + (w - 1):dc * 3 + w])
                mk.v.tensor_copy(x0s[:, i * 512 + 1:(i + 1) * 512],
                                 xf[:, dc * L + t0: dc * L + t0 + 511])

            def rhs_x(dc, fr0, frw):
                if dc < 4:
                    return xf[:, dc * L + t0 + fr0: dc * L + t0 + fr0 + frw]
                return x0s[:, (dc - 4) * 512 + fr0:(dc - 4) * 512 + fr0 + frw]

            mk.g.iota(self.t_row[:], [[1, 512]], base=t0, channel_multiplier=0,
                      allow_small_or_imprecise_dtypes=True)
            self.ts(self.pw_row[:], self.t_row[:], 1.0 / 256.0, -1024.0 / 256.0)
            pt = self.psb[PA]
            mk.t.matmul(pt[0:128, 0:512], self.if_row[:],
                        self.t_row[0:1, 0:512], start=True, stop=True)
            for which, bias in (("sin", 0.0), ("cos", 0.25)):
                self.ts(wf[:], pt[0:128, 0:512], 1.0 / (2 * PI), bias)
                mk.v.tensor_copy(ib3[:], wf[:])
                mk.v.tensor_copy(wf[:], ib3[:])
                self.stt(wf[:], wf[:], -2 * PI, pt[0:128, 0:512])
                thr = PI32 if which == "sin" else PI32 / 2
                self.ts(sq[:], wf[:], thr, op0=ALU.is_gt)
                self.stt(wf[:], sq[:], -2 * PI, wf[:])
                if which == "sin":
                    self.ts(wf[:], wf[:], -PI32, PI32, op0=ALU.max, op1=ALU.min)
                    self.act(qsh[:], wf[:], AF.Sin)
                else:
                    self.ts(wf[:], wf[:], -PI32 - PI / 2, PI32 - PI / 2,
                            op0=ALU.max, op1=ALU.min)
                    self.act(tabs[:, 3 * 512:4 * 512], wf[:], AF.Sin,
                             bias=PI / 2)
            mk.t.matmul(pt[0:128, 0:512], self.lb_row[:],
                        self.pw_row[0:1, 0:512], start=True, stop=True)
            self.act(wf[:], pt[0:128, 0:512], AF.Exp)
            self.tt(tabs[:, 0:512], tabs[:, 3 * 512:4 * 512], wf[:])
            self.tt(tabs[:, 512:1024], qsh[:], wf[:])
            self.act(wf[:], pt[0:128, 0:512], AF.Exp, scale=-1.0)
            self.tt(tabs[:, 1024:1536], tabs[:, 3 * 512:4 * 512], wf[:])
            self.tt(tabs[:, 3 * 512:4 * 512], qsh[:], wf[:])
            self.ts(tabs[:, 0:512], tabs[:, 0:512], qs_c[:, 0:1])
            self.ts(tabs[:, 0:512], tabs[:, 0:512], 8.0)
            self.tt(tabs[:, 512:1024], tabs[:, 512:1024],
                    self.bcast_free(self.sgnc[:, 0:1], 512))
            self.ts(tabs[:, 512:1024], tabs[:, 512:1024], qs_s[:, 0:1])
            self.ts(tabs[:, 512:1024], tabs[:, 512:1024], 8.0)
            self.ts(tabs[:, 1024:1536], tabs[:, 1024:1536], qs_c[:, 1:2])
            self.tt(tabs[:, 3 * 512:4 * 512], tabs[:, 3 * 512:4 * 512],
                    self.bcast_free(self.sgnc[:, 0:1], 512))
            self.ts(tabs[:, 3 * 512:4 * 512], tabs[:, 3 * 512:4 * 512],
                    qs_s[:, 1:2])

            qkseq = [("q", mc) for mc in range(4)] + [("k", mc) for mc in range(4)]
            mk.dma("y", wsl[0][:], self.wqkt.ap()[l * 8 + 0])
            for qidx, (side, mc) in enumerate(qkseq):
                dst = qw if side == "q" else kcur
                buf = wsl[qidx % 2]
                if qidx + 1 < 8:
                    nside, nmc = qkseq[qidx + 1]
                    mk.dma("y", wsl[(qidx + 1) % 2][:],
                           self.wqkt.ap()[l * 8 +
                                          (0 if nside == "q" else 1) * 4 + nmc])
                if True:
                    ps = self.psb[PB]
                    for kc in range(8):
                        mk.t.matmul(ps[0:128, 0:512],
                                    buf[:, kc * 128:(kc + 1) * 128],
                                    rhs_x(kc, 0, 512), start=(kc == 0),
                                    stop=(kc == 7))
                    pa = ps[0:128, 0:512]
                    mk.v.tensor_copy(dst[:, mc * 512:(mc + 1) * 512], pa)
                    self.act(sq[:], pa, AF.Square)
                    nps = self.psb[PC]
                    nrt = nrq if side == "q" else nrk
                    for hh in range(2):
                        b = hh * 64
                        mk.t.matmul(nps[0:1, 0:512],
                                    self.ones_col[b:b + 64, 0:1],
                                    sq[b:b + 64, :], start=True, stop=True)
                        mk.s.copy(nst[0:1, 0:512], nps[0:1, 0:512])
                        mk.dma("g", nrt[2 * mc + hh:2 * mc + hh + 1, :],
                               nst[0:1, 0:512])
            vpb = [PA, PS0, PS1, PV]
            for kc in range(8):
                vs = vsl[:, (kc % 2) * 512:(kc % 2) * 512 + 512]
                mk.dma("y", vs, self.wvt.ap()[l * 8 + kc])
                for i in range(4):
                    mk.t.matmul(self.psb[vpb[i]][0:128, 0:512],
                                rhs_x(kc, i * 128, 128), vs,
                                start=(kc == 0), stop=(kc == 7))
            for i in range(4):
                ti = (4 * w + i) % 8
                mk.v.tensor_copy(
                    bass.AP(vtm, ti * 520, [[8 * 520, 128], [65, 8], [1, 64]]),
                    self.psb[vpb[i]][0:128, 0:512])
            for nrt in (nrq, nrk):
                self.ts(nrt[:], nrt[:], 1e-24, op0=ALU.max)
                self.act(nrt[:], nrt[:], AF.Ln)
                self.act(nrt[:], nrt[:], AF.Exp, scale=-0.5)
            for side, dst in (("q", qw), ("k", kcur)):
                nrt = nrq if side == "q" else nrk
                tb = 0 if side == "q" else 1024
                for mc in range(4):
                    csl = slice(mc * 512, (mc + 1) * 512)
                    pr = self.psb[PR]
                    mk.t.matmul(pr[0:128, 0:512],
                                self.esel[:, mc * 128:(mc + 1) * 128],
                                nrt[0:8, :], start=True, stop=True)
                    self.tt(dst[:, csl], dst[:, csl], pr[0:128, 0:512])
                    for b in (0, 64):
                        mk.dma("g", qsh[b:b + 32, :], dst[b + 32:b + 64, csl])
                        mk.dma("g", qsh[b + 32:b + 64, :], dst[b:b + 32, csl])
                    self.tt(qsh[:], qsh[:], tabs[:, tb + 512:tb + 1024])
                    self.tt(dst[:, csl], dst[:, csl], tabs[:, tb:tb + 512])
                    self.tt(dst[:, csl], dst[:, csl], qsh[:], ALU.add)

            for h in range(FH):
                hb = (h % 2) * 64
                mc = h // 2
                csl = slice(mc * 512, (mc + 1) * 512)
                rlist = list(range(4, 8)) if w == 0 else list(range(8))
                qs_ap = qw[hb:hb + 64, csl]
                for idx, r in enumerate(rlist):
                    kbuf = kprev if r < 4 else kcur
                    co = mc * 512 + (r % 4) * 128
                    pss = self.psb[[PS0, PS1][idx % 2]]
                    mk.t.matmul(pss[0:128, 0:512], kbuf[hb:hb + 64, co:co + 128],
                                qs_ap, start=True, stop=True)
                    self.tt(ssb[:, r * 512:(r + 1) * 512], pss[0:128, 0:512],
                            self.emask[:, 896 - 128 * r:1408 - 128 * r],
                            ALU.add)
                nr_ = len(rlist)
                r0_ = rlist[0]
                view = bass.AP(ssb, r0_ * 512,
                               [[8 * 512, 128], [1, 512], [512, nr_]])
                mk.v.reduce_max(m1[:, 0:512], view, axis=AX.X)
                for qsub in range(4):
                    ps = self.psb[PR]
                    mk.t.transpose(ps[0:128, 0:128],
                                   m1[:, qsub * 128:(qsub + 1) * 128],
                                   self.ident[:])
                    mk.v.reduce_max(mxc[:, qsub:qsub + 1], ps[0:128, 0:128],
                                    axis=AX.X)
                ps = self.psb[PR]
                mk.t.transpose(ps[0:4, 0:128], mxc[:, 0:4], self.ident[:])
                mk.v.tensor_copy(mxe[0:4, 0:128], ps[0:4, 0:128])
                mk.dma("g", bass.AP(mxr, 0, [[512, 1], [128, 4], [1, 128]]),
                       bass.AP(mxe, 0, [[128, 4], [1, 128]]))
                mk.t.matmul(ps[0:128, 0:512], self.ones_row1[0:1, 0:128],
                            mxr[0:1, 0:512], start=True, stop=True)
                for r in rlist:
                    self.tt(ssb[:, r * 512:(r + 1) * 512],
                            ssb[:, r * 512:(r + 1) * 512], ps[0:128, 0:512],
                            ALU.subtract)
                self.act(bass.AP(ssb, r0_ * 512, [[8 * 512, 128], [1, nr_ * 512]]),
                         bass.AP(ssb, r0_ * 512, [[8 * 512, 128], [1, nr_ * 512]]),
                         AF.Exp)
                pv = self.psb[PV]
                for idx, r in enumerate(rlist):
                    ti = (4 * (w - 1) + r) % 8
                    mk.t.matmul(pv[0:65, 0:512],
                                bass.AP(vtm, ti * 520 + h * 65,
                                        [[8 * 520, 128], [1, 65]]),
                                ssb[:, r * 512:(r + 1) * 512],
                                start=(idx == 0), stop=(idx == nr_ - 1))
                if hb == 0:
                    mk.v.tensor_copy(obuf[0:64, csl], pv[0:64, 0:512])
                else:
                    mk.v.tensor_copy(sq[0:64, 0:512], pv[0:64, 0:512])
                    mk.dma("g", obuf[64:128, csl], sq[0:64, 0:512])
                mk.s.copy(m1[64:65, 0:512], pv[64:65, 0:512])
                mk.dma("g", dnr[h:h + 1, :], m1[64:65, 0:512])
            mk.v.reciprocal(dnr[:], dnr[:])
            for mc in range(4):
                csl = slice(mc * 512, (mc + 1) * 512)
                pr = self.psb[PR]
                mk.t.matmul(pr[0:128, 0:512],
                            self.esel[:, mc * 128:(mc + 1) * 128],
                            dnr[0:8, :], start=True, stop=True)
                self.tt(obuf[:, csl], obuf[:, csl], pr[0:128, 0:512])
            import os as _os
            if w == 0 and l == 0 and _os.environ.get("FFN_DBG"):
                mk.dma("s", self.sc_dbg.ap()[0:128, :], qw[:])
                mk.dma("s", self.sc_dbg.ap()[128:256, :], kcur[:])
                mk.dma("s", self.sc_dbg.ap()[256:384, :], obuf[:])
                mk.dma("s", self.sc_dbg.ap()[384:512, 0:1408], self.emask[:])
                mk.dma("s", self.sc_dbg.ap()[512:640, :], tabs[:])
                mk.dma("s", self.sc_dbg.ap()[640:648, 0:512], dnr[:])
                mk.dma("s", self.sc_dbg.ap()[648:656, 0:512], nrq[:])
                mk.dma("s", self.sc_dbg.ap()[656:664, 0:512], nrk[:])
                mk.dma("s", self.sc_dbg.ap()[792:920, :],
                       bass.AP(ssb, 2048, [[8 * 512, 128], [1, 2048]]))
                mk.dma("s", self.sc_dbg.ap()[920:1024, 0:520],
                       bass.AP(vtm, 0, [[8 * 520, 104], [1, 520]]))
            mk.dma("y", wsl[0][:, 0:512], self.wot.ap()[l * 8 + 0])
            for m in range(8):
                if w < 3:
                    mk.v.tensor_copy(bstash[:, m * 3 + w:m * 3 + w + 1],
                                     xf[:, m * L + t0 + 511:m * L + t0 + 512])
                buf = wsl[m % 2]
                if m + 1 < 8:
                    mk.dma("y", wsl[(m + 1) % 2][:, 0:512],
                           self.wot.ap()[l * 8 + m + 1])
                ps = self.psb[PB]
                for kc in range(4):
                    mk.t.matmul(ps[0:128, 0:512], buf[:, kc * 128:(kc + 1) * 128],
                                obuf[:, kc * 512:(kc + 1) * 512],
                                start=(kc == 0), stop=(kc == 3))
                self.tt(xf[:, m * L + t0:m * L + t0 + 512],
                        xf[:, m * L + t0:m * L + t0 + 512], ps[0:128, 0:512],
                        ALU.add)

    def ffn_mlp(self, l, xf):
        mk = self.mk
        ar = self.ar
        lngb = ar("lngb", [128, 16])
        self.ffn_load_cols(lngb, self.ln_g.ap()[l], 8, off=0)
        self.ffn_load_cols(lngb, self.ln_b.ap()[l], 8, off=8)
        b1row = ar("b1row", [1, 2 * HID])
        b2row = ar("b2row", [1, D])
        mk.dma("y", b1row[:], self.b1.ap()[l])
        mk.dma("y", b2row[:], self.b2.ap()[l])
        xln = ar("xlnb", [128, 8 * 512])
        h1 = ar("h1b", [128, 22 * 512])
        gel = ar("gelb", [128, 512])
        sqm = ar("sqmb", [128, 512])
        rows = ar("rowsb", [1, 3 * 512])
        wsl = [ar("wslm0", [128, 1024]), ar("wslm1", [128, 1024])]
        w2s = [ar("w2sb0", [128, 1408]), ar("w2sb1", [128, 1408])]
        mstash = ar("mstash", [128, 8 * 3])
        x0m = ar("x0m", [128, 4])
        PB0, PB1, PRA, PRB, PRC = 0, 1, 2, 3, 4
        hch = [(i * 128, 128) for i in range(21)] + [(21 * 128, HID - 21 * 128)]

        for tg in range(4):
            t0 = tg * 512
            for i, dc in enumerate(range(4, 8)):
                if tg == 0:
                    mk.g.memset(x0m[:, i:i + 1], 0.0)
                else:
                    mk.v.tensor_copy(x0m[:, i:i + 1],
                                     mstash[:, dc * 3 + tg - 1:dc * 3 + tg])

            def sx(dc, fr0, frw):
                if dc < 4:
                    return xf[:, dc * L + t0 + fr0: dc * L + t0 + fr0 + frw]
                return xf[:, dc * L + t0 + fr0 - 1: dc * L + t0 + fr0 - 1 + frw]

            pra, prb = self.psb[PRA], self.psb[PRB]
            for kc in range(8):
                dst = xln[:, kc * 512:(kc + 1) * 512]
                if kc < 4:
                    mk.v.tensor_copy(dst, xf[:, kc * L + t0: kc * L + t0 + 512])
                else:
                    mk.v.tensor_copy(xln[:, kc * 512 + 1:(kc + 1) * 512],
                                     xf[:, kc * L + t0: kc * L + t0 + 511])
                    mk.v.tensor_copy(xln[:, kc * 512:kc * 512 + 1],
                                     x0m[:, kc - 4:kc - 3])
            for kc in range(8):
                sl = xln[:, kc * 512:(kc + 1) * 512]
                mk.t.matmul(pra[0:1, 0:512], self.ones_col[:], sl,
                            start=(kc == 0), stop=(kc == 7))
                self.act(sqm[:], sl, AF.Square)
                mk.t.matmul(prb[0:1, 0:512], self.ones_col[:], sqm[:],
                            start=(kc == 0), stop=(kc == 7))
            mu = rows[0:1, 0:512]
            ivr = rows[0:1, 512:1024]
            t1 = rows[0:1, 1024:1536]
            self.ts(mu, pra[0:1, 0:512], 1.0 / D)
            self.act(t1, mu, AF.Square)
            self.stt(ivr, prb[0:1, 0:512], 1.0 / D, t1, op0=ALU.mult,
                     op1=ALU.subtract)
            self.act(ivr, ivr, AF.Ln, bias=1e-5)
            self.act(ivr, ivr, AF.Exp, scale=-0.5)
            prc = self.psb[PRC]
            mk.t.matmul(prc[0:128, 0:512], self.ones_row1[0:1, 0:128], mu,
                        start=True, stop=True)
            pra2 = self.psb[PRA]
            mk.t.matmul(pra2[0:128, 0:512], self.ones_row1[0:1, 0:128], ivr,
                        start=True, stop=True)
            for kc in range(8):
                sl = xln[:, kc * 512:(kc + 1) * 512]
                self.tt(sl, sl, prc[0:128, 0:512], ALU.subtract)
                self.tt(sl, sl, pra2[0:128, 0:512])
                self.ts(sl, sl, lngb[:, kc:kc + 1], lngb[:, 8 + kc:9 + kc],
                        op0=ALU.mult, op1=ALU.add)
            slabseq = [(mi, half) for mi in range(22) for half in (0, 1)]
            mk.dma("y", wsl[0][:], self.w1t.ap()[l * 44 + 0 * 22 + 0])
            for idx, (mi, half) in enumerate(slabseq):
                buf = wsl[idx % 2]
                if idx + 1 < len(slabseq):
                    nmi, nhalf = slabseq[idx + 1]
                    mk.dma("y", wsl[(idx + 1) % 2][:],
                           self.w1t.ap()[l * 44 + nhalf * 22 + nmi])
                m0, mw = hch[mi]
                col0 = m0 + half * HID
                ps = self.psb[PB0 if half == 0 else PB1]
                for kc in range(8):
                    mk.t.matmul(ps[0:mw, 0:512],
                                buf[:, kc * 128:kc * 128 + mw],
                                xln[:, kc * 512:(kc + 1) * 512],
                                start=(kc == 0), stop=False)
                mk.t.matmul(ps[0:mw, 0:512],
                            b1row[0:1, col0:col0 + mw],
                            self.ones_row1[0:1, 0:512],
                            start=False, stop=True)
                if half == 1:
                    psA, psB = self.psb[PB0], self.psb[PB1]
                    self.act(gel[0:mw, :], psB[0:mw, 0:512], AF.Gelu)
                    self.tt(h1[0:mw, mi * 512:(mi + 1) * 512],
                            psA[0:mw, 0:512], gel[0:mw, :])
            def w2ap(m, hh):
                return self.w2t.ap()[l * 8 + m][:, hh * 1408:(hh + 1) * 1408]
            mk.dma("y", w2s[0][:], w2ap(0, 0))
            for m in range(8):
                if tg < 3:
                    mk.v.tensor_copy(mstash[:, m * 3 + tg:m * 3 + tg + 1],
                                     xf[:, m * L + t0 + 511:m * L + t0 + 512])
                ps = self.psb[PB0 if m % 2 == 0 else PB1]
                for seg in (0, 1):
                    sidx = m * 2 + seg
                    buf = w2s[sidx % 2]
                    if sidx + 1 < 16:
                        nm, nh = divmod(sidx + 1, 2)
                        mk.dma("y", w2s[(sidx + 1) % 2][:], w2ap(nm, nh))
                    for ki in range(seg * 11, seg * 11 + 11):
                        k0, kwd = hch[ki]
                        c0 = (ki - seg * 11) * 128
                        mk.t.matmul(ps[0:128, 0:512],
                                    buf[0:kwd, c0:c0 + 128],
                                    h1[0:kwd, ki * 512:(ki + 1) * 512],
                                    start=(ki == 0), stop=False)
                mk.t.matmul(ps[0:128, 0:512], b2row[0:1, m * 128:(m + 1) * 128],
                            self.ones_row1[0:1, 0:512], start=False, stop=True)
                self.tt(xf[:, m * L + t0:m * L + t0 + 512],
                        xf[:, m * L + t0:m * L + t0 + 512], ps[0:128, 0:512],
                        ALU.add)



_PROG_CACHE = {}

_HCH = [(i * 128, 128) for i in range(21)] + [(21 * 128, HID - 21 * 128)]


def _tile_w1(w1):
    out = np.zeros((2, 44, 128, 1024), np.float32)
    for l in range(2):
        for half in range(2):
            for mi, (m0, mw) in enumerate(_HCH):
                col0 = half * HID + m0
                blk = w1[l, :, col0:col0 + mw].reshape(8, 128, mw)
                out[l, half * 22 + mi].reshape(128, 8, 128)[:, :, :mw] = \
                    blk.transpose(1, 0, 2)
    return out.reshape(88, 128, 1024)


def _tile_wqk(wqkv):
    out = np.zeros((2, 2, 4, 128, 1024), np.float32)
    for l in range(2):
        for s in range(2):
            for mc in range(4):
                col0 = s * FI + mc * 128
                blk = wqkv[l, :, col0:col0 + 128].reshape(8, 128, 128)
                out[l, s, mc].reshape(128, 8, 128)[:] = blk.transpose(1, 0, 2)
    return out.reshape(16, 128, 1024)


def _tile_wv(wqkv):
    out = np.zeros((2, 8, 128, 512), np.float32)
    for l in range(2):
        for kc in range(8):
            out[l, kc] = wqkv[l, kc * 128:(kc + 1) * 128, 2 * FI:3 * FI]
    return out.reshape(16, 128, 512)


def _tile_wot(wo):
    out = np.zeros((2, 8, 128, 512), np.float32)
    for l in range(2):
        for m in range(8):
            blk = wo[l, :, m * 128:(m + 1) * 128].reshape(4, 128, 128)
            out[l, m].reshape(128, 4, 128)[:] = blk.transpose(1, 0, 2)
    return out.reshape(16, 128, 512)


def _tile_w2(w2):
    out = np.zeros((2, 8, 128, 2816), np.float32)
    for l in range(2):
        for m in range(8):
            for ki, (k0, kwd) in enumerate(_HCH):
                out[l, m, :kwd, ki * 128:(ki + 1) * 128] = \
                    w2[l, k0:k0 + kwd, m * 128:(m + 1) * 128]
    return out.reshape(16, 128, 2816)


def _build_program():
    if "k" not in _PROG_CACHE:
        k = Kern()
        k.build_all()
        _PROG_CACHE["k"] = k
    return _PROG_CACHE["k"]


def kernel(L0_attn_x, L0_attn_W_in, L0_attn_dw_weight, L0_attn_dw_bias,
           L0_attn_W_out, L0_attn_skip, L0_attn_dt_bias, L0_attn_theta_raw,
           L0_attn_k_raw, L0_ffn_wqkv, L0_ffn_q_scale, L0_ffn_k_scale,
           L0_ffn_wo, L0_ffn_ln_g, L0_ffn_ln_b, L0_ffn_w1, L0_ffn_b1,
           L0_ffn_w2, L0_ffn_b2):
    k = _build_program()
    f32 = np.float32
    common = {
        "WinT": np.ascontiguousarray(L0_attn_W_in.T).astype(f32, copy=False),
        "dw_w": np.ascontiguousarray(L0_attn_dw_weight).astype(f32, copy=False),
        "dw_b": np.ascontiguousarray(L0_attn_dw_bias[None, :]).astype(f32, copy=False),
        "WoutT": np.ascontiguousarray(L0_attn_W_out.T).astype(f32, copy=False),
        "skip": np.ascontiguousarray(L0_attn_skip[None, :]).astype(f32, copy=False),
        "dt_bias": np.ascontiguousarray(L0_attn_dt_bias[None, :]).astype(f32, copy=False),
        "thetaT": np.ascontiguousarray(L0_attn_theta_raw.T).astype(f32, copy=False),
        "krawT": np.ascontiguousarray(L0_attn_k_raw.T).astype(f32, copy=False),
        "wqkt": _tile_wqk(np.asarray(L0_ffn_wqkv, dtype=f32)),
        "wvt": _tile_wv(np.asarray(L0_ffn_wqkv, dtype=f32)),
        "qscale": np.ascontiguousarray(L0_ffn_q_scale[:, None, :]).astype(f32, copy=False),
        "kscale": np.ascontiguousarray(L0_ffn_k_scale[:, None, :]).astype(f32, copy=False),
        "wot": _tile_wot(np.asarray(L0_ffn_wo, dtype=f32)),
        "ln_g": np.ascontiguousarray(L0_ffn_ln_g[:, None, :]).astype(f32, copy=False),
        "ln_b": np.ascontiguousarray(L0_ffn_ln_b[:, None, :]).astype(f32, copy=False),
        "w1t": _tile_w1(np.asarray(L0_ffn_w1, dtype=f32)),
        "b1": np.ascontiguousarray(L0_ffn_b1[:, None, :]).astype(f32, copy=False),
        "w2t": _tile_w2(np.asarray(L0_ffn_w2, dtype=f32)),
        "b2": np.ascontiguousarray(L0_ffn_b2[:, None, :]).astype(f32, copy=False),
    }
    in_maps = []
    for b in range(8):
        m = dict(common)
        m["x"] = np.ascontiguousarray(L0_attn_x[b]).astype(f32, copy=False)
        in_maps.append(m)
    res = run_bass_kernel_spmd(k.nc, in_maps, core_ids=list(range(8)))
    return np.stack([res.results[b]["out"] for b in range(8)], axis=0)


# version 09 (final)
# speedup vs baseline: 1.3044x; 1.0252x over previous
import numpy as np

import concourse.bass as bass
import concourse.mybir as mybir
from concourse.bass_utils import run_bass_kernel_spmd

F32 = mybir.dt.float32
I32 = mybir.dt.int32
AF = mybir.ActivationFunctionType
ALU = mybir.AluOpType
AX = mybir.AxisListType

L = 2048
D = 1024
DI = 2048
NH = 32
NS = 128
PD = 64
TC = 128
NCH = L // TC
IN_DIM = 2 * DI + NH + NH * 4 * NS
HID = 2730
FH = 8
FD = 64
FI = 512
FW = 512

PI = float(np.pi)
PI32 = float(np.float32(np.pi))
NEG_BIG = -1e9



_RMW_METHODS = {"select", "copy_predicated"}
_DMA_QUEUE_SEM = {"t": "dmt", "v": "dmv", "s": "dms", "g": "dmg", "y": "dmy"}


def _ap_of(x):
    return x if isinstance(x, bass.AP) else None


def _tname(ap):
    return ap.tensor.name


class Op:
    __slots__ = ("engine", "method", "args", "kwargs", "outs", "ins",
                 "inc_token", "waits", "drain", "pos", "is_dma", "dma_q",
                 "_inc_val", "_dma_val", "seq", "dma_slot", "emitted")

    def __init__(self, engine, method, args, kwargs, outs, ins, is_dma=False,
                 dma_q=None):
        self.engine = engine
        self.method = method
        self.args = args
        self.kwargs = kwargs
        self.outs = outs
        self.ins = ins
        self.inc_token = False
        self.waits = []
        self.drain = False
        self.pos = -1
        self.is_dma = is_dma
        self.dma_q = dma_q
        self.emitted = False


class Eng:
    def __init__(self, mk, engine):
        self._mk = mk
        self._e = engine

    def __getattr__(self, method):
        def call(*args, **kwargs):
            return self._mk._record(self._e, method, args, kwargs)
        return call


class MK:
    ENGS = ("t", "v", "s", "g", "y")
    ENG_ATTR = {"t": "tensor", "v": "vector", "s": "scalar", "g": "gpsimd",
                "y": "sync"}

    def __init__(self, nc):
        self.nc = nc
        self.eng_ops = {e: [] for e in self.ENGS}
        self.writer = {}
        self.readers = {}
        self._seq = 0
        self._dma_count = {}
        self.dma_slots = {}
        self._rec_hw = {}
        self._drain_cov = {}
        self.NSLOT = 8
        self.t = Eng(self, "t")
        self.v = Eng(self, "v")
        self.s = Eng(self, "s")
        self.g = Eng(self, "g")
        self.y = Eng(self, "y")

    def _token_of(self, op):
        return ("dma", op.dma_q, op) if op.is_dma else ("eng", op.engine, op)

    def _add_dep(self, op, tok):
        if tok[0] == "eng" and tok[1] == op.engine:
            e = op.engine
            cov = self._drain_cov.get(e, -1)
            if tok[2].seq > cov:
                op.drain = True
                prev = self.eng_ops[e][-1] if self.eng_ops[e] else None
                self._drain_cov[e] = prev.seq if prev is not None else -1
            return
        hw = self._rec_hw.setdefault(op.engine, {})
        key = self._tok_key(tok)
        prev = hw.get(key)
        if prev is not None and prev.seq >= tok[2].seq:
            return
        hw[key] = tok[2]
        if tok[0] == "eng":
            tok[2].inc_token = True
        for t2 in op.waits:
            if t2[2] is tok[2]:
                return
        op.waits.append(tok)

    def _classify(self, method, args, kwargs):
        pos_aps = [_ap_of(a) for a in args]
        pos_aps = [a for a in pos_aps if a is not None]
        kw_out = []
        kw_in = []
        for k, a in kwargs.items():
            ap = _ap_of(a)
            if ap is None:
                continue
            if k in ("out",):
                kw_out.append(ap)
            elif k == "accum_out":
                kw_out.append(ap)
            else:
                kw_in.append(ap)
        if method in ("memset", "memzero", "iota"):
            outs = (kw_out + pos_aps)[:1]
            ins = []
        elif method in _RMW_METHODS:
            allaps = kw_out + pos_aps + kw_in
            outs = allaps[:1]
            ins = allaps
        else:
            if kw_out:
                outs = kw_out
                ins = pos_aps + kw_in
            else:
                outs = pos_aps[:1] + [a for a in kw_out]
                ins = pos_aps[1:] + kw_in
        return outs, ins

    def _tok_key(self, tok):
        if tok[0] == "eng":
            return ("e", tok[1])
        return ("d", tok[2].dma_q, tok[2].dma_slot)

    @staticmethod
    def _prange(ap):
        pitch, cnt = ap.ap[0]
        if pitch == 0:
            return (0, cnt)
        return (ap.tensor.base_partition + ap.offset // pitch, cnt)

    def _check_partitions(self, method, outs, ins):
        if method in ("matmul",):
            if len(ins) >= 2:
                a, b = self._prange(ins[0]), self._prange(ins[1])
                assert a == b, f"matmul operand partitions differ: {a} vs {b}"
            return
        if method in ("transpose", "iota", "memset", "memzero",
                      "partition_all_reduce", "partition_broadcast"):
            return
        rr = [self._prange(ap) for ap in outs + ins]
        assert all(r == rr[0] for r in rr), \
            f"{method}: operand partition ranges differ: {rr}"

    def _record(self, engine, method, args, kwargs):
        is_dma = method in ("dma_start", "dma_start_transpose")
        outs, ins = self._classify(method, args, kwargs)
        if not is_dma:
            self._check_partitions(method, outs, ins)
        self._last_real = getattr(self, "_last_real", {})
        op = Op(engine, method, args, kwargs, outs, ins, is_dma,
                _DMA_QUEUE_SEM[engine] if is_dma else None)
        op.seq = self._seq
        self._seq += 1
        if is_dma:
            q = op.dma_q
            c = self._dma_count.get(q, 0)
            self._dma_count[q] = c + 1
            op.dma_slot = c % self.NSLOT
            self.dma_slots.setdefault((q, op.dma_slot), []).append(op)
        for ap in op.ins:
            n = _tname(ap)
            for w in self.writer.get(n, {}).values():
                self._add_dep(op, w)
            if n.startswith("pb"):
                for r in self.readers.get(n, {}).values():
                    self._add_dep(op, r)
        for ap in op.outs:
            n = _tname(ap)
            for w in self.writer.get(n, {}).values():
                self._add_dep(op, w)
            for r in self.readers.get(n, {}).values():
                self._add_dep(op, r)
        tok = self._token_of(op)
        key = self._tok_key(tok)
        for ap in op.outs:
            n = _tname(ap)
            self.writer.setdefault(n, {})[key] = tok
            self.readers[n] = {}
        for ap in op.ins:
            self.readers.setdefault(_tname(ap), {})[key] = tok
        self.eng_ops[engine].append(op)
        return op

    def dma(self, q, out, in_, **kw):
        return getattr(self, q).dma_start(out=out, in_=in_, **kw)

    def barrier(self):
        last = {}
        for e in self.ENGS:
            if self.eng_ops[e]:
                last[e] = self.eng_ops[e][-1]
        dma_last = {k: lst[-1] for k, lst in self.dma_slots.items() if lst}
        for e in self.ENGS:
            nop = Op(e, "nop", (), {}, [], [])
            nop.seq = self._seq
            self._seq += 1
            for e2, lop in last.items():
                if e2 == e:
                    if not lop.is_dma:
                        nop.drain = True
                        continue
                if lop.emitted and not lop.is_dma:
                    continue
                self._add_dep(nop, self._token_of(lop))
            for (q, sl), dop in dma_last.items():
                self._add_dep(nop, ("dma", q, dop))
            self.eng_ops[e].append(nop)
        self.writer = {}
        self.readers = {}

    def _ensure_sems(self):
        if not hasattr(self, "_eng_sems"):
            self._eng_sems = {e: self.nc.alloc_semaphore(name=f"SE_{e}")
                              for e in self.ENGS}
            self._dma_sems = {}
            self._inc_no = {e: 0 for e in self.ENGS}
            self._emitted = {e: 0 for e in self.ENGS}
            self._wait_hw = {}

    def flush(self):
        nc = self.nc
        self._ensure_sems()
        eng_sems, dma_sems = self._eng_sems, self._dma_sems
        for e in self.ENGS:
            for op in self.eng_ops[e][self._emitted[e]:]:
                if op.is_dma:
                    key = (op.dma_q, op.dma_slot)
                    if key not in dma_sems:
                        dma_sems[key] = nc.alloc_semaphore(
                            name=f"SD_{key[0]}_{key[1]}")
                    lst = self.dma_slots[key]
                    op._dma_val = 16 * (lst.index(op) + 1)
                elif op.inc_token:
                    self._inc_no[e] += 1
                    op._inc_val = self._inc_no[e]

        mkself = self

        def emit_stream(engobj, e, start, end):
            for op in mkself.eng_ops[e][start:end]:
                for tok in op.waits:
                    if tok[0] == "eng":
                        engobj.wait_ge(eng_sems[tok[1]], tok[2]._inc_val)
                    else:
                        d = tok[2]
                        engobj.wait_ge(dma_sems[(d.dma_q, d.dma_slot)],
                                       d._dma_val)
                if op.drain:
                    engobj.drain()
                if op.is_dma and op._dma_val > 16:
                    engobj.wait_ge(dma_sems[(op.dma_q, op.dma_slot)],
                                   op._dma_val - 16)
                if op.method == "nop":
                    inst = engobj.nop()
                else:
                    inst = getattr(engobj, op.method)(*op.args, **op.kwargs)
                if op.is_dma:
                    inst.then_inc(dma_sems[(op.dma_q, op.dma_slot)], 16)
                elif op.inc_token:
                    inst.then_inc(eng_sems[e], 1)

        spans = {e: (self._emitted[e], len(self.eng_ops[e])) for e in self.ENGS}
        if all(a == b for a, b in spans.values()):
            return
        with nc.Block() as blk:
            for e in self.ENGS:
                a, b = spans[e]
                if a == b:
                    continue
                getattr(blk, self.ENG_ATTR[e])(
                    lambda engobj, e=e, a=a, b=b: emit_stream(engobj, e, a, b))
        for e in self.ENGS:
            for op in self.eng_ops[e][self._emitted[e]:spans[e][1]]:
                op.emitted = True
            self._emitted[e] = spans[e][1]

    def realize(self):
        self.barrier()
        self.flush()

    def n_ops(self):
        return sum(len(v) for v in self.eng_ops.values())



def cdiv(a, b):
    return -(-a // b)


class Builder:
    SBUF_BYTES = 207 * 1024

    def __init__(self):
        self.nc = bass.Bass("TRN2", target_bir_lowering=False)
        self.mk = MK(self.nc)
        self._dram = {}
        self._persist_bytes = 0
        self._arena_handle = None
        self._guards = []
        self._arena_off = 0
        self._arena_size = 0
        self._arena_peak = 0
        self._uid = 0

    def uid(self):
        self._uid += 1
        return self._uid

    def dram(self, name, shape, dtype=F32, kind="Internal"):
        h = self.nc.dram_tensor(name, list(shape), dtype, kind=kind)
        self._dram[name] = h
        return h

    def sb(self, name, shape, dtype=F32):
        h = self.nc.alloc_sbuf_tensor(name, list(shape), dtype)
        per_part = int(np.prod(shape[1:])) * mybir.dt.size(dtype)
        self._persist_bytes += ((per_part + 31) // 32) * 32
        return h

    def finish_persist(self):
        pass

    def ar(self, name, shape, dtype=F32):
        g = self.nc.sbuf_tensor(f"{name}_{self.uid()}", list(shape), dtype)
        h = g.__enter__()
        self._guards.append(g)
        per_part = int(np.prod(shape[1:])) * mybir.dt.size(dtype)
        self._arena_off += ((per_part + 31) // 32) * 32
        self._arena_peak = max(self._arena_peak, self._arena_off)
        return h

    def arena_mark(self):
        return len(self._guards)

    def arena_reset(self, mark=0):
        self.mk.barrier()
        self.mk.flush()
        while len(self._guards) > mark:
            self._guards.pop().__exit__(None, None, None)

    def psum(self, name, banks=1, part=128):
        return self.nc.alloc_psum_tensor(name, [part, 512 * banks], F32)

    def constcol(self, val):
        val = float(val)
        if not hasattr(self, "_cpool"):
            self._cpool = self.sb("CPOOL", [128, 8])
            self._cvals = {}
        if val not in self._cvals:
            i = len(self._cvals)
            assert i < 8
            self._cvals[val] = i
            self.mk.g.memset(self._cpool[:, i:i + 1], val)
        i = self._cvals[val]
        return self._cpool[:, i:i + 1]



class Kern(Builder):

    def __init__(self):
        super().__init__()
        self.declare_io()
        self.declare_static()
        self.constcol(2.0)
        self.psb = [self.psum(f"pb{i}") for i in range(8)]
        self.finish_persist()

    def declare_io(self):
        d = self.dram
        self.x_in = d("x", [L, D], kind="ExternalInput")
        self.WinT = d("WinT", [D, IN_DIM], kind="ExternalInput")
        self.dw_w = d("dw_w", [DI, 4], kind="ExternalInput")
        self.dw_b = d("dw_b", [1, DI], kind="ExternalInput")
        self.wott = d("wott", [8, 128, 2048], kind="ExternalInput")
        self.skip_in = d("skip", [1, DI], kind="ExternalInput")
        self.dt_bias = d("dt_bias", [1, NH], kind="ExternalInput")
        self.thetaT = d("thetaT", [NS, NH], kind="ExternalInput")
        self.krawT = d("krawT", [NS, NH], kind="ExternalInput")
        self.wqkt = d("wqkt", [16, 128, 1024], kind="ExternalInput")
        self.wvt = d("wvt", [16, 128, 512], kind="ExternalInput")
        self.wot = d("wot", [16, 128, 512], kind="ExternalInput")
        self.qscale = d("qscale", [2, 1, FD], kind="ExternalInput")
        self.kscale = d("kscale", [2, 1, FD], kind="ExternalInput")
        self.ln_g = d("ln_g", [2, 1, D], kind="ExternalInput")
        self.ln_b = d("ln_b", [2, 1, D], kind="ExternalInput")
        self.w1t = d("w1t", [88, 128, 1024], kind="ExternalInput")
        self.b1 = d("b1", [2, 1, 2 * HID], kind="ExternalInput")
        self.w2t = d("w2t", [16, 128, 2816], kind="ExternalInput")
        self.b2 = d("b2", [2, 1, D], kind="ExternalInput")
        self.out = d("out", [L, D], kind="ExternalOutput")
        self.sc_bc = d("sc_bc", [NH * 4 * NS, L])
        self.sc_u = d("sc_u", [DI, L])
        self.sc_ut = d("sc_ut", [NH * L, PD])
        self.sc_z = d("sc_z", [DI, L])
        self.sc_yg = d("sc_yg", [DI, L])
        self.sc_s = d("sc_s", [D, L])
        self.sc_g = d("sc_g", [D, L])
        self.sc_xn = d("sc_xn", [D, L])
        self.sc_dbg = d("sc_dbg", [1024, 2048])

    def declare_static(self):
        sb = self.sb
        self.ones_col = sb("ones_col", [128, 1])
        self.ones_row1 = sb("ones_row1", [1, 512])
        self.ident = sb("ident", [128, 128])
        self.tri_incl = sb("tri_incl", [128, 128])
        self.hpar = sb("hpar", [NS, 16 * NH])
        self.dwW = sb("dwW", [128, 16 * 4])
        self.dwB = sb("dwB", [128, 16])
        self.skip_c = sb("skip_c", [128, 16])
        self.dtb_col = sb("dtb_col", [NH, 1])

    HP = {name: i for i, name in enumerate(
        ["A", "k2", "sig", "rs", "mos", "mhy",
         "irs_o", "irs_h2", "negA", "k2n", "mos_h", "mhy_h", "kf"])}

    def hp(self, name):
        i = self.HP[name]
        return self.hpar[:, i:i + 1]

    def ts(self, out, in0, s1, s2=None, op0=ALU.mult, op1=ALU.add, eng="v"):
        e = getattr(self.mk, eng)
        if s2 is None:
            return e.tensor_scalar(out, in0, s1, None, op0)
        return e.tensor_scalar(out, in0, s1, s2, op0, op1)

    def stt(self, out, in0, s, in1, op0=ALU.mult, op1=ALU.add, eng="v"):
        e = getattr(self.mk, eng)
        return e.scalar_tensor_tensor(out, in0, s, in1, op0, op1)

    def tt(self, out, a, b, op=ALU.mult, eng="v"):
        e = getattr(self.mk, eng)
        return e.tensor_tensor(out, a, b, op)

    def act(self, out, in_, func, bias=0.0, scale=1.0, accum_out=None):
        kw = {}
        if accum_out is not None:
            kw["accum_out"] = accum_out
        if isinstance(bias, float) and bias not in (0.0, 1.0):
            p = out.ap[0][1] if isinstance(out, bass.AP) else 128
            bias = self.constcol(bias)[0:p, 0:1]
        return self.mk.s.activation(out=out, in_=in_, func=func, bias=bias,
                                    scale=scale, **kw)

    def bcast_free(self, colap, n):
        t = colap.tensor
        psz = colap.ap[0]
        return bass.AP(t, colap.offset, [list(psz), [0, n]])

    def phase_alloc0(self):
        self.stripe01 = self.ar("stripe01", [128, L])
        self.dt_fm = self.ar("dt_fm", [NH, L])
        self.rn_sq = self.ar("rn_sq", [64, L])
        self.rn_tm = self.ar("rn_tm", [128, NCH * 64])
        g = self.mk.g
        g.memset(self.stripe01[:], 1.0)
        g.memset(bass.AP(self.stripe01, 0, [[L, 128], [TC, NCH]]), 0.0)
        return self.arena_mark()

    def build_all(self):
        self.phase_prep()
        self.arena_reset()
        mA = self.phase_alloc0()
        self.phase_ln()
        mB = mA + 1
        self.arena_reset(mB)
        self.phase_dt()
        self.phase_bc_norms()
        self.phase_norm_finish()
        self.arena_reset(mB)
        self.phase_z_xc_u()
        self.arena_reset(mA)
        self.phase_ssm_core()
        self.arena_reset(0)
        self.phase_wout()
        self.arena_reset(0)
        self.phase_ffn()
        self.arena_reset(0)
        self.phase_final()
        self.mk.realize()

    def hpc(self, name, h):
        i = self.HP[name]
        return self.hpar[:, i * NH + h:i * NH + h + 1]

    def hpall(self, name):
        i = self.HP[name]
        return self.hpar[:, i * NH:(i + 1) * NH]

    def phase_prep(self):
        mk, nc = self.mk, self.nc
        g = mk.g
        g.memset(self.ones_col[:], 1.0)
        g.memset(self.ones_row1[:], 1.0)
        g.memset(self.tri_incl[:], 1.0)
        g.affine_select(self.ident[:], self.tri_incl[:], [[1, 128]],
                        ALU.is_ge, 0.0, base=0, channel_multiplier=-1)
        g.affine_select(self.ident[:], self.ident[:], [[-1, 128]],
                        ALU.is_ge, 0.0, base=0, channel_multiplier=1)
        g.affine_select(self.tri_incl[:], self.tri_incl[:], [[1, 128]],
                        ALU.is_ge, 0.0, base=0, channel_multiplier=-1)

        th_raw = self.ar("th_raw", [NS, NH])
        k_raw = self.ar("k_raw", [NS, NH])
        mk.dma("y", th_raw[:], self.thetaT.ap())
        mk.dma("y", k_raw[:], self.krawT.ap())
        A, k2, sig, rs = (self.hpall(n) for n in ("A", "k2", "sig", "rs"))
        def sigmoid_chain(dst, src_ap):
            self.act(dst, src_ap, AF.Exp, scale=-1.0)
            self.act(dst, dst, AF.Ln, bias=1.0)
            self.act(dst, dst, AF.Exp, scale=-1.0)
        sigmoid_chain(A, th_raw[:])
        self.act(A, A, AF.Square, scale=PI)
        sigmoid_chain(k2, k_raw[:])
        self.ts(k2, k2, 0.25)
        self.act(sig, k2, AF.Square)
        self.tt(sig, sig, A, ALU.subtract)
        ab_t = self.ar("abs_sig", [NS, NH])
        self.ts(ab_t[:], sig, -1.0)
        self.tt(ab_t[:], ab_t[:], sig, ALU.max)
        self.ts(ab_t[:], ab_t[:], 1e-30, op0=ALU.max)
        self.act(rs, ab_t[:], AF.Ln)
        self.act(rs, rs, AF.Exp, scale=0.5)
        mos, mhy = self.hpall("mos"), self.hpall("mhy")
        self.ts(mos, sig, 0.0, op0=ALU.is_lt)
        self.ts(mhy, mos, -1.0, 1.0, op0=ALU.mult, op1=ALU.add)
        irs = self.ar("irs", [NS, NH])
        mk.v.reciprocal(irs[:], rs)
        self.tt(self.hpall("irs_o"), mos, irs[:], ALU.mult)
        t2 = self.ar("t2", [NS, NH])
        self.ts(t2[:], irs[:], 0.5)
        self.tt(self.hpall("irs_h2"), mhy, t2[:], ALU.mult)
        self.ts(self.hpall("negA"), A, -1.0)
        self.ts(self.hpall("k2n"), k2, -1.0)
        self.ts(self.hpall("mos_h"), mos, 0.5)
        self.ts(self.hpall("mhy_h"), mhy, 0.5)
        self.ts(self.hpall("kf"), k2, 2.0)

        for c in range(16):
            mk.dma("y", self.dwW[:, c * 4:(c + 1) * 4],
                   self.dw_w.ap()[c * 128:(c + 1) * 128, :])
        self.load_cols16(self.dwB, self.dw_b)
        self.load_cols16(self.skip_c, self.skip_in)
        mk.dma("y", self.dtb_col[:, 0:1],
               bass.AP(self.dt_bias, 0, [[1, NH], [1, 1]]))

    def load_cols16(self, dst, dram_row):
        mk = self.mk
        tmp = self.ar(f"lc16_{self.uid()}", [16, 128])
        mk.dma("y", tmp[:], dram_row.ap().rearrange("o (c p) -> (o c) p", p=128))
        ps = self.psb[7]
        mk.t.transpose(ps[0:128, 0:16], tmp[:], self.ident[0:16, 0:16])
        mk.v.tensor_copy(dst[:, 0:16], ps[0:128, 0:16])

    def trans128(self, dst_sb_ap, src_sb_ap, bank, rows, cols, eng="v", bp=0):
        ps = self.psb[bank]
        self.mk.t.transpose(ps[0:cols, 0:rows], src_sb_ap,
                            self.ident[bp:bp + rows, bp:bp + rows])
        if eng == "s":
            self.mk.s.copy(dst_sb_ap, ps[0:cols, 0:rows])
        else:
            getattr(self.mk, eng).tensor_copy(dst_sb_ap, ps[0:cols, 0:rows])

    def phase_ln(self):
        mk = self.mk
        self.xn = self.ar("xn", [128, 8 * L])
        xtm = self.ar("xtm", [128, D])
        xsq = self.ar("xsq", [128, D])
        cols = self.ar("lncols", [128, 8])
        for ti in range(16):
            mk.dma("y", xtm[:], self.x_in.ap()[ti * 128:(ti + 1) * 128, :])
            ssum = cols[:, 0:1]
            ssq = cols[:, 1:2]
            mu = cols[:, 2:3]
            musq = cols[:, 3:4]
            var = cols[:, 4:5]
            isd = cols[:, 5:6]
            mk.v.reduce_sum(ssum, xtm[:], axis=AX.X)
            self.act(xsq[:], xtm[:], AF.Square, accum_out=ssq)
            self.ts(mu, ssum, 1.0 / D)
            self.act(musq, mu, AF.Square)
            self.stt(var, ssq, 1.0 / D, musq, op0=ALU.mult, op1=ALU.subtract)
            self.act(var, var, AF.Ln, bias=1e-5)
            self.act(isd, var, AF.Exp, scale=-0.5)
            self.ts(xtm[:], xtm[:], mu, op0=ALU.subtract)
            self.ts(xtm[:], xtm[:], isd, op0=ALU.mult)
            for dc in range(8):
                self.trans128(self.xn[:, dc * L + ti * 128: dc * L + ti * 128 + 128],
                              xtm[:, dc * 128:(dc + 1) * 128], bank=dc % 2,
                              rows=128, cols=128)
        for dc in range(8):
            mk.dma("s", self.sc_xn.ap()[dc * 128:(dc + 1) * 128, :],
                   self.xn[:, dc * L:(dc + 1) * L])

    def load_wslab(self, dst, wT_dram, m0, mw, name="w"):
        for kc in range(8):
            self.mk.dma("y", dst[:, kc * mw:(kc + 1) * mw],
                        wT_dram.ap()[kc * 128:(kc + 1) * 128, m0:m0 + mw])

    def proj_mtile(self, wbuf, mw, bank, evac):
        mk = self.mk
        ps = self.psb[bank]
        for tg in range(4):
            for kc in range(8):
                mk.t.matmul(ps[0:mw, 0:512],
                            wbuf[:, kc * mw:(kc + 1) * mw],
                            self.xn[:, kc * L + tg * 512: kc * L + (tg + 1) * 512],
                            start=(kc == 0), stop=(kc == 7))
            evac(tg, ps[0:mw, 0:512])

    def phase_dt(self):
        mk = self.mk
        wdt = self.ar("wdt", [128, 8 * NH])
        self.load_wslab(wdt, self.WinT, 2 * DI, NH)
        dtr = self.ar("dtraw", [NH, 512])

        def evac(tg, ps):
            sl = self.dt_fm[:, tg * 512:(tg + 1) * 512]
            self.ts(dtr[:], ps, self.dtb_col[0:NH, 0:1], 0.0,
                    op0=ALU.add, op1=ALU.min)
            self.act(dtr[:], dtr[:], AF.Exp)
            self.act(dtr[:], dtr[:], AF.Ln, bias=1.0)
            self.ts(sl, dtr[:], 1e-4, 0.1, op0=ALU.max, op1=ALU.min)

        self.proj_mtile(wdt, NH, 6, evac)

    def phase_bc_norms(self):
        mk = self.mk
        wb = [self.ar(f"wbc{i}", [128, 8 * 128]) for i in range(2)]
        bcb = [self.ar("bcb0", [128, 4 * L])]
        sqf = self.ar("sqf", [128, L])
        sqb = self.ar("sqb", [128, 512])
        nstage = self.ar("nstage", [1, 512])
        BC0 = 2 * DI + NH
        for h in range(NH):
            wbuf = wb[h % 2]
            bcbuf = bcb[0]
            for mt in range(4):
                self.load_wslab(wbuf, self.WinT, BC0 + h * 512 + mt * 128, 128)
                ps = self.psb[mt % 2]
                nps = self.psb[2]
                for tg in range(4):
                    for kc in range(8):
                        mk.t.matmul(ps[0:128, 0:512],
                                    wbuf[:, kc * 128:(kc + 1) * 128],
                                    self.xn[:, kc * L + tg * 512: kc * L + (tg + 1) * 512],
                                    start=(kc == 0), stop=(kc == 7))
                    pa = ps[0:128, 0:512]
                    mk.v.tensor_copy(bcbuf[:, mt * L + tg * 512: mt * L + (tg + 1) * 512], pa)
                    if mt % 2 == 0:
                        self.act(sqf[:, tg * 512:(tg + 1) * 512], pa, AF.Square)
                    else:
                        self.act(sqb[:], pa, AF.Square)
                        mk.t.matmul(nps[0:1, 0:512], self.ones_col[:],
                                    sqf[:, tg * 512:(tg + 1) * 512],
                                    start=True, stop=False)
                        mk.t.matmul(nps[0:1, 0:512], self.ones_col[:], sqb[:],
                                    start=False, stop=True)
                        j = mt // 2
                        mk.s.copy(nstage[0:1, 0:512], nps[0:1, 0:512])
                        mk.dma("g", self.rn_sq[2 * h + j:2 * h + j + 1,
                                               tg * 512:(tg + 1) * 512],
                               nstage[0:1, 0:512])
            mk.dma("s", self.sc_bc.ap()[h * 512:(h + 1) * 512, :]
                   .rearrange("(m p) t -> p m t", p=128),
                   bass.AP(bcbuf, 0, [[4 * L, 128], [L, 4], [1, L]]))

    def phase_norm_finish(self):
        mk = self.mk
        self.act(self.rn_sq[:], self.rn_sq[:], AF.Ln, bias=1e-8)
        self.act(self.rn_sq[:], self.rn_sq[:], AF.Exp, scale=-0.5)
        for ti in range(16):
            self.trans128(self.rn_tm[:, ti * 64:(ti + 1) * 64],
                          self.rn_sq[:, ti * 128:(ti + 1) * 128], bank=6,
                          rows=64, cols=128)

    def phase_z_xc_u(self):
        mk = self.mk
        wz = self.ar("wz", [128, 8 * 128])
        wx = self.ar("wx", [128, 8 * 128])
        zb = self.ar("zb", [128, L])
        xcb = self.ar("xcb", [128, 4 + L])
        ub = self.ar("ub", [128, L])
        utb = self.ar("utb", [128, 2 * 16 * 64])
        mk.g.memset(xcb[:, 0:3], 0.0)
        for c in range(16):
            self.load_wslab(wz, self.WinT, c * 128, 128)
            self.load_wslab(wx, self.WinT, DI + c * 128, 128)

            def evz(tg, pa):
                self.act(zb[:, tg * 512:(tg + 1) * 512], pa, AF.Silu)

            def evx(tg, pa):
                mk.v.tensor_copy(xcb[:, 3 + tg * 512: 3 + (tg + 1) * 512], pa)

            self.proj_mtile(wz, 128, 3, evz)
            self.proj_mtile(wx, 128, 4, evx)
            mk.dma("s", self.sc_z.ap()[c * 128:(c + 1) * 128, :], zb[:])
            w4 = self.dwW[:, c * 4:(c + 1) * 4]
            self.ts(ub[:], xcb[:, 0:L], w4[:, 0:1])
            for r in range(1, 4):
                self.stt(ub[:], xcb[:, r:r + L], w4[:, r:r + 1], ub[:])
            self.act(ub[:], ub[:], AF.Silu, bias=self.dwB[:, c:c + 1])
            mk.dma("s", self.sc_u.ap()[c * 128:(c + 1) * 128, :], ub[:])
            for hi in range(2):
                for ci in range(16):
                    self.trans128(utb[:, hi * 1024 + ci * 64: hi * 1024 + (ci + 1) * 64],
                                  ub[hi * 64:(hi + 1) * 64, ci * 128:(ci + 1) * 128],
                                  bank=5, rows=64, cols=128, bp=hi * 64)
                h = 2 * c + hi
                mk.dma("s", self.sc_ut.ap()[h * L:(h + 1) * L, :]
                       .rearrange("(c p) d -> p c d", p=128),
                       bass.AP(utb, hi * 1024, [[2048, 128], [64, 16], [1, 64]]))

    def phase_ssm_core(self):
        mk = self.mk
        ar = self.ar
        bch = ar("bch", [128, L])
        dt_bc = ar("dt_bc", [128, L])
        rd = ar("rd", [128, L])
        ab = ar("abb", [128, L])
        bb = ar("bbb", [128, L])
        idet = ar("idet", [128, L])
        Kf = [ar(f"Kf{j}", [128, L]) for j in range(4)]
        Qf = [ar(f"Qf{j}", [128, L]) for j in range(4)]
        w1 = ar("w1", [128, L]); w2 = ar("w2", [128, L]); w3 = ar("w3", [128, L])
        ibuf = ar("ibuf", [128, L // 4], I32)
        ebuf = ar("ebuf", [128, 16 * 16])
        dtrow = ar("dtrow", [1, L])
        rcrow = ar("rcrow", [1, L])
        ygb = ar("ygb", [64, L])
        rCb = ar("rCb", [64, L])
        utc = ar("utc", [128, PD])
        gsb = ar("gsb", [128, 128])
        kysb = ar("kysb", [128, 128])
        ktmp = ar("ktmp", [128, 128])
        Yst = [ar("Yst0", [128, PD]), ar("Yst1", [128, PD])]
        Vst = [ar("Vst0", [128, PD]), ar("Vst1", [128, PD])]
        sYt = ar("sYt", [128, PD])

        def eslot(i):
            return ebuf[:, i * 16:(i + 1) * 16]

        def ecol(i, c):
            return ebuf[:, i * 16 + c:i * 16 + c + 1]

        PG, PO, PKY, PKV, PYA, PVA, PRP = 0, 1, 2, 3, 4, 5, 6

        for h in range(NH):
            hc = lambda name: self.hpc(name, h)
            mk.dma("g", dtrow[0:1, :], self.dt_fm[h:h + 1, :])
            mk.dma("g", rcrow[0:1, :], self.rn_sq[2 * h + 1:2 * h + 2, :])
            prp = self.psb[PRP]
            for tg in range(4):
                mk.t.matmul(prp[0:128, 0:512], self.ones_row1[0:1, 0:128],
                            dtrow[0:1, tg * 512:(tg + 1) * 512],
                            start=True, stop=True)
                mk.s.copy(dt_bc[:, tg * 512:(tg + 1) * 512], prp[0:128, 0:512])
            for tg in range(4):
                mk.t.matmul(prp[0:64, 0:512], self.ones_row1[0:1, 0:64],
                            rcrow[0:1, tg * 512:(tg + 1) * 512],
                            start=True, stop=True)
                mk.s.copy(rCb[0:64, tg * 512:(tg + 1) * 512], prp[0:64, 0:512])
            mk.dma("y", bch[:], self.sc_bc.ap()[h * 512 + 128:h * 512 + 256, :])

            self.ts(w1[:], dt_bc[:], hc("k2"), 1.0)
            self.ts(w2[:], dt_bc[:], hc("rs"))
            self.ts(w3[:], dt_bc[:], hc("A"), hc("kf"))
            self.tt(w3[:], w3[:], dt_bc[:])
            self.act(w3[:], w3[:], AF.Ln, bias=1.0)
            self.act(rd[:], w3[:], AF.Exp, scale=-1.0)
            self.tt(bb[:], w1[:], w2[:], ALU.add)
            self.act(bb[:], bb[:], AF.Ln)
            self.ts(idet[:], w3[:], hc("mos_h"))
            self.stt(idet[:], bb[:], hc("mhy"), idet[:])
            self.tt(ab[:], w1[:], w2[:], ALU.subtract)
            self.act(ab[:], ab[:], AF.Ln)
            mk.v.reciprocal(w1[:], w1[:])
            self.tt(w2[:], w2[:], w1[:])
            self.act(w2[:], w2[:], AF.Arctan)
            self.ts(w2[:], w2[:], hc("mos"))
            self.stt(ab[:], ab[:], hc("mhy"), w2[:])
            mk.v.tensor_tensor_scan(w1[:], self.stripe01[:], idet[:], 0.0,
                                    ALU.mult, ALU.add)
            mk.v.tensor_tensor_scan(w2[:], self.stripe01[:], ab[:], 0.0,
                                    ALU.mult, ALU.add)
            mk.v.tensor_tensor_scan(idet[:], self.stripe01[:], w3[:], 0.0,
                                    ALU.mult, ALU.add)
            self.act(idet[:], idet[:], AF.Exp, scale=-1.0)
            self.act(w1[:], w1[:], AF.Exp)
            self.act(w3[:], w2[:], AF.Exp)
            self.ts(bb[:], w2[:], 1.0 / (2 * PI))
            for hh in range(4):
                hsl = slice(hh * (L // 4), (hh + 1) * (L // 4))
                mk.v.tensor_copy(ibuf[:], bb[:, hsl])
                mk.v.tensor_copy(bb[:, hsl], ibuf[:])
            self.stt(bb[:], bb[:], -2 * PI, w2[:])
            self.ts(Kf[1][:], bb[:], PI32, op0=ALU.is_gt)
            self.stt(bb[:], Kf[1][:], -2 * PI, bb[:])
            self.ts(bb[:], bb[:], -PI32, PI32, op0=ALU.max, op1=ALU.min)
            self.act(bb[:], bb[:], AF.Sin)
            self.ts(Kf[0][:], w2[:], 1.0 / (2 * PI), 0.25)
            for hh in range(4):
                hsl = slice(hh * (L // 4), (hh + 1) * (L // 4))
                mk.v.tensor_copy(ibuf[:], Kf[0][:, hsl])
                mk.v.tensor_copy(Kf[0][:, hsl], ibuf[:])
            self.stt(Kf[0][:], Kf[0][:], -2 * PI, w2[:])
            self.ts(Kf[1][:], Kf[0][:], PI32 / 2, op0=ALU.is_gt)
            self.stt(Kf[0][:], Kf[1][:], -2 * PI, Kf[0][:])
            self.ts(Kf[0][:], Kf[0][:], -PI32 - PI / 2, PI32 - PI / 2,
                    op0=ALU.max, op1=ALU.min)
            self.act(Kf[0][:], Kf[0][:], AF.Sin, bias=PI / 2)
            self.tt(w2[:], w1[:], Kf[0][:])
            self.tt(Kf[1][:], w1[:], w3[:], ALU.add)
            self.ts(w2[:], w2[:], hc("mos"))
            self.stt(ab[:], Kf[1][:], hc("mhy_h"), w2[:])
            self.tt(Kf[1][:], w1[:], w3[:], ALU.subtract)
            self.tt(w1[:], w1[:], bb[:])
            self.ts(w1[:], w1[:], hc("irs_o"))
            self.stt(bb[:], Kf[1][:], hc("irs_h2"), w1[:])

            self.tt(w1[:], dt_bc[:], rd[:])
            self.tt(w2[:], w1[:], bch[:])
            self.tt(w1[:], dt_bc[:], w2[:])
            mk.dma("y", bch[:], self.sc_bc.ap()[h * 512:h * 512 + 128, :])
            self.tt(w3[:], dt_bc[:], bch[:])
            self.tt(w3[:], w3[:], w1[:], ALU.add)
            self.tt(Kf[0][:], ab[:], w3[:])
            self.tt(Kf[1][:], ab[:], w2[:])
            self.ts(w1[:], bb[:], -1.0)
            self.tt(Kf[2][:], w1[:], w3[:])
            self.tt(Kf[3][:], w1[:], w2[:])
            mk.dma("y", bch[:], self.sc_bc.ap()[h * 512 + 384:h * 512 + 512, :])
            self.ts(w1[:], bch[:], hc("negA"))
            self.ts(w2[:], bch[:], hc("k2n"))
            self.tt(w3[:], ab[:], bch[:])
            self.ts(Qf[3][:], bb[:], hc("sig"))
            self.tt(Qf[3][:], Qf[3][:], bch[:])
            mk.dma("y", bch[:], self.sc_bc.ap()[h * 512 + 256:h * 512 + 384, :])
            self.stt(w1[:], bch[:], hc("k2"), w1[:])
            self.tt(w2[:], bch[:], w2[:], ALU.add)
            self.tt(Qf[1][:], bb[:], w2[:])
            self.tt(Qf[1][:], w3[:], Qf[1][:], ALU.add)
            self.tt(Qf[1][:], Qf[1][:], idet[:])
            self.tt(w3[:], ab[:], bch[:])
            self.tt(Qf[0][:], bb[:], w1[:])
            self.tt(Qf[0][:], w3[:], Qf[0][:], ALU.add)
            self.tt(Qf[0][:], Qf[0][:], idet[:])
            self.ts(w3[:], bb[:], hc("sig"))
            self.tt(Qf[2][:], w3[:], bch[:])
            self.tt(w1[:], ab[:], w1[:])
            self.tt(Qf[2][:], Qf[2][:], w1[:], ALU.add)
            self.tt(Qf[2][:], Qf[2][:], idet[:])
            self.tt(w2[:], ab[:], w2[:])
            self.tt(Qf[3][:], Qf[3][:], w2[:], ALU.add)
            self.tt(Qf[3][:], Qf[3][:], idet[:])

            mk.v.tensor_copy(eslot(0), bass.AP(ab, TC - 1, [[L, 128], [TC, 16]]))
            mk.v.tensor_copy(eslot(1), bass.AP(bb, TC - 1, [[L, 128], [TC, 16]]))
            mk.v.tensor_copy(eslot(2), bass.AP(idet, TC - 1, [[L, 128], [TC, 16]]))
            self.tt(eslot(3), eslot(0), eslot(2))
            self.tt(eslot(4), eslot(1), eslot(2))
            self.stt(eslot(5), eslot(4), hc("k2"), eslot(3))
            self.ts(eslot(6), eslot(4), hc("sig"))
            self.stt(eslot(7), eslot(3), hc("k2"), eslot(6))
            self.ts(eslot(8), eslot(4), hc("negA"))
            self.stt(eslot(9), eslot(4), hc("k2n"), eslot(3))
            self.ts(eslot(10), eslot(3), hc("negA"))
            self.stt(eslot(11), eslot(3), hc("k2n"), eslot(6))
            self.stt(eslot(12), eslot(1), hc("k2"), eslot(0))
            self.tt(eslot(12), eslot(12), eslot(2))
            self.ts(eslot(13), eslot(4), hc("negA"))
            self.stt(eslot(14), eslot(1), hc("k2n"), eslot(0))
            self.tt(eslot(14), eslot(14), eslot(2))
            DQ = {1: 5, 2: 4, 3: 7, 4: 3}
            FQ = {1: 8, 2: 9, 3: 10, 4: 11}

            Y, V = Yst[0], Vst[0]
            Y2, V2 = Yst[1], Vst[1]
            for c in range(NCH):
                sl = slice(c * TC, (c + 1) * TC)
                mk.dma("y", utc[:], self.sc_ut.ap()[h * L + c * TC:
                                                    h * L + (c + 1) * TC, :])
                self.ts(utc[:], utc[:],
                        self.rn_tm[:, c * 64 + 2 * h:c * 64 + 2 * h + 1])
                pg = self.psb[PG]
                for j in range(4):
                    mk.t.matmul(pg[0:128, 0:128], Kf[j][:, sl], Qf[j][:, sl],
                                start=(j == 0), stop=(j == 3))
                self.tt(gsb[:], pg[0:128, 0:128], self.tri_incl[:])
                po = self.psb[PO]
                oc = (c % 4) * 128
                mk.t.matmul(po[0:64, oc:oc + 128], utc[:], gsb[:],
                            start=True, stop=(c == 0))
                if c > 0:
                    mk.t.matmul(po[0:64, oc:oc + 128], Y[:, 0:64], Qf[0][:, sl],
                                start=False, stop=False)
                    mk.t.matmul(po[0:64, oc:oc + 128], V[:, 0:64], Qf[1][:, sl],
                                start=False, stop=True)
                pky, pkv = self.psb[PKY], self.psb[PKV]
                for j in (1, 2, 3, 4):
                    self.ts(ktmp[:], Kf[j - 1][:, sl], ecol(DQ[j], c))
                    mk.t.matmul(pky[0:128, 0:128], ktmp[:], self.ident[:],
                                is_transpose=True, start=(j == 1), stop=(j == 4))
                mk.s.copy(kysb[:], pky[0:128, 0:128])
                pya, pva = self.psb[PYA], self.psb[PVA]
                mk.t.matmul(pya[0:128, 0:64], kysb[:], utc[:],
                            start=True, stop=True)
                for j in (1, 2, 3, 4):
                    self.ts(ktmp[:], Kf[j - 1][:, sl], ecol(FQ[j], c))
                    mk.t.matmul(pkv[0:128, 0:128], ktmp[:], self.ident[:],
                                is_transpose=True, start=(j == 1), stop=(j == 4))
                mk.s.copy(kysb[:], pkv[0:128, 0:128])
                mk.t.matmul(pva[0:128, 0:64], kysb[:], utc[:],
                            start=True, stop=True)
                if c == 0:
                    mk.s.copy(Y2[:], pya[0:128, 0:64])
                    mk.s.copy(V2[:], pva[0:128, 0:64])
                else:
                    self.stt(sYt[:], Y[:], ecol(12, c), pya[0:128, 0:64])
                    self.stt(Y2[:], V[:], ecol(4, c), sYt[:])
                    self.stt(sYt[:], Y[:], ecol(13, c), pva[0:128, 0:64])
                    self.stt(V2[:], V[:], ecol(14, c), sYt[:])
                Y, Y2 = Y2, Y
                V, V2 = V2, V
                if c % 4 == 3:
                    g0 = (c - 3) * TC
                    self.tt(ygb[0:64, g0:g0 + 512], po[0:64, 0:512],
                            rCb[0:64, g0:g0 + 512])
            mk.dma("s", self.sc_yg.ap()[h * 64:(h + 1) * 64, :], ygb[0:64, :])

    def phase_wout(self):
        mk = self.mk
        ygs = [self.ar(f"ygs{i}", [128, 512]) for i in range(16)]
        uzt = self.ar("uzt", [128, 512])
        wsl = [self.ar(f"wsl{i}", [128, 16 * 128]) for i in range(2)]
        sout = self.ar("sout", [128, 512])
        for tg in range(4):
            for kc in range(16):
                mk.dma("y", ygs[kc][:],
                       self.sc_yg.ap()[kc * 128:(kc + 1) * 128,
                                       tg * 512:(tg + 1) * 512])
                mk.dma("y", uzt[:], self.sc_u.ap()[kc * 128:(kc + 1) * 128,
                                                   tg * 512:(tg + 1) * 512])
                self.stt(ygs[kc][:], uzt[:], self.skip_c[:, kc:kc + 1],
                         ygs[kc][:])
                mk.dma("y", uzt[:], self.sc_z.ap()[kc * 128:(kc + 1) * 128,
                                                   tg * 512:(tg + 1) * 512])
                self.tt(ygs[kc][:], ygs[kc][:], uzt[:])
            for m in range(8):
                wb = wsl[m % 2]
                if tg == 0 and m == 0:
                    mk.dma("y", wb[:], self.wott.ap()[0])
                nidx = tg * 8 + m + 1
                if nidx < 32:
                    mk.dma("y", wsl[nidx % 2][:], self.wott.ap()[nidx % 8])
                ps = self.psb[m % 2]
                for kc in range(16):
                    mk.t.matmul(ps[0:128, 0:512], wb[:, kc * 128:(kc + 1) * 128],
                                ygs[kc][:], start=(kc == 0), stop=(kc == 15))
                mk.v.tensor_copy(sout[:], ps[0:128, 0:512])
                mk.dma("s", self.sc_s.ap()[m * 128:(m + 1) * 128,
                                           tg * 512:(tg + 1) * 512], sout[:])

    def phase_final(self):
        mk = self.mk
        xt = self.ar("fxt", [128, D])
        st = self.ar("fst", [128, D])
        gt = self.ar("fgt", [128, D])
        sfm = self.ar("sfm", [128, D])
        gfm = self.ar("gfm", [128, D])
        for ti in range(16):
            mk.dma("y", xt[:], self.x_in.ap()[ti * 128:(ti + 1) * 128, :])
            for dc in range(8):
                mk.dma("y", sfm[:, dc * 128:(dc + 1) * 128],
                       self.sc_s.ap()[dc * 128:(dc + 1) * 128,
                                      ti * 128:(ti + 1) * 128])
                mk.dma("y", gfm[:, dc * 128:(dc + 1) * 128],
                       self.sc_g.ap()[dc * 128:(dc + 1) * 128,
                                      ti * 128:(ti + 1) * 128])
            for dc in range(8):
                self.trans128(st[:, dc * 128:(dc + 1) * 128],
                              sfm[:, dc * 128:(dc + 1) * 128], bank=dc % 2,
                              rows=128, cols=128)
                self.trans128(gt[:, dc * 128:(dc + 1) * 128],
                              gfm[:, dc * 128:(dc + 1) * 128], bank=2 + dc % 2,
                              rows=128, cols=128, eng="s")
            self.act(gt[:], gt[:], AF.Silu)
            self.tt(st[:], st[:], gt[:])
            self.tt(st[:], st[:], xt[:], ALU.add)
            mk.dma("s", self.out.ap()[ti * 128:(ti + 1) * 128, :], st[:])

    def ffn_load_cols(self, dst, dram_row_ap, n, off=0):
        tmp = self.ar(f"flc_{self.uid()}", [max(n, 1), 128])
        self.mk.dma("y", tmp[0:n, :],
                    dram_row_ap.rearrange("o (c p) -> (o c) p", p=128))
        ps = self.psb[7]
        self.mk.t.transpose(ps[0:128, 0:n], tmp[0:n, :], self.ident[0:n, 0:n])
        self.mk.v.tensor_copy(dst[:, off:off + n], ps[0:128, 0:n])

    def phase_ffn_prep(self):
        mk = self.mk
        g = mk.g
        ar = self.ar
        self.t_row = ar("t_row", [1, 512])
        self.pw_row = ar("pw_row", [1, 512])
        self.if_row = ar("if_row", [1, 128])
        self.lb_row = ar("lb_row", [1, 128])
        dr = ar("dr", [1, 128])
        g.iota(dr[:], [[1, 128]], channel_multiplier=0,
               allow_small_or_imprecise_dtypes=True)
        tmp = ar("drt", [1, 128])
        ib2 = ar("ib2", [1, 128], I32)
        self.ts(tmp[:], dr[:], 1.0 / 32.0)
        mk.v.tensor_copy(ib2[:], tmp[:])
        mk.v.tensor_copy(tmp[:], ib2[:])
        self.stt(dr[:], tmp[:], -32.0, dr[:])
        self.ts(tmp[:], dr[:], 0.0, op0=ALU.is_lt)
        self.stt(dr[:], tmp[:], 32.0, dr[:])
        self.ts(self.if_row[:], dr[:], -2.0 / 64.0 * float(np.log(10000.0)))
        self.act(self.if_row[:], self.if_row[:], AF.Exp)
        self.ts(self.lb_row[:], dr[:], 2.0 / 89.6, 25.6 / 89.6)
        self.act(self.lb_row[:], self.lb_row[:], AF.Ln)
        self.emask = ar("emask", [128, 1408])
        g.memset(self.emask[:], 0.0)
        g.affine_select(self.emask[:], self.emask[:], [[1, 1408]], ALU.is_ge,
                        NEG_BIG, base=-384, channel_multiplier=-1)
        g.affine_select(self.emask[:], self.emask[:], [[-1, 1408]], ALU.is_ge,
                        NEG_BIG, base=896, channel_multiplier=1)
        self.esel = ar("esel", [8, 4 * 128])
        g.memset(self.esel[:], 0.0)
        for dc in range(4):
            sl = self.esel[:, dc * 128:(dc + 1) * 128]
            g.memset(sl, 1.0)
            g.affine_select(sl, sl, [[1, 128]], ALU.is_ge, 0.0,
                            base=128 * dc, channel_multiplier=-64)
            g.affine_select(sl, sl, [[-1, 128]], ALU.is_ge, 0.0,
                            base=63 - 128 * dc, channel_multiplier=64)
        self.sgnc = ar("sgnc", [128, 1])
        g.memset(self.sgnc[:], 1.0)
        g.memset(self.sgnc[0:32, 0:1], -1.0)
        g.memset(self.sgnc[64:96, 0:1], -1.0)

    def phase_ffn(self, cut=None):
        mk = self.mk
        xf = self.ar("xf", [128, 8 * L])
        for dc in range(8):
            mk.dma("y", xf[:, dc * L:(dc + 1) * L],
                   self.sc_xn.ap()[dc * 128:(dc + 1) * 128, :])
        self.phase_ffn_prep()
        ffn_mark = self.arena_mark()
        done = False
        for l in range(2):
            self.ffn_attn(l, xf)
            self.arena_reset(ffn_mark)
            if cut == f"attn{l}":
                done = True
                break
            self.ffn_mlp(l, xf)
            self.arena_reset(ffn_mark)
            if cut == f"mlp{l}":
                done = True
                break
        for dc in range(8):
            mk.dma("s", self.sc_g.ap()[dc * 128:(dc + 1) * 128, :],
                   xf[:, dc * L:(dc + 1) * L])

    def ffn_attn(self, l, xf):
        mk = self.mk
        ar = self.ar
        qs_c = ar("qs_c", [128, 2])
        qs_s = ar("qs_s", [128, 2])
        t2 = ar("t2r", [2, 64])
        mk.dma("y", t2[0:1, :], self.qscale.ap()[l])
        mk.dma("y", t2[1:2, :], self.kscale.ap()[l])
        ps7 = self.psb[7]
        mk.t.transpose(ps7[0:64, 0:2], t2[0:2, 0:64], self.ident[0:2, 0:2])
        mk.v.tensor_copy(qs_c[0:64, 0:2], ps7[0:64, 0:2])
        mk.dma("g", qs_c[64:128, 0:2], qs_c[0:64, 0:2])
        for base in (0, 64):
            mk.dma("g", qs_s[base:base + 32, 0:2], qs_c[base + 32:base + 64, 0:2])
            mk.dma("g", qs_s[base + 32:base + 64, 0:2], qs_c[base:base + 32, 0:2])
        qw = ar("qw", [128, 4 * 512])
        kw = [ar("kw0", [128, 4 * 512]), ar("kw1", [128, 4 * 512])]
        qsh = ar("qsh", [128, 512])
        vtm = ar("vtm", [128, 8 * 520])
        tabs = ar("tabs", [128, 4 * 512])
        ssb = ar("ssb", [128, 8 * 512])
        sq = ar("sqf2", [128, 512])
        nrq = ar("nrq", [8, 512])
        nrk = ar("nrk", [8, 512])
        nst = ar("nst", [1, 512])
        m1 = ar("m1w", [128, 512])
        mxc = ar("mxc", [128, 4])
        mxe = ar("mxe", [4, 128])
        mxr = ar("mxr", [1, 512])
        dnr = ar("dnr", [8, 512])
        obuf = ar("obuf", [128, 4 * 512])
        wsl = [ar("wslf0", [128, 1024]), ar("wslf1", [128, 1024])]
        vsl = ar("vslf", [128, 2 * 512])
        ib3 = ar("ib3", [128, 512], I32)
        wf = m1
        bstash = ar("bstash", [128, 8 * 3])
        x0s = obuf
        PA, PB, PC, PS0, PS1, PV, PR = 0, 1, 2, 3, 4, 5, 6
        for ti in range(8):
            for hh in range(FH):
                mk.g.memset(vtm[:, ti * 520 + hh * 65 + 64:
                                ti * 520 + hh * 65 + 65], 1.0)

        for w in range(4):
            t0 = w * FW
            kcur = kw[w % 2]
            kprev = kw[(w + 1) % 2]
            for i, dc in enumerate(range(4, 8)):
                if w == 0:
                    mk.g.memset(x0s[:, i * 512:i * 512 + 1], 0.0)
                else:
                    mk.v.tensor_copy(x0s[:, i * 512:i * 512 + 1],
                                     bstash[:, dc * 3 + (w - 1):dc * 3 + w])
                mk.v.tensor_copy(x0s[:, i * 512 + 1:(i + 1) * 512],
                                 xf[:, dc * L + t0: dc * L + t0 + 511])

            def rhs_x(dc, fr0, frw):
                if dc < 4:
                    return xf[:, dc * L + t0 + fr0: dc * L + t0 + fr0 + frw]
                return x0s[:, (dc - 4) * 512 + fr0:(dc - 4) * 512 + fr0 + frw]

            mk.g.iota(self.t_row[:], [[1, 512]], base=t0, channel_multiplier=0,
                      allow_small_or_imprecise_dtypes=True)
            self.ts(self.pw_row[:], self.t_row[:], 1.0 / 256.0, -1024.0 / 256.0)
            pt = self.psb[PA]
            mk.t.matmul(pt[0:128, 0:512], self.if_row[:],
                        self.t_row[0:1, 0:512], start=True, stop=True)
            for which, bias in (("sin", 0.0), ("cos", 0.25)):
                self.ts(wf[:], pt[0:128, 0:512], 1.0 / (2 * PI), bias)
                mk.v.tensor_copy(ib3[:], wf[:])
                mk.v.tensor_copy(wf[:], ib3[:])
                self.stt(wf[:], wf[:], -2 * PI, pt[0:128, 0:512])
                thr = PI32 if which == "sin" else PI32 / 2
                self.ts(sq[:], wf[:], thr, op0=ALU.is_gt)
                self.stt(wf[:], sq[:], -2 * PI, wf[:])
                if which == "sin":
                    self.ts(wf[:], wf[:], -PI32, PI32, op0=ALU.max, op1=ALU.min)
                    self.act(qsh[:], wf[:], AF.Sin)
                else:
                    self.ts(wf[:], wf[:], -PI32 - PI / 2, PI32 - PI / 2,
                            op0=ALU.max, op1=ALU.min)
                    self.act(tabs[:, 3 * 512:4 * 512], wf[:], AF.Sin,
                             bias=PI / 2)
            mk.t.matmul(pt[0:128, 0:512], self.lb_row[:],
                        self.pw_row[0:1, 0:512], start=True, stop=True)
            self.act(wf[:], pt[0:128, 0:512], AF.Exp)
            self.tt(tabs[:, 0:512], tabs[:, 3 * 512:4 * 512], wf[:])
            self.tt(tabs[:, 512:1024], qsh[:], wf[:])
            self.act(wf[:], pt[0:128, 0:512], AF.Exp, scale=-1.0)
            self.tt(tabs[:, 1024:1536], tabs[:, 3 * 512:4 * 512], wf[:])
            self.tt(tabs[:, 3 * 512:4 * 512], qsh[:], wf[:])
            self.ts(tabs[:, 0:512], tabs[:, 0:512], qs_c[:, 0:1])
            self.ts(tabs[:, 0:512], tabs[:, 0:512], 8.0)
            self.tt(tabs[:, 512:1024], tabs[:, 512:1024],
                    self.bcast_free(self.sgnc[:, 0:1], 512))
            self.ts(tabs[:, 512:1024], tabs[:, 512:1024], qs_s[:, 0:1])
            self.ts(tabs[:, 512:1024], tabs[:, 512:1024], 8.0)
            self.ts(tabs[:, 1024:1536], tabs[:, 1024:1536], qs_c[:, 1:2])
            self.tt(tabs[:, 3 * 512:4 * 512], tabs[:, 3 * 512:4 * 512],
                    self.bcast_free(self.sgnc[:, 0:1], 512))
            self.ts(tabs[:, 3 * 512:4 * 512], tabs[:, 3 * 512:4 * 512],
                    qs_s[:, 1:2])

            qkseq = [("q", mc) for mc in range(4)] + [("k", mc) for mc in range(4)]
            mk.dma("y", wsl[0][:], self.wqkt.ap()[l * 8 + 0])
            for qidx, (side, mc) in enumerate(qkseq):
                dst = qw if side == "q" else kcur
                buf = wsl[qidx % 2]
                if qidx + 1 < 8:
                    nside, nmc = qkseq[qidx + 1]
                    mk.dma("y", wsl[(qidx + 1) % 2][:],
                           self.wqkt.ap()[l * 8 +
                                          (0 if nside == "q" else 1) * 4 + nmc])
                if True:
                    ps = self.psb[PB]
                    for kc in range(8):
                        mk.t.matmul(ps[0:128, 0:512],
                                    buf[:, kc * 128:(kc + 1) * 128],
                                    rhs_x(kc, 0, 512), start=(kc == 0),
                                    stop=(kc == 7))
                    pa = ps[0:128, 0:512]
                    mk.v.tensor_copy(dst[:, mc * 512:(mc + 1) * 512], pa)
                    self.act(sq[:], pa, AF.Square)
                    nps = self.psb[PC]
                    nrt = nrq if side == "q" else nrk
                    for hh in range(2):
                        b = hh * 64
                        mk.t.matmul(nps[0:1, 0:512],
                                    self.ones_col[b:b + 64, 0:1],
                                    sq[b:b + 64, :], start=True, stop=True)
                        mk.s.copy(nst[0:1, 0:512], nps[0:1, 0:512])
                        mk.dma("g", nrt[2 * mc + hh:2 * mc + hh + 1, :],
                               nst[0:1, 0:512])
            vpb = [PA, PS0, PS1, PV]
            for kc in range(8):
                vs = vsl[:, (kc % 2) * 512:(kc % 2) * 512 + 512]
                mk.dma("y", vs, self.wvt.ap()[l * 8 + kc])
                for i in range(4):
                    mk.t.matmul(self.psb[vpb[i]][0:128, 0:512],
                                rhs_x(kc, i * 128, 128), vs,
                                start=(kc == 0), stop=(kc == 7))
            for i in range(4):
                ti = (4 * w + i) % 8
                mk.v.tensor_copy(
                    bass.AP(vtm, ti * 520, [[8 * 520, 128], [65, 8], [1, 64]]),
                    self.psb[vpb[i]][0:128, 0:512])
            for nrt in (nrq, nrk):
                self.ts(nrt[:], nrt[:], 1e-24, op0=ALU.max)
                self.act(nrt[:], nrt[:], AF.Ln)
                self.act(nrt[:], nrt[:], AF.Exp, scale=-0.5)
            for side, dst in (("q", qw), ("k", kcur)):
                nrt = nrq if side == "q" else nrk
                tb = 0 if side == "q" else 1024
                for mc in range(4):
                    csl = slice(mc * 512, (mc + 1) * 512)
                    pr = self.psb[PR]
                    mk.t.matmul(pr[0:128, 0:512],
                                self.esel[:, mc * 128:(mc + 1) * 128],
                                nrt[0:8, :], start=True, stop=True)
                    self.tt(dst[:, csl], dst[:, csl], pr[0:128, 0:512])
                    for b in (0, 64):
                        mk.dma("g", qsh[b:b + 32, :], dst[b + 32:b + 64, csl])
                        mk.dma("g", qsh[b + 32:b + 64, :], dst[b:b + 32, csl])
                    self.tt(qsh[:], qsh[:], tabs[:, tb + 512:tb + 1024])
                    self.tt(dst[:, csl], dst[:, csl], tabs[:, tb:tb + 512])
                    self.tt(dst[:, csl], dst[:, csl], qsh[:], ALU.add)

            for h in range(FH):
                hb = (h % 2) * 64
                mc = h // 2
                csl = slice(mc * 512, (mc + 1) * 512)
                rlist = list(range(4, 8)) if w == 0 else list(range(8))
                qs_ap = qw[hb:hb + 64, csl]
                for idx, r in enumerate(rlist):
                    kbuf = kprev if r < 4 else kcur
                    co = mc * 512 + (r % 4) * 128
                    pss = self.psb[[PS0, PS1][idx % 2]]
                    mk.t.matmul(pss[0:128, 0:512], kbuf[hb:hb + 64, co:co + 128],
                                qs_ap, start=True, stop=True)
                    self.tt(ssb[:, r * 512:(r + 1) * 512], pss[0:128, 0:512],
                            self.emask[:, 896 - 128 * r:1408 - 128 * r],
                            ALU.add)
                nr_ = len(rlist)
                r0_ = rlist[0]
                view = bass.AP(ssb, r0_ * 512,
                               [[8 * 512, 128], [1, 512], [512, nr_]])
                mk.v.reduce_max(m1[:, 0:512], view, axis=AX.X)
                for qsub in range(4):
                    ps = self.psb[PR]
                    mk.t.transpose(ps[0:128, 0:128],
                                   m1[:, qsub * 128:(qsub + 1) * 128],
                                   self.ident[:])
                    mk.v.reduce_max(mxc[:, qsub:qsub + 1], ps[0:128, 0:128],
                                    axis=AX.X)
                ps = self.psb[PR]
                mk.t.transpose(ps[0:4, 0:128], mxc[:, 0:4], self.ident[:])
                mk.v.tensor_copy(mxe[0:4, 0:128], ps[0:4, 0:128])
                mk.dma("g", bass.AP(mxr, 0, [[512, 1], [128, 4], [1, 128]]),
                       bass.AP(mxe, 0, [[128, 4], [1, 128]]))
                mk.t.matmul(ps[0:128, 0:512], self.ones_row1[0:1, 0:128],
                            mxr[0:1, 0:512], start=True, stop=True)
                for r in rlist:
                    self.tt(ssb[:, r * 512:(r + 1) * 512],
                            ssb[:, r * 512:(r + 1) * 512], ps[0:128, 0:512],
                            ALU.subtract)
                self.act(bass.AP(ssb, r0_ * 512, [[8 * 512, 128], [1, nr_ * 512]]),
                         bass.AP(ssb, r0_ * 512, [[8 * 512, 128], [1, nr_ * 512]]),
                         AF.Exp)
                pv = self.psb[PV]
                for idx, r in enumerate(rlist):
                    ti = (4 * (w - 1) + r) % 8
                    mk.t.matmul(pv[0:65, 0:512],
                                bass.AP(vtm, ti * 520 + h * 65,
                                        [[8 * 520, 128], [1, 65]]),
                                ssb[:, r * 512:(r + 1) * 512],
                                start=(idx == 0), stop=(idx == nr_ - 1))
                if hb == 0:
                    mk.v.tensor_copy(obuf[0:64, csl], pv[0:64, 0:512])
                else:
                    mk.v.tensor_copy(sq[0:64, 0:512], pv[0:64, 0:512])
                    mk.dma("g", obuf[64:128, csl], sq[0:64, 0:512])
                mk.s.copy(m1[64:65, 0:512], pv[64:65, 0:512])
                mk.dma("g", dnr[h:h + 1, :], m1[64:65, 0:512])
            mk.v.reciprocal(dnr[:], dnr[:])
            for mc in range(4):
                csl = slice(mc * 512, (mc + 1) * 512)
                pr = self.psb[PR]
                mk.t.matmul(pr[0:128, 0:512],
                            self.esel[:, mc * 128:(mc + 1) * 128],
                            dnr[0:8, :], start=True, stop=True)
                self.tt(obuf[:, csl], obuf[:, csl], pr[0:128, 0:512])
            import os as _os
            if w == 0 and l == 0 and _os.environ.get("FFN_DBG"):
                mk.dma("s", self.sc_dbg.ap()[0:128, :], qw[:])
                mk.dma("s", self.sc_dbg.ap()[128:256, :], kcur[:])
                mk.dma("s", self.sc_dbg.ap()[256:384, :], obuf[:])
                mk.dma("s", self.sc_dbg.ap()[384:512, 0:1408], self.emask[:])
                mk.dma("s", self.sc_dbg.ap()[512:640, :], tabs[:])
                mk.dma("s", self.sc_dbg.ap()[640:648, 0:512], dnr[:])
                mk.dma("s", self.sc_dbg.ap()[648:656, 0:512], nrq[:])
                mk.dma("s", self.sc_dbg.ap()[656:664, 0:512], nrk[:])
                mk.dma("s", self.sc_dbg.ap()[792:920, :],
                       bass.AP(ssb, 2048, [[8 * 512, 128], [1, 2048]]))
                mk.dma("s", self.sc_dbg.ap()[920:1024, 0:520],
                       bass.AP(vtm, 0, [[8 * 520, 104], [1, 520]]))
            mk.dma("y", wsl[0][:, 0:512], self.wot.ap()[l * 8 + 0])
            for m in range(8):
                if w < 3:
                    mk.v.tensor_copy(bstash[:, m * 3 + w:m * 3 + w + 1],
                                     xf[:, m * L + t0 + 511:m * L + t0 + 512])
                buf = wsl[m % 2]
                if m + 1 < 8:
                    mk.dma("y", wsl[(m + 1) % 2][:, 0:512],
                           self.wot.ap()[l * 8 + m + 1])
                ps = self.psb[PB]
                for kc in range(4):
                    mk.t.matmul(ps[0:128, 0:512], buf[:, kc * 128:(kc + 1) * 128],
                                obuf[:, kc * 512:(kc + 1) * 512],
                                start=(kc == 0), stop=(kc == 3))
                self.tt(xf[:, m * L + t0:m * L + t0 + 512],
                        xf[:, m * L + t0:m * L + t0 + 512], ps[0:128, 0:512],
                        ALU.add)

    def ffn_mlp(self, l, xf):
        mk = self.mk
        ar = self.ar
        lngb = ar("lngb", [128, 16])
        self.ffn_load_cols(lngb, self.ln_g.ap()[l], 8, off=0)
        self.ffn_load_cols(lngb, self.ln_b.ap()[l], 8, off=8)
        b1row = ar("b1row", [1, 2 * HID])
        b2row = ar("b2row", [1, D])
        mk.dma("y", b1row[:], self.b1.ap()[l])
        mk.dma("y", b2row[:], self.b2.ap()[l])
        xln = ar("xlnb", [128, 8 * 512])
        h1 = ar("h1b", [128, 22 * 512])
        gel = ar("gelb", [128, 512])
        sqm = ar("sqmb", [128, 512])
        rows = ar("rowsb", [1, 3 * 512])
        wsl = [ar("wslm0", [128, 1024]), ar("wslm1", [128, 1024])]
        w2s = [ar("w2sb0", [128, 1408]), ar("w2sb1", [128, 1408])]
        mstash = ar("mstash", [128, 8 * 3])
        x0m = ar("x0m", [128, 4])
        PB0, PB1, PRA, PRB, PRC = 0, 1, 2, 3, 4
        hch = [(i * 128, 128) for i in range(21)] + [(21 * 128, HID - 21 * 128)]

        for tg in range(4):
            t0 = tg * 512
            for i, dc in enumerate(range(4, 8)):
                if tg == 0:
                    mk.g.memset(x0m[:, i:i + 1], 0.0)
                else:
                    mk.v.tensor_copy(x0m[:, i:i + 1],
                                     mstash[:, dc * 3 + tg - 1:dc * 3 + tg])

            def sx(dc, fr0, frw):
                if dc < 4:
                    return xf[:, dc * L + t0 + fr0: dc * L + t0 + fr0 + frw]
                return xf[:, dc * L + t0 + fr0 - 1: dc * L + t0 + fr0 - 1 + frw]

            pra, prb = self.psb[PRA], self.psb[PRB]
            for kc in range(8):
                dst = xln[:, kc * 512:(kc + 1) * 512]
                if kc < 4:
                    mk.v.tensor_copy(dst, xf[:, kc * L + t0: kc * L + t0 + 512])
                else:
                    mk.v.tensor_copy(xln[:, kc * 512 + 1:(kc + 1) * 512],
                                     xf[:, kc * L + t0: kc * L + t0 + 511])
                    mk.v.tensor_copy(xln[:, kc * 512:kc * 512 + 1],
                                     x0m[:, kc - 4:kc - 3])
            for kc in range(8):
                sl = xln[:, kc * 512:(kc + 1) * 512]
                mk.t.matmul(pra[0:1, 0:512], self.ones_col[:], sl,
                            start=(kc == 0), stop=(kc == 7))
                self.act(sqm[:], sl, AF.Square)
                mk.t.matmul(prb[0:1, 0:512], self.ones_col[:], sqm[:],
                            start=(kc == 0), stop=(kc == 7))
            mu = rows[0:1, 0:512]
            ivr = rows[0:1, 512:1024]
            t1 = rows[0:1, 1024:1536]
            self.ts(mu, pra[0:1, 0:512], 1.0 / D)
            self.act(t1, mu, AF.Square)
            self.stt(ivr, prb[0:1, 0:512], 1.0 / D, t1, op0=ALU.mult,
                     op1=ALU.subtract)
            self.act(ivr, ivr, AF.Ln, bias=1e-5)
            self.act(ivr, ivr, AF.Exp, scale=-0.5)
            prc = self.psb[PRC]
            mk.t.matmul(prc[0:128, 0:512], self.ones_row1[0:1, 0:128], mu,
                        start=True, stop=True)
            pra2 = self.psb[PRA]
            mk.t.matmul(pra2[0:128, 0:512], self.ones_row1[0:1, 0:128], ivr,
                        start=True, stop=True)
            for kc in range(8):
                sl = xln[:, kc * 512:(kc + 1) * 512]
                self.tt(sl, sl, prc[0:128, 0:512], ALU.subtract)
                self.tt(sl, sl, pra2[0:128, 0:512])
                self.ts(sl, sl, lngb[:, kc:kc + 1], lngb[:, 8 + kc:9 + kc],
                        op0=ALU.mult, op1=ALU.add)
            slabseq = [(mi, half) for mi in range(22) for half in (0, 1)]
            mk.dma("y", wsl[0][:], self.w1t.ap()[l * 44 + 0 * 22 + 0])
            for idx, (mi, half) in enumerate(slabseq):
                buf = wsl[idx % 2]
                if idx + 1 < len(slabseq):
                    nmi, nhalf = slabseq[idx + 1]
                    mk.dma("y", wsl[(idx + 1) % 2][:],
                           self.w1t.ap()[l * 44 + nhalf * 22 + nmi])
                m0, mw = hch[mi]
                col0 = m0 + half * HID
                ps = self.psb[PB0 if half == 0 else PB1]
                for kc in range(8):
                    mk.t.matmul(ps[0:mw, 0:512],
                                buf[:, kc * 128:kc * 128 + mw],
                                xln[:, kc * 512:(kc + 1) * 512],
                                start=(kc == 0), stop=False)
                mk.t.matmul(ps[0:mw, 0:512],
                            b1row[0:1, col0:col0 + mw],
                            self.ones_row1[0:1, 0:512],
                            start=False, stop=True)
                if half == 1:
                    psA, psB = self.psb[PB0], self.psb[PB1]
                    self.act(gel[0:mw, :], psB[0:mw, 0:512], AF.Gelu)
                    self.tt(h1[0:mw, mi * 512:(mi + 1) * 512],
                            psA[0:mw, 0:512], gel[0:mw, :])
            def w2ap(m, hh):
                return self.w2t.ap()[l * 8 + m][:, hh * 1408:(hh + 1) * 1408]
            mk.dma("y", w2s[0][:], w2ap(0, 0))
            for m in range(8):
                if tg < 3:
                    mk.v.tensor_copy(mstash[:, m * 3 + tg:m * 3 + tg + 1],
                                     xf[:, m * L + t0 + 511:m * L + t0 + 512])
                ps = self.psb[PB0 if m % 2 == 0 else PB1]
                for seg in (0, 1):
                    sidx = m * 2 + seg
                    buf = w2s[sidx % 2]
                    if sidx + 1 < 16:
                        nm, nh = divmod(sidx + 1, 2)
                        mk.dma("y", w2s[(sidx + 1) % 2][:], w2ap(nm, nh))
                    for ki in range(seg * 11, seg * 11 + 11):
                        k0, kwd = hch[ki]
                        c0 = (ki - seg * 11) * 128
                        mk.t.matmul(ps[0:128, 0:512],
                                    buf[0:kwd, c0:c0 + 128],
                                    h1[0:kwd, ki * 512:(ki + 1) * 512],
                                    start=(ki == 0), stop=False)
                mk.t.matmul(ps[0:128, 0:512], b2row[0:1, m * 128:(m + 1) * 128],
                            self.ones_row1[0:1, 0:512], start=False, stop=True)
                self.tt(xf[:, m * L + t0:m * L + t0 + 512],
                        xf[:, m * L + t0:m * L + t0 + 512], ps[0:128, 0:512],
                        ALU.add)



_PROG_CACHE = {}

_HCH = [(i * 128, 128) for i in range(21)] + [(21 * 128, HID - 21 * 128)]


def _tile_w1(w1):
    out = np.zeros((2, 44, 128, 1024), np.float32)
    for l in range(2):
        for half in range(2):
            for mi, (m0, mw) in enumerate(_HCH):
                col0 = half * HID + m0
                blk = w1[l, :, col0:col0 + mw].reshape(8, 128, mw)
                out[l, half * 22 + mi].reshape(128, 8, 128)[:, :, :mw] = \
                    blk.transpose(1, 0, 2)
    return out.reshape(88, 128, 1024)


def _tile_wout(woutT):
    out = np.zeros((8, 128, 2048), np.float32)
    for m in range(8):
        blk = woutT[:, m * 128:(m + 1) * 128].reshape(16, 128, 128)
        out[m].reshape(128, 16, 128)[:] = blk.transpose(1, 0, 2)
    return out


def _tile_wqk(wqkv):
    out = np.zeros((2, 2, 4, 128, 1024), np.float32)
    for l in range(2):
        for s in range(2):
            for mc in range(4):
                col0 = s * FI + mc * 128
                blk = wqkv[l, :, col0:col0 + 128].reshape(8, 128, 128)
                out[l, s, mc].reshape(128, 8, 128)[:] = blk.transpose(1, 0, 2)
    return out.reshape(16, 128, 1024)


def _tile_wv(wqkv):
    out = np.zeros((2, 8, 128, 512), np.float32)
    for l in range(2):
        for kc in range(8):
            out[l, kc] = wqkv[l, kc * 128:(kc + 1) * 128, 2 * FI:3 * FI]
    return out.reshape(16, 128, 512)


def _tile_wot(wo):
    out = np.zeros((2, 8, 128, 512), np.float32)
    for l in range(2):
        for m in range(8):
            blk = wo[l, :, m * 128:(m + 1) * 128].reshape(4, 128, 128)
            out[l, m].reshape(128, 4, 128)[:] = blk.transpose(1, 0, 2)
    return out.reshape(16, 128, 512)


def _tile_w2(w2):
    out = np.zeros((2, 8, 128, 2816), np.float32)
    for l in range(2):
        for m in range(8):
            for ki, (k0, kwd) in enumerate(_HCH):
                out[l, m, :kwd, ki * 128:(ki + 1) * 128] = \
                    w2[l, k0:k0 + kwd, m * 128:(m + 1) * 128]
    return out.reshape(16, 128, 2816)


def _build_program():
    if "k" not in _PROG_CACHE:
        k = Kern()
        k.build_all()
        _PROG_CACHE["k"] = k
    return _PROG_CACHE["k"]


def kernel(L0_attn_x, L0_attn_W_in, L0_attn_dw_weight, L0_attn_dw_bias,
           L0_attn_W_out, L0_attn_skip, L0_attn_dt_bias, L0_attn_theta_raw,
           L0_attn_k_raw, L0_ffn_wqkv, L0_ffn_q_scale, L0_ffn_k_scale,
           L0_ffn_wo, L0_ffn_ln_g, L0_ffn_ln_b, L0_ffn_w1, L0_ffn_b1,
           L0_ffn_w2, L0_ffn_b2):
    k = _build_program()
    f32 = np.float32
    common = {
        "WinT": np.ascontiguousarray(L0_attn_W_in.T).astype(f32, copy=False),
        "dw_w": np.ascontiguousarray(L0_attn_dw_weight).astype(f32, copy=False),
        "dw_b": np.ascontiguousarray(L0_attn_dw_bias[None, :]).astype(f32, copy=False),
        "wott": _tile_wout(np.ascontiguousarray(L0_attn_W_out.T).astype(f32, copy=False)),
        "skip": np.ascontiguousarray(L0_attn_skip[None, :]).astype(f32, copy=False),
        "dt_bias": np.ascontiguousarray(L0_attn_dt_bias[None, :]).astype(f32, copy=False),
        "thetaT": np.ascontiguousarray(L0_attn_theta_raw.T).astype(f32, copy=False),
        "krawT": np.ascontiguousarray(L0_attn_k_raw.T).astype(f32, copy=False),
        "wqkt": _tile_wqk(np.asarray(L0_ffn_wqkv, dtype=f32)),
        "wvt": _tile_wv(np.asarray(L0_ffn_wqkv, dtype=f32)),
        "qscale": np.ascontiguousarray(L0_ffn_q_scale[:, None, :]).astype(f32, copy=False),
        "kscale": np.ascontiguousarray(L0_ffn_k_scale[:, None, :]).astype(f32, copy=False),
        "wot": _tile_wot(np.asarray(L0_ffn_wo, dtype=f32)),
        "ln_g": np.ascontiguousarray(L0_ffn_ln_g[:, None, :]).astype(f32, copy=False),
        "ln_b": np.ascontiguousarray(L0_ffn_ln_b[:, None, :]).astype(f32, copy=False),
        "w1t": _tile_w1(np.asarray(L0_ffn_w1, dtype=f32)),
        "b1": np.ascontiguousarray(L0_ffn_b1[:, None, :]).astype(f32, copy=False),
        "w2t": _tile_w2(np.asarray(L0_ffn_w2, dtype=f32)),
        "b2": np.ascontiguousarray(L0_ffn_b2[:, None, :]).astype(f32, copy=False),
    }
    in_maps = []
    for b in range(8):
        m = dict(common)
        m["x"] = np.ascontiguousarray(L0_attn_x[b]).astype(f32, copy=False)
        in_maps.append(m)
    res = run_bass_kernel_spmd(k.nc, in_maps, core_ids=list(range(8)))
    return np.stack([res.results[b]["out"] for b in range(8)], axis=0)
